# version 01
# baseline (speedup 1.0000x reference)
import math
from contextlib import ExitStack

import numpy as np

import concourse.bass as bass
import concourse.mybir as mybir
from concourse.bass_utils import run_bass_kernel_spmd
from concourse.alu_op_type import AluOpType as AL

F32 = mybir.dt.float32
BF16 = mybir.dt.bfloat16
I32 = mybir.dt.int32
AF = mybir.ActivationFunctionType
AX = mybir.AxisListType

T = 4096
DM = 1024
FH = 3072
MH = 2816
NE = 8
FLEN = 8448
FPAD = 4223
WINW = 4608
WINBASE = 3584
SHIFT = 16.0
QS = 0.125
OWN = 2048
EPS = 1e-5
NKC = 32

KCMIN = [max(0, (3457 - 512 * g + 127) // 128) for g in range(8)]



def _bucket_np():
    d = np.arange(T)
    rp = np.maximum(d, 1).astype(np.float32)
    large = 16 + (np.log(rp / np.float32(16.0))
                  / np.float32(math.log(8.0)) * 16).astype(np.int32)
    large = np.minimum(large, 31)
    return np.where(d < 16, d, large)


def stage_core(inputs, core):
    b, h = core // 2, core % 2
    x = np.ascontiguousarray(np.asarray(inputs["L0_attn_x"][b]).T.astype(np.float32, copy=False))
    d = {}
    d["x_fm"] = x
    d["x_own"] = np.ascontiguousarray(x[:, 2048 * h:2048 * h + 2048])

    Wa = np.asarray(inputs["L0_attn_W_attn"])
    ba = np.asarray(inputs["L0_attn_b_attn"])
    l0w = np.empty((4, 1024, 384), np.float32)
    l0b = np.empty((4, 128, 3), np.float32)
    for p in range(4):
        P = 4 * h + p
        l0w[p, :, 0:128] = Wa[:, 128 * P:128 * P + 128]
        l0w[p, :, 128:256] = Wa[:, 1024 + 128 * P:1024 + 128 * P + 128]
        l0w[p, :, 256:384] = Wa[:, 2048 + 128 * P:2048 + 128 * P + 128]
        l0b[p, :, 0] = ba[128 * P:128 * P + 128]
        l0b[p, :, 1] = ba[1024 + 128 * P:1024 + 128 * P + 128]
        l0b[p, :, 2] = ba[2048 + 128 * P:2048 + 128 * P + 128]
    d["l0w"] = l0w
    d["l0b"] = l0b
    d["l0proj"] = np.ascontiguousarray(
        np.asarray(inputs["L0_attn_W_proj"])[512 * h:512 * h + 512, :])
    pb = np.asarray(inputs["L0_attn_b_proj"]).reshape(8, 128).T
    d["l0pb"] = np.ascontiguousarray(pb) if h == 0 else np.zeros((128, 8), np.float32)

    d["table_own"] = np.ascontiguousarray(
        np.asarray(inputs["L0_attn_bias_table"])[:, 8 * h:8 * h + 8])
    bk = _bucket_np()
    oh = np.zeros((32, T), np.float32)
    oh[bk, np.arange(T)] = 1.0
    d["oh0"] = oh

    d["wg_t"] = np.ascontiguousarray(np.asarray(inputs["L0_ffn_w_gate"]).T)
    d["wu_t"] = np.ascontiguousarray(np.asarray(inputs["L0_ffn_w_up"]).T)
    d["wd_t"] = np.ascontiguousarray(np.asarray(inputs["L0_ffn_w_down"]).T)

    l1w = np.empty((4, 1024, 384), np.float32)
    for p in range(4):
        P = 4 * h + p
        l1w[p, :, 0:128] = np.asarray(inputs["L1_attn_Wq"])[128 * P:128 * P + 128, :].T
        l1w[p, :, 128:256] = np.asarray(inputs["L1_attn_Wk"])[128 * P:128 * P + 128, :].T
        l1w[p, :, 256:384] = np.asarray(inputs["L1_attn_Wv"])[128 * P:128 * P + 128, :].T
    d["l1w"] = l1w
    d["l1proj"] = np.ascontiguousarray(
        np.asarray(inputs["L1_attn_Wo"]).T[512 * h:512 * h + 512, :])
    d["l1bias_own"] = np.ascontiguousarray(
        np.asarray(inputs["L1_attn_bias"])[8 * h:8 * h + 8, :])
    d["tau_own"] = np.ascontiguousarray(
        np.asarray(inputs["L1_attn_tau"])[8 * h:8 * h + 8].reshape(1, 8))
    d["riot"] = (1.0 / np.arange(1, T + 1, dtype=np.float64)).astype(np.float32).reshape(1, T)

    Wg = np.asarray(inputs["L1_ffn_Wg"])
    d["wg8"] = np.ascontiguousarray(
        Wg.T.reshape(8, 128, 8).transpose(1, 0, 2).reshape(128, 64))
    d["w1t"] = np.ascontiguousarray(np.transpose(np.asarray(inputs["L1_ffn_W1"]), (0, 2, 1)))
    d["w3t"] = np.ascontiguousarray(np.transpose(np.asarray(inputs["L1_ffn_W3"]), (0, 2, 1)))
    d["w2t"] = np.ascontiguousarray(np.transpose(np.asarray(inputs["L1_ffn_W2"]), (0, 2, 1)))
    d["ident"] = np.eye(128, dtype=np.float32)
    return d



class PB:
    ENGS = ("t", "v", "s", "g", "cc")
    NSLOT = 12

    def __init__(self, nc):
        self.nc = nc
        self.sched = []
        self.cnt = {}
        self.hdl = {}
        self.lastwait = {}
        self.dmaq = {}

    def sem(self, ctx, name):
        self.hdl[name] = ctx.enter_context(self.nc.semaphore(name))
        self.cnt[name] = 0

    def dsem(self, ctx, q):
        self.dmaq[q] = 0
        for i in range(self.NSLOT):
            n = f"{q}{i}"
            self.hdl[n] = ctx.enter_context(self.nc.semaphore(n))
            self.cnt[n] = 0

    def op(self, eng, build_fn, waits=(), incs=()):
        if eng in ("v", "s", "g"):
            waits = list(waits) + [(eng, self.cnt[eng])]
        w = []
        for sname, v in waits:
            assert v <= self.cnt[sname], (sname, v, self.cnt[sname])
            if v <= 0:
                continue
            key = (eng, sname)
            if self.lastwait.get(key, -1) >= v:
                continue
            self.lastwait[key] = v
            w.append((self.hdl[sname], v))
        iz = []
        for sname, amt in incs:
            self.cnt[sname] += amt
            iz.append((self.hdl[sname], amt))

        def fn(e, w=w, iz=iz, build_fn=build_fn):
            for hh, v in w:
                e.wait_ge(hh, v)
            ins = build_fn(e)
            for hh, amt in iz:
                ins.then_inc(hh, amt)
        self.sched.append((eng, fn))

    def bar(self, *exclude):
        ex = set(exclude)
        w = [(n, self.cnt[n]) for n in self.ENGS if n not in ex and self.cnt[n] > 0]
        for q in self.dmaq:
            if q in ex:
                continue
            for i in range(self.NSLOT):
                n = f"{q}{i}"
                if self.cnt[n] > 0:
                    w.append((n, self.cnt[n]))
        return w

    def dma(self, q, out, in_, waits=()):
        j = self.dmaq[q]
        self.dmaq[q] = j + 1
        n = f"{q}{j % self.NSLOT}"
        w = list(waits)
        if self.cnt[n] > 0:
            w.append((n, self.cnt[n]))
        self.op(q, lambda e: e.dma_start(out=out, in_=in_), w, ((n, 16),))
        return (n, self.cnt[n])

    def mm(self, out, lhsT, rhs, start, stop, tp=None, waits=(), fin=False):
        incs = (("t", 1),) if fin else ()
        self.op("t", lambda e: e.matmul(out, lhsT, rhs, start=start, stop=stop,
                                        tile_position=tp), waits, incs)
        return ("t", self.cnt["t"])

    def tr(self, out, in_, ident, waits=()):
        self.op("t", lambda e: e.transpose(out, in_, ident), waits, (("t", 1),))
        return ("t", self.cnt["t"])

    def act(self, out, in_, func, bias=0.0, scale=1.0, waits=()):
        self.op("s", lambda e: e.activation(out, in_, func, bias=bias, scale=scale),
                waits, (("s", 1),))
        return ("s", self.cnt["s"])

    def scopy(self, out, in_, waits=()):
        self.op("s", lambda e: e.copy(out, in_), waits, (("s", 1),))
        return ("s", self.cnt["s"])

    def tt(self, out, in0, in1, alu, waits=()):
        self.op("v", lambda e: e.tensor_tensor(out, in0, in1, alu), waits, (("v", 1),))
        return ("v", self.cnt["v"])

    def ts(self, out, in0, s1, s2, op0, op1=None, waits=()):
        if op1 is None:
            self.op("v", lambda e: e.tensor_scalar(out, in0, s1, s2, op0),
                    waits, (("v", 1),))
        else:
            self.op("v", lambda e: e.tensor_scalar(out, in0, s1, s2, op0, op1),
                    waits, (("v", 1),))
        return ("v", self.cnt["v"])

    def stt(self, out, in0, scalar, in1, op0, op1, waits=()):
        self.op("v", lambda e: e.scalar_tensor_tensor(out, in0, scalar, in1, op0, op1),
                waits, (("v", 1),))
        return ("v", self.cnt["v"])

    def vred(self, out, in_, alu, negate=False, waits=()):
        self.op("v", lambda e: e.tensor_reduce(out, in_, AX.X, alu, negate=negate),
                waits, (("v", 1),))
        return ("v", self.cnt["v"])

    def vcopy(self, out, in_, waits=()):
        self.op("v", lambda e: e.tensor_copy(out, in_), waits, (("v", 1),))
        return ("v", self.cnt["v"])

    def vrecip(self, out, in_, waits=()):
        self.op("v", lambda e: e.reciprocal(out, in_), waits, (("v", 1),))
        return ("v", self.cnt["v"])

    def recip_pos(self, out, in_, tmp, waits=()):
        self.act_ln = None
        self.op("s", lambda e: e.activation(tmp, in_, AF.Ln), waits, (("s", 1),))
        self.op("s", lambda e: e.activation(out, tmp, AF.Exp, scale=-1.0), (),
                (("s", 1),))
        return ("s", self.cnt["s"])

    def vmemset(self, ap, val, waits=()):
        self.op("v", lambda e: e.memset(ap, val), waits, (("v", 1),))
        return ("v", self.cnt["v"])



def build_program(mode, race_detect=False, skip=()):
    nc = bass.Bass("TRN2", target_bir_lowering=False, num_devices=8,
                   detect_race_conditions=race_detect)
    L0 = mode in ("L0", "FUSED")
    L1 = mode in ("L1", "FUSED")

    input_names = []

    def din(name, shape, dt):
        input_names.append(name)
        return nc.dram_tensor(name, shape, dt, kind="ExternalInput")

    ident_d = din("ident", [128, 128], F32)
    riot_d = din("riot", [1, T], F32)
    if L0:
        x_fm = din("x_fm", [DM, T], F32)
        x_own = din("x_own", [DM, OWN], F32)
        l0w = din("l0w", [4, DM, 384], F32)
        l0b = din("l0b", [4, 128, 3], F32)
        l0proj = din("l0proj", [512, DM], F32)
        l0pb = din("l0pb", [128, 8], F32)
        table_own = din("table_own", [32, 8], F32)
        oh0 = din("oh0", [32, T], F32)
        wg_t = din("wg_t", [DM, FH], I32)
        wu_t = din("wu_t", [DM, FH], I32)
        wd_t = din("wd_t", [FH, DM], I32)
    if L1:
        l1w = din("l1w", [4, DM, 384], F32)
        l1proj = din("l1proj", [512, DM], F32)
        l1bias_own = din("l1bias_own", [8, 2048], F32)
        tau_own = din("tau_own", [1, 8], F32)
        wg8 = din("wg8", [128, 64], F32)
        w1t = din("w1t", [NE, DM, MH], F32)
        w3t = din("w3t", [NE, DM, MH], F32)
        w2t = din("w2t", [NE, MH, DM], F32)
        if mode == "L1":
            x1_in = din("x1_in", [DM, OWN], F32)

    if mode == "L0":
        x1_out = nc.dram_tensor("x1_out", [DM, OWN], F32, kind="ExternalOutput")
    else:
        y_out = nc.dram_tensor("y_out", [DM, OWN], F32, kind="ExternalOutput")
    dbg_out = (nc.dram_tensor("dbg_out", [24, 2048], F32, kind="ExternalOutput")
               if "dbg" in skip else None)
    dbg2_out = (nc.dram_tensor("dbg2_out", [128, 128], F32, kind="ExternalOutput")
                if "dbg" in skip else None)

    F0d = nc.dram_tensor("F0d", [8, FLEN], BF16) if L0 else None
    F1d = nc.dram_tensor("F1d", [8, FLEN], BF16) if L1 else None
    xn0_d = nc.dram_tensor("xn0_d", [DM, T], BF16) if L0 else None
    x1_d = nc.dram_tensor("x1_d", [DM, OWN], F32)
    pp_in = nc.dram_tensor("pp_in", [2, DM, OWN], BF16)
    pp_out = nc.dram_tensor("pp_out", [DM, OWN], BF16)
    pp_in1 = nc.dram_tensor("pp_in1", [2, DM, OWN], BF16)
    pp_out1 = nc.dram_tensor("pp_out1", [DM, OWN], BF16)
    if L1:
        xn1_in = nc.dram_tensor("xn1_in", [DM, OWN], BF16)
        xn1_g = nc.dram_tensor("xn1_g", [2, DM, OWN], BF16)

    p = PB(nc)
    with ExitStack() as ctx:
        en = ctx.enter_context
        XN = en(nc.sbuf_tensor("XN", [128, 8 * 2048], BF16))
        Y = en(nc.sbuf_tensor("Y", [128, 4 * 4096], BF16))
        M1 = en(nc.sbuf_tensor("M1", [128, 24576], BF16))
        M2 = en(nc.sbuf_tensor("M2", [128, 20480], BF16))
        WINB = en(nc.sbuf_tensor("WINB", [128, WINW], BF16))
        QP = en(nc.sbuf_tensor("QP", [128, T], BF16))
        EB = en(nc.sbuf_tensor("EB", [128, 2048], BF16))
        PBUF = en(nc.sbuf_tensor("PBUF", [128, 1024], BF16))
        STAT = en(nc.sbuf_tensor("STAT", [128, 2048], F32))
        SCR = en(nc.sbuf_tensor("SCR", [128, 3072], F32))
        SMB = en(nc.sbuf_tensor("SMB", [128, 2048], BF16))
        ROWS = en(nc.sbuf_tensor("ROWS", [128, 1664], BF16))
        IDT = en(nc.sbuf_tensor("IDT", [128, 128], BF16))
        BIA = en(nc.sbuf_tensor("BIA", [128, 32], F32))
        PS = [en(nc.psum_tensor(f"PS{i}", [128, 512], F32)) for i in range(7)]
        PSB = en(nc.psum_tensor("PSB", [128, 1024], BF16))

        for s in PB.ENGS:
            p.sem(ctx, s)
        p.dsem(ctx, "dg")
        p.dsem(ctx, "ds")
        block = en(nc.Block())

        KR0, VR0, PTA0 = 16384, 20480, 0
        VT0, PTB0 = 16384, 0

        ONES1 = ROWS[0:1, 0:128]
        ONES128 = ROWS[0:128, 0:1]
        ONESF = BIA[0:128, 26:27]

        m_id = p.dma("dg", IDT[:, :], ident_d[:, :])
        p.vmemset(ROWS[0:128, 0:128], 1.0)
        p.vmemset(ONESF, 1.0)
        p.vmemset(BIA[0:128, 24:25], -SHIFT)
        p.vmemset(BIA[0:128, 25:26], EPS)

        def build_F(Fd, is_l0):
            zt = SMB[0:16, 256:768]
            p.vmemset(zt, 0.0)
            for i in range(9):
                w = min(512, FPAD - 512 * i)
                p.dma("ds", bass.AP(Fd, 512 * i, [[FLEN, 8], [1, w]]),
                      zt[0:8, 0:w], waits=p.bar("ds"))
            if is_l0:
                tb = SMB[0:32, 0:16]
                p.dma("dg", tb[0:32, 0:8], table_own[:, :])
                ohc = SMB[0:32, 768:1280]
                for i in range(8):
                    p.dma("dg", ohc[:, :], oh0[:, 512 * i:512 * i + 512],
                          waits=p.bar("dg"))
                    bank = PS[0]
                    mmf = p.mm(bank[0:8, :], tb[0:32, 0:8], ohc[:, :], True, True,
                               waits=p.bar("t"), fin=True)
                    ev = p.act(SMB[0:8, 1280:1792], bank[0:8, :], AF.Exp,
                               bias=BIA[0:8, 24:25], waits=p.bar("s"))
                    p.dma("ds", bass.AP(Fd, FPAD + 512 * i, [[FLEN, 8], [1, 512]]),
                          SMB[0:8, 1280:1792], waits=[ev])
                p.dma("ds", bass.AP(Fd, FPAD + 4096, [[FLEN, 8], [1, FLEN - FPAD - 4096]]),
                      zt[0:8, 0:FLEN - FPAD - 4096], waits=p.bar("ds"))
            else:
                bc = SMB[0:8, 768:1280]
                for i in range(4):
                    p.dma("ds", SCR[0:8, 2048:2560],
                          l1bias_own[:, 512 * i:512 * i + 512], waits=p.bar("ds"))
                    ev = p.act(bc, SCR[0:8, 2048:2560], AF.Exp,
                               bias=BIA[0:8, 24:25], waits=p.bar("s"))
                    p.dma("ds", bass.AP(Fd, FPAD + 512 * i, [[FLEN, 8], [1, 512]]),
                          bc, waits=[ev])
                et = SMB[0:16, 1280:1792]
                p.vmemset(et, math.exp(-SHIFT), waits=p.bar("v"))
                off = FPAD + 2048
                rem = FLEN - off
                for i in range((rem + 511) // 512):
                    w = min(512, rem - 512 * i)
                    p.dma("ds", bass.AP(Fd, off + 512 * i, [[FLEN, 8], [1, w]]),
                          et[0:8, 0:w], waits=p.bar("ds"))

        if L0:
            build_F(F0d, True)
        if L1:
            build_F(F1d, False)

        def layernorm(src_d, ntok, out_sbuf, out_dram):
            stride = int(src_d.shape[1])
            nts = ntok // 512
            for tsl in range(nts):
                c0 = 512 * tsl
                for dchunk in range(8):
                    xa = SCR[0:128, 2048:2560]
                    p.dma("ds", xa,
                          bass.AP(src_d, (128 * dchunk) * stride + c0,
                                  [[stride, 128], [1, 512]]),
                          waits=p.bar("ds"))
                    p.mm(PS[0][0:1, :], ONESF, xa, dchunk == 0, dchunk == 7,
                         waits=p.bar("t"), fin=(dchunk == 7))
                    p.act(SCR[0:128, 2560:3072], xa, AF.Square, waits=p.bar("s"))
                    p.mm(PS[1][0:1, :], ONESF, SCR[0:128, 2560:3072],
                         dchunk == 0, dchunk == 7, waits=p.bar("t"),
                         fin=True)
                sA = STAT[0:1, 0:512]
                sB = STAT[0:1, 512:1024]
                sC = STAT[0:1, 1024:1536]
                sD = STAT[0:1, 1536:2048]
                p.ts(sA, PS[0][0:1, :], 1.0 / DM, None, AL.mult, waits=p.bar("v"))
                p.ts(sB, PS[1][0:1, :], 1.0 / DM, None, AL.mult, waits=p.bar("v"))
                p.tt(sC, sA, sA, AL.mult, waits=p.bar("v"))
                p.tt(sD, sB, sC, AL.subtract, waits=p.bar("v"))
                p.act(sC, sD, AF.Ln, bias=BIA[0:1, 25:26], waits=p.bar("s"))
                p.act(sD, sC, AF.Exp, scale=-0.5, waits=p.bar("s"))
                p.tt(sC, sA, sD, AL.mult, waits=p.bar("v"))
                ab = ROWS[0:1, 128:640]
                bb = ROWS[0:1, 640:1152]
                p.vcopy(ab, sD, waits=p.bar("v"))
                p.vcopy(bb, sC, waits=p.bar("v"))
                p.mm(PS[2][0:128, :], ONES1, ab, True, True,
                     waits=p.bar("t"), fin=True)
                p.mm(PS[3][0:128, :], ONES1, bb, True, True,
                     waits=p.bar("t"), fin=True)
                p.scopy(SCR[0:128, 2048:2560], PS[2][:, :], waits=p.bar("s"))
                p.scopy(SCR[0:128, 2560:3072], PS[3][:, :], waits=p.bar("s"))
                for dchunk in range(8):
                    xb = SCR[0:128, 0:512]
                    p.dma("ds", xb,
                          bass.AP(src_d, (128 * dchunk) * stride + c0,
                                  [[stride, 128], [1, 512]]),
                          waits=p.bar("ds"))
                    t1 = STAT[0:128, 0:512]
                    p.tt(t1, xb, SCR[0:128, 2048:2560], AL.mult, waits=p.bar("v"))
                    if out_sbuf is not None:
                        dst = out_sbuf[0:128, 2048 * dchunk + c0:
                                       2048 * dchunk + c0 + 512]
                    else:
                        dst = EB[0:128, 0:512]
                    p.tt(dst, t1, SCR[0:128, 2560:3072], AL.subtract,
                         waits=p.bar("v"))
                    if out_dram is not None:
                        p.dma("ds",
                              bass.AP(out_dram, (128 * dchunk) * int(out_dram.shape[1]) + c0,
                                      [[int(out_dram.shape[1]), 128], [1, 512]]),
                              dst, waits=p.bar("ds"))

        def attention(layer):
            wsrc = l0w if layer == 0 else l1w
            Fd = F0d if layer == 0 else F1d
            if layer == 1:
                p.dma("ds", BIA[0:1, 16:24], tau_own[:, :], waits=p.bar("ds"))
            for pr in range(4):
                WPS = WINB[0:128, 0:3072]
                for c in range(8):
                    p.dma("dg", WPS[:, 384 * c:384 * c + 384],
                          bass.AP(wsrc, pr * DM * 384 + (128 * c) * 384,
                                  [[384, 128], [1, 384]]),
                          waits=p.bar("dg"))
                if layer == 0:
                    p.dma("dg", BIA[0:128, 0:3],
                          bass.AP(l0b, pr * 128 * 3, [[3, 128], [1, 3]]),
                          waits=p.bar("dg"))
                    p.ts(BIA[0:128, 4:5], BIA[0:128, 0:1], QS, None, AL.mult,
                         waits=p.bar("v"))
                for tsl in range(8):
                    for dchunk in range(8):
                        rb = EB[0:128, 512 * (dchunk % 2):512 * (dchunk % 2) + 512]
                        if layer == 0:
                            srcp = bass.AP(xn0_d, (128 * dchunk) * T + 512 * tsl,
                                           [[T, 128], [1, 512]])
                        else:
                            gg = tsl // 4
                            lc = 512 * (tsl % 4)
                            srcp = bass.AP(xn1_g, gg * DM * OWN + (128 * dchunk) * OWN + lc,
                                           [[OWN, 128], [1, 512]])
                        mrb = p.dma("dg", rb, srcp, waits=p.bar("dg"))
                        st, sp = dchunk == 0, dchunk == 7
                        wl = p.bar("t") if dchunk == 0 else [mrb]
                        p.mm(PS[0][:, :], WPS[:, 384 * dchunk:384 * dchunk + 128],
                             rb, st, sp, waits=wl, fin=sp)
                        p.mm(PS[1][:, :], WPS[:, 384 * dchunk + 128:384 * dchunk + 256],
                             rb, st, sp, waits=[], fin=sp)
                        p.mm(PS[2][:, :], WPS[:, 384 * dchunk + 256:384 * dchunk + 384],
                             rb, st, sp, waits=[], fin=True)
                    if layer == 0:
                        p.act(QP[:, 512 * tsl:512 * tsl + 512], PS[0][:, :],
                              AF.Identity, bias=BIA[0:128, 4:5], scale=QS,
                              waits=p.bar("s"))
                        p.ts(bass.AP(M1, KR0 + (T - 1 - 512 * tsl),
                                     [[24576, 128], [-1, 512]]),
                             PS[1][:, :], BIA[0:128, 1:2], None, AL.add,
                             waits=p.bar("v"))
                        p.ts(bass.AP(M1, VR0 + (T - 1 - 512 * tsl),
                                     [[24576, 128], [-1, 512]]),
                             PS[2][:, :], BIA[0:128, 2:3], None, AL.add,
                             waits=p.bar("v"))
                    else:
                        p.act(QP[:, 512 * tsl:512 * tsl + 512], PS[0][:, :],
                              AF.Copy, scale=QS, waits=p.bar("s"))
                        p.vcopy(bass.AP(M1, KR0 + (T - 1 - 512 * tsl),
                                        [[24576, 128], [-1, 512]]),
                                PS[1][:, :], waits=p.bar("v"))
                        p.vcopy(bass.AP(M1, VR0 + (T - 1 - 512 * tsl),
                                        [[24576, 128], [-1, 512]]),
                                PS[2][:, :], waits=p.bar("v"))
                for kc in range(NKC):
                    p.tr(PSB[0:128, 0:128],
                         M1[0:128, VR0 + 128 * kc:VR0 + 128 * kc + 128],
                         IDT[:, :], waits=p.bar("t"))
                    p.vcopy(M2[0:128, VT0 + 128 * kc:VT0 + 128 * kc + 128],
                            PSB[0:128, 0:128], waits=p.bar("v"))
                for hh in range(2):
                    head = 2 * pr + hh
                    rb0 = 64 * hh
                    tp_row = (rb0, 0)
                    tp_col = None if hh == 0 else (0, 64)
                    ob0 = 64 * hh
                    mwin = p.dma("ds", WINB[:, :],
                                 bass.AP(Fd, head * FLEN + WINBASE,
                                         [[1, 128], [1, WINW]]),
                                 waits=p.bar("ds"))
                    for g in range(8):
                        q0 = 512 * g
                        kmin = KCMIN[g]
                        nk = NKC - kmin
                        PVb = PS[2]
                        SGb = PS[6]
                        slab = M1 if hh == 0 else M2
                        slab0 = PTA0 if hh == 0 else PTB0
                        for i, kc in enumerate(range(kmin, NKC)):
                            sq = PS[0] if i % 2 == 0 else PS[4]
                            lhs = M1[rb0:rb0 + 64, KR0 + 128 * kc:KR0 + 128 * kc + 128]
                            rhs = QP[rb0:rb0 + 64, q0:q0 + 512]
                            p.mm(sq[:, :], lhs, rhs, True, True, tp=tp_row,
                                 waits=[mwin, ("s", p.cnt["s"]), ("v", p.cnt["v"])],
                                 fin=True)
                            mmS = ("t", p.cnt["t"])
                            ebuf = EB[0:128, 512 * (i % 2):512 * (i % 2) + 512]
                            ex = p.act(ebuf, sq[:, :], AF.Exp,
                                       waits=[mmS, ("v", p.cnt["v"])])
                            u0 = q0 + 128 * kc - 3456
                            if layer == 0:
                                pt = slab[0:128, slab0 + 512 * (i % 2):
                                          slab0 + 512 * (i % 2) + 512]
                            else:
                                pt = slab[0:128, slab0 + 512 * i:slab0 + 512 * i + 512]
                            mul = p.tt(pt, ebuf, WINB[0:128, u0:u0 + 512], AL.mult,
                                       waits=[ex, ("t", p.cnt["t"])])
                            p.mm(SGb[0:1, :], ONES128, pt, i == 0, i == nk - 1,
                                 waits=[mul], fin=(i == nk - 1))
                            if layer == 0:
                                vts = M2[0:128, VT0 + 128 * kc + 64 * hh:
                                         VT0 + 128 * kc + 64 * hh + 64]
                                p.mm(PVb[ob0:ob0 + 64, :], vts, pt,
                                     i == 0, i == nk - 1, tp=tp_col,
                                     waits=[mul], fin=(i == nk - 1))
                        if layer == 1:
                            rio = STAT[0:1, 0:512]
                            mrio = p.dma("ds", rio, riot_d[0:1, q0:q0 + 512],
                                         waits=p.bar("ds"))
                            cr0 = STAT[0:1, 512:1024]
                            p.tt(cr0, SGb[0:1, :], rio, AL.mult,
                                 waits=[mrio, ("t", p.cnt["t"]), ("s", p.cnt["s"])])
                            cr = STAT[0:1, 1024:1536]
                            p.ts(cr, cr0, BIA[0:1, 16 + head:17 + head], None,
                                 AL.mult, waits=p.bar("v"))
                            crb = ROWS[0:1, 128:640]
                            p.vcopy(crb, cr, waits=p.bar("v"))
                            p.mm(PS[1][:, :], ONES1, crb, True, True,
                                 waits=p.bar("t"), fin=True)
                            tb2 = EB[0:128, 1024:1536]
                            p.scopy(tb2, PS[1][:, :], waits=p.bar("s"))
                            etb = ("s", p.cnt["s"])
                            for i in range(nk):
                                kc = kmin + i
                                pt = slab[0:128, slab0 + 512 * i:slab0 + 512 * i + 512]
                                pb1 = PBUF[0:128, 0:512]
                                pb2 = PBUF[0:128, 512:1024]
                                p.tt(pb1, pt, tb2, AL.add,
                                     waits=[etb, ("t", p.cnt["t"])])
                                mx = p.ts(pb2, pb1, 0.0, None, AL.max,
                                          waits=[("v", p.cnt["v"])])
                                vts = M2[0:128, VT0 + 128 * kc + 64 * hh:
                                         VT0 + 128 * kc + 64 * hh + 64]
                                p.mm(PVb[ob0:ob0 + 64, :], vts, pb2,
                                     i == 0, i == nk - 1, tp=tp_col,
                                     waits=[mx], fin=True)
                        rc = STAT[0:1, 1536:2048]
                        p.recip_pos(rc, SGb[0:1, :], STAT[0:1, 0:512],
                                    waits=p.bar("s"))
                        rcb = ROWS[0:1, 640:1152]
                        p.vcopy(rcb, rc, waits=p.bar("v"))
                        p.mm(PS[5][ob0:ob0 + 64, :], ONES1[0:1, 0:64], rcb,
                             True, True, tp=tp_col, waits=p.bar("t"), fin=True)
                        rbuf = EB[0:128, 1536:2048]
                        p.scopy(rbuf[ob0:ob0 + 64, 0:512], PS[5][ob0:ob0 + 64, :],
                                waits=p.bar("s"))
                        ycol = 4096 * pr + 512 * g
                        p.tt(Y[rb0:rb0 + 64, ycol:ycol + 512], PVb[ob0:ob0 + 64, :],
                             rbuf[ob0:ob0 + 64, 0:512], AL.mult, waits=p.bar("v"))

        def proj_partial(layer):
            wsrc = l0proj if layer == 0 else l1proj
            dst = pp_in if layer == 0 else pp_in1
            cout = pp_out if layer == 0 else pp_out1
            Wp = M1[0:128, 0:4096]
            mload = []
            for c in range(4):
                mload.append(p.dma("dg", Wp[:, 1024 * c:1024 * c + 1024],
                                   bass.AP(wsrc, (128 * c) * DM, [[DM, 128], [1, 1024]]),
                                   waits=p.bar("dg")))
            if layer == 0:
                mload.append(p.dma("dg", BIA[0:128, 8:16], l0pb[:, :],
                                   waits=p.bar("dg")))
            mst = None
            for od in range(8):
                for tsl in range(8):
                    bank = PS[tsl % 2]
                    for c in range(4):
                        p.mm(bank[:, :], Wp[:, 1024 * c + 128 * od:
                                            1024 * c + 128 * od + 128],
                             Y[:, 4096 * c + 512 * tsl:4096 * c + 512 * tsl + 512],
                             c == 0, c == 3,
                             waits=(mload + [("s", p.cnt["s"]), ("v", p.cnt["v"])]
                                    if c == 0 else []),
                             fin=(c == 3))
                    mmf = ("t", p.cnt["t"])
                    ob = EB[0:128, 0:512]
                    gg2 = tsl // 4
                    lc = 512 * (tsl % 4)
                    wl_ev = [mmf] + ([mst] if mst is not None else [])
                    if layer == 0:
                        ev = p.act(ob, bank[:, :], AF.Identity,
                                   bias=BIA[0:128, 8 + od:9 + od], waits=wl_ev)
                    else:
                        ev = p.scopy(ob, bank[:, :], waits=wl_ev)
                    mst = p.dma("ds",
                                bass.AP(dst, gg2 * DM * OWN + (128 * od) * OWN + lc,
                                        [[OWN, 128], [1, 512]]),
                                ob, waits=[ev])
            p.op("g", lambda e: e.collective_compute(
                "ReduceScatter", AL.add,
                replica_groups=[[0, 1], [2, 3], [4, 5], [6, 7]],
                ins=[dst.ap().opt()], outs=[cout.ap().opt()]),
                waits=p.bar("cc"), incs=(("cc", 1),))
            return ("cc", p.cnt["cc"])

        def gate_up_h(w_gate_d, w_up_d, nhh, tcols, tw, scale_g, scale_u,
                      e_idx=None):
            cap = 20480 // tw

            def slot(i):
                if i < 0:
                    return WINB[0:128, 2 * tw:3 * tw]
                if i < cap:
                    return M2[0:128, tw * i:tw * i + tw]
                return WINB[0:128, tw * (i - cap):tw * (i - cap) + tw]

            NC_ = nhh * 128
            wstride = int(w_gate_d.shape[-1])

            def wslab(src_d):
                toks = []
                for c in range(8):
                    base = (e_idx * DM * MH if e_idx is not None else 0)
                    toks.append(p.dma("dg", M1[0:128, NC_ * c:NC_ * (c + 1)],
                                      bass.AP(src_d, base + (128 * c) * wstride,
                                              [[wstride, 128], [1, NC_]]),
                                      waits=p.bar("dg")))
                return toks

            nts = tw // 512
            mw = wslab(w_gate_d)
            for hh2 in range(nhh):
                for tsub in range(nts):
                    bank = PS[(hh2 * nts + tsub) % 2]
                    for c in range(8):
                        p.mm(bank[:, :],
                             M1[0:128, NC_ * c + 128 * hh2:NC_ * c + 128 * hh2 + 128],
                             XN[0:128, 2048 * c + tcols + 512 * tsub:
                                2048 * c + tcols + 512 * tsub + 512],
                             c == 0, c == 7,
                             waits=(mw + [("s", p.cnt["s"])] if c == 0 else []),
                             fin=(c == 7))
                    p.act(slot(hh2)[:, 512 * tsub:512 * tsub + 512], bank[:, :],
                          AF.Silu, scale=scale_g, waits=[("t", p.cnt["t"])])
            mw = wslab(w_up_d)
            for hh2 in range(nhh):
                for tsub in range(nts):
                    bank = PS[(hh2 * nts + tsub) % 2]
                    for c in range(8):
                        p.mm(bank[:, :],
                             M1[0:128, NC_ * c + 128 * hh2:NC_ * c + 128 * hh2 + 128],
                             XN[0:128, 2048 * c + tcols + 512 * tsub:
                                2048 * c + tcols + 512 * tsub + 512],
                             c == 0, c == 7,
                             waits=(mw + [("s", p.cnt["s"]), ("v", p.cnt["v"])]
                                    if c == 0 else []),
                             fin=(c == 7))
                    ue = EB[0:128, 512 * (tsub % 2):512 * (tsub % 2) + 512]
                    p.act(ue, bank[:, :], AF.Copy, scale=scale_u,
                          waits=[("t", p.cnt["t"]), ("v", p.cnt["v"])])
                    p.tt(slot(hh2 - 1)[:, 512 * tsub:512 * tsub + 512],
                         ue, slot(hh2)[:, 512 * tsub:512 * tsub + 512], AL.mult,
                         waits=[("s", p.cnt["s"]), ("t", p.cnt["t"])])
            return slot

        if L0:
            layernorm(x_fm, T, None, xn0_d)
            layernorm(x_own, OWN, XN, None)
            attention(0)
            cc_l0 = proj_partial(0)
            sG = 1.0 / 127.0
            sU = sG * sG
            for tb in range(4):
                slot = gate_up_h(wg_t, wu_t, 24, 512 * tb, 512, sG, sU)
                msd = []
                for c in range(24):
                    msd.append(p.dma("dg", M1[0:128, 1024 * c:1024 * c + 1024],
                                     bass.AP(wd_t, (128 * c) * DM, [[DM, 128], [1, 1024]]),
                                     waits=p.bar("dg")))
                for od in range(8):
                    bank = PS[od % 2]
                    for c in range(24):
                        p.mm(bank[:, :],
                             M1[0:128, 1024 * c + 128 * od:1024 * c + 128 * od + 128],
                             slot(c - 1)[:, 0:512], c == 0, c == 23,
                             waits=(msd + [("v", p.cnt["v"])] if c == 0 else []),
                             fin=(c == 23))
                    mmf = ("t", p.cnt["t"])
                    xb = SCR[0:128, 0:512]
                    mx_ = p.dma("ds", xb,
                                bass.AP(x_own, (128 * od) * OWN + 512 * tb,
                                        [[OWN, 128], [1, 512]]),
                                waits=p.bar("ds"))
                    rb = EB[0:128, 1024:1536]
                    mr = p.dma("ds", rb,
                               bass.AP(pp_out, (128 * od) * OWN + 512 * tb,
                                       [[OWN, 128], [1, 512]]),
                               waits=[cc_l0] + p.bar("ds"))
                    t1 = SCR[0:128, 512:1024]
                    p.tt(t1, bank[:, :], rb, AL.add,
                         waits=[mmf, mr, mx_, ("v", p.cnt["v"])])
                    t2 = SCR[0:128, 2048:2560]
                    p.tt(t2, t1, xb, AL.add, waits=p.bar("v"))
                    dd = x1_out if mode == "L0" else x1_d
                    p.dma("ds", bass.AP(dd, (128 * od) * OWN + 512 * tb,
                                        [[OWN, 128], [1, 512]]),
                          t2, waits=p.bar("ds"))

        if L1:
            x1src = x1_in if mode == "L1" else x1_d
            layernorm(x1src, OWN, XN, xn1_in)
            p.op("g", lambda e: e.collective_compute(
                "AllGather", AL.bypass,
                replica_groups=[[0, 1], [2, 3], [4, 5], [6, 7]],
                ins=[xn1_in.ap().opt()], outs=[xn1_g.ap().opt()]),
                waits=p.bar("cc"), incs=(("cc", 1),))
            if "l1attn" not in skip:
                attention(1)
            cc_l1 = proj_partial(1)
            mwg = p.dma("dg", ROWS[0:128, 1152:1216], wg8[:, :], waits=p.bar("dg"))
            for tsl in range(4):
                for c in range(8):
                    p.mm(PS[tsl % 4][0:8, :], ROWS[0:128, 1152 + 8 * c:1160 + 8 * c],
                         XN[0:128, 2048 * c + 512 * tsl:2048 * c + 512 * tsl + 512],
                         c == 0, c == 7,
                         waits=([mwg, ("s", p.cnt["s"]), ("v", p.cnt["v"])]
                                if c == 0 else []),
                         fin=(c == 7))
                p.scopy(SCR[0:8, 512 * tsl:512 * tsl + 512], PS[tsl % 4][0:8, :],
                        waits=p.bar("s"))
            lg_done = ("s", p.cnt["s"])
            for tc in range(16):
                lgb = ROWS[0:8, 1280:1408]
                p.vcopy(lgb, SCR[0:8, 128 * tc:128 * tc + 128],
                        waits=[lg_done] + p.bar("v"))
                p.tr(PSB[0:128, 0:8], lgb, IDT[0:8, 0:8],
                     waits=p.bar("t"))
                ltm = ROWS[0:128, 1216:1224]
                p.vcopy(ltm, PSB[0:128, 0:8], waits=p.bar("v"))
                TRE = ROWS[0:128, 1408:1424]
                TREF = SCR[0:128, 2048:2072]

                def tree8(dst_col, tile, alu):
                    p.tt(TRE[0:128, 0:4], tile[0:128, 0:4], tile[0:128, 4:8], alu,
                         waits=p.bar("v"))
                    p.tt(TRE[0:128, 4:6], TRE[0:128, 0:2], TRE[0:128, 2:4], alu,
                         waits=p.bar("v"))
                    p.tt(dst_col, TRE[0:128, 4:5], TRE[0:128, 5:6], alu,
                         waits=p.bar("v"))

                m0 = TREF[0:128, 0:1]
                tree8(m0, ltm, AL.max)
                nm = TREF[0:128, 1:2]
                p.ts(nm, m0, -1.0, None, AL.mult, waits=p.bar("v"))
                e8 = ROWS[0:128, 1224:1232]
                p.act(e8, ltm, AF.Exp, bias=nm, waits=p.bar("s"))
                s8 = TREF[0:128, 2:3]
                tree8(s8, e8, AL.add)
                ri = TREF[0:128, 3:4]
                p.recip_pos(ri, s8, TREF[0:128, 4:5], waits=p.bar("s"))
                p8 = ROWS[0:128, 1232:1240]
                p.ts(p8, e8, ri, None, AL.mult, waits=p.bar("v"))
                m1 = TREF[0:128, 5:6]
                tree8(m1, p8, AL.max)
                eq = TREF[0:128, 6:14]
                p.ts(eq, p8, m1, None, AL.is_equal, waits=p.bar("v"))
                pm = ROWS[0:128, 1248:1256]
                p.stt(pm, eq, -2.0, p8, AL.mult, AL.add, waits=p.bar("v"))
                m2 = TREF[0:128, 14:15]
                tree8(m2, pm, AL.max)
                keep = ROWS[0:128, 1256:1264]
                p.ts(keep, p8, m2, None, AL.is_ge, waits=p.bar("v"))
                cwu = ROWS[0:128, 1264:1272]
                p.tt(cwu, p8, keep, AL.mult, waits=p.bar("v"))
                s2 = TREF[0:128, 15:16]
                tree8(s2, cwu, AL.add)
                ri2 = TREF[0:128, 16:17]
                p.recip_pos(ri2, s2, TREF[0:128, 4:5], waits=p.bar("s"))
                cwt = ROWS[0:128, 1272:1280]
                p.ts(cwt, cwu, ri2, None, AL.mult, waits=p.bar("v"))
                if dbg_out is not None and tc == 0:
                    for k, tile in enumerate([ltm, e8, p8, pm, keep, cwu, cwt]):
                        p.vcopy(PBUF[0:128, 8 * k:8 * k + 8], tile, waits=p.bar("v"))
                    p.vcopy(PBUF[0:128, 56:64], eq[0:128, 0:8], waits=p.bar("v"))
                    for k in range(8):
                        p.vcopy(PBUF[0:128, 64 + k:65 + k],
                                TREF[0:128, k:k + 1], waits=p.bar("v"))
                p.tr(PSB[0:8, 128:256], cwt, IDT[:, :], waits=p.bar("t"))
                p.vcopy(SMB[0:8, 128 * tc:128 * tc + 128], PSB[0:8, 128:256],
                        waits=p.bar("v"))
            cw_done = ("v", p.cnt["v"])
            for tb in range(2):
                for e_i in ([] if "moe" in skip else range(NE)):
                    slot = gate_up_h(w1t, w3t, 22, 1024 * tb, 1024, 1.0, 1.0,
                                     e_idx=e_i)
                    msd = []
                    for c in range(22):
                        msd.append(p.dma("dg", M1[0:128, 1024 * c:1024 * c + 1024],
                                         bass.AP(w2t, e_i * MH * DM + (128 * c) * DM,
                                                 [[DM, 128], [1, 1024]]),
                                         waits=p.bar("dg")))
                    if "cw" in skip:
                        p.vmemset(EB[0:128, 1024:1536], 0.125, waits=p.bar("v"))
                        p.vmemset(EB[0:128, 1536:2048], 0.125, waits=p.bar("v"))
                        ecw = ("v", p.cnt["v"])
                    else:
                        mcw = p.dma("dg", ROWS[0:1, 128:1152],
                                    bass.AP(SMB, e_i * 2048 + 1024 * tb,
                                            [[2048, 1], [1, 1024]]),
                                    waits=[cw_done] + p.bar("dg"))
                        p.mm(PS[2][:, :], ONES1, ROWS[0:1, 128:640], True, True,
                             waits=[mcw, ("s", p.cnt["s"])], fin=True)
                        p.mm(PS[3][:, :], ONES1, ROWS[0:1, 640:1152], True, True,
                             waits=[mcw], fin=True)
                        p.scopy(EB[0:128, 1024:1536], PS[2][:, :], waits=p.bar("s"))
                        p.scopy(EB[0:128, 1536:2048], PS[3][:, :], waits=p.bar("s"))
                        ecw = ("s", p.cnt["s"])
                    for od in range(8):
                        for tsub in range(2):
                            bank = PS[(od * 2 + tsub) % 2]
                            for c in range(22):
                                p.mm(bank[:, :],
                                     M1[0:128, 1024 * c + 128 * od:
                                        1024 * c + 128 * od + 128],
                                     slot(c - 1)[:, 512 * tsub:512 * tsub + 512],
                                     c == 0, c == 21,
                                     waits=(msd + [("v", p.cnt["v"]),
                                            ("s", p.cnt["s"])] if c == 0 else []),
                                     fin=(c == 21))
                            mmf = ("t", p.cnt["t"])
                            cwb = (EB[0:128, 1024:1536] if tsub == 0
                                   else EB[0:128, 1536:2048])
                            ycol = 8192 * (e_i % 2) + 1024 * od + 512 * tsub
                            yprev = 8192 * ((e_i + 1) % 2) + 1024 * od + 512 * tsub
                            if e_i == 0:
                                p.tt(Y[0:128, ycol:ycol + 512], bank[:, :], cwb,
                                     AL.mult, waits=[mmf, ecw, ("v", p.cnt["v"])])
                            else:
                                tmp = PBUF[0:128, 0:512]
                                p.tt(tmp, bank[:, :], cwb, AL.mult,
                                     waits=[mmf, ecw, ("v", p.cnt["v"])])
                                p.tt(Y[0:128, ycol:ycol + 512],
                                     Y[0:128, yprev:yprev + 512], tmp, AL.add,
                                     waits=p.bar("v"))
                ylast = 8192 * ((NE - 1) % 2)
                for od in range(8):
                    for tsub in range(2):
                        xb = SCR[0:128, 0:512]
                        mx_ = p.dma("ds", xb,
                                    bass.AP(x1src, (128 * od) * OWN + 1024 * tb
                                            + 512 * tsub,
                                            [[OWN, 128], [1, 512]]),
                                    waits=p.bar("ds"))
                        rb = EB[0:128, 0:512]
                        mr = p.dma("ds", rb,
                                   bass.AP(pp_out1, (128 * od) * OWN + 1024 * tb
                                           + 512 * tsub,
                                           [[OWN, 128], [1, 512]]),
                                   waits=[cc_l1] + p.bar("ds"))
                        t1 = SCR[0:128, 512:1024]
                        p.tt(t1, xb, rb, AL.add, waits=[mr, mx_, ("v", p.cnt["v"]),
                                                        ("t", p.cnt["t"])])
                        ycol = ylast + 1024 * od + 512 * tsub
                        t2 = SCR[0:128, 2048:2560]
                        p.tt(t2, t1, Y[0:128, ycol:ycol + 512], AL.add,
                             waits=p.bar("v"))
                        p.dma("ds", bass.AP(y_out, (128 * od) * OWN + 1024 * tb
                                            + 512 * tsub,
                                            [[OWN, 128], [1, 512]]),
                              t2, waits=p.bar("ds"))

        if dbg_out is not None:
            p.dma("dg", dbg2_out[:, 0:72], PBUF[0:128, 0:72], waits=p.bar("dg"))
            p.dma("dg", bass.AP(dbg_out, 0, [[2048, 8], [1, 2048]]),
                  SMB[0:8, 0:2048], waits=p.bar("dg"))
            p.dma("dg", bass.AP(dbg_out, 8 * 2048, [[2048, 8], [1, 2048]]),
                  SCR[0:8, 0:2048], waits=p.bar("dg"))
            p.dma("dg", bass.AP(dbg_out, 16 * 2048, [[2048, 8], [1, 2048]]),
                  XN[0:8, 0:2048], waits=p.bar("dg"))
        p.op("ds", lambda e: e.nop(), waits=p.bar(), incs=())

        @block.tensor
        def _(t):
            p.replay_eng = None
            for e, fn in p.sched:
                if e == "t":
                    fn(t)

        @block.vector
        def _(v):
            for e, fn in p.sched:
                if e == "v":
                    fn(v)

        @block.scalar
        def _(s):
            for e, fn in p.sched:
                if e == "s":
                    fn(s)

        @block.gpsimd
        def _(g):
            for e, fn in p.sched:
                if e in ("dg", "g", "cc"):
                    fn(g)

        @block.sync
        def _(s):
            for e, fn in p.sched:
                if e == "ds":
                    fn(s)

    nc._input_names = input_names
    return nc



def _run(mode, in_maps):
    nc = build_program(mode)
    names = set(nc._input_names)
    in_maps = [{k: v for k, v in m.items() if k in names} for m in in_maps]
    return run_bass_kernel_spmd(nc, in_maps, core_ids=list(range(8)))


def kernel(**inputs):
    in_maps = [stage_core(inputs, c) for c in range(8)]
    res = _run("FUSED", in_maps)
    out = np.empty((4, T, DM), np.float32)
    for c in range(8):
        b, h = c // 2, c % 2
        y = np.asarray(res.results[c]["y_out"], np.float32)
        out[b, 2048 * h:2048 * h + 2048, :] = y.T
    return out


# version 04
# speedup vs baseline: 1.1223x; 1.1223x over previous
import math
from contextlib import ExitStack

import numpy as np

import concourse.bass as bass
import concourse.mybir as mybir
from concourse.bass_utils import run_bass_kernel_spmd
from concourse.alu_op_type import AluOpType as AL

F32 = mybir.dt.float32
BF16 = mybir.dt.bfloat16
I32 = mybir.dt.int32
AF = mybir.ActivationFunctionType
AX = mybir.AxisListType

T = 4096
DM = 1024
FH = 3072
MH = 2816
NE = 8
FLEN = 8448
FPAD = 4223
WINW = 4608
WINBASE = 3584
SHIFT = 16.0
QS = 0.125
OWN = 2048
EPS = 1e-5
NKC = 32

KCMIN = [max(0, (3457 - 512 * g + 127) // 128) for g in range(8)]



def _bucket_np():
    d = np.arange(T)
    rp = np.maximum(d, 1).astype(np.float32)
    large = 16 + (np.log(rp / np.float32(16.0))
                  / np.float32(math.log(8.0)) * 16).astype(np.int32)
    large = np.minimum(large, 31)
    return np.where(d < 16, d, large)


def stage_core(inputs, core):
    b, h = core // 2, core % 2
    x = np.ascontiguousarray(np.asarray(inputs["L0_attn_x"][b]).T.astype(np.float32, copy=False))
    d = {}
    d["x_fm"] = x
    d["x_own"] = np.ascontiguousarray(x[:, 2048 * h:2048 * h + 2048])

    Wa = np.asarray(inputs["L0_attn_W_attn"])
    ba = np.asarray(inputs["L0_attn_b_attn"])
    l0w = np.empty((4, 1024, 384), np.float32)
    l0b = np.empty((4, 128, 3), np.float32)
    for p in range(4):
        P = 4 * h + p
        l0w[p, :, 0:128] = Wa[:, 128 * P:128 * P + 128]
        l0w[p, :, 128:256] = Wa[:, 1024 + 128 * P:1024 + 128 * P + 128]
        l0w[p, :, 256:384] = Wa[:, 2048 + 128 * P:2048 + 128 * P + 128]
        l0b[p, :, 0] = ba[128 * P:128 * P + 128]
        l0b[p, :, 1] = ba[1024 + 128 * P:1024 + 128 * P + 128]
        l0b[p, :, 2] = ba[2048 + 128 * P:2048 + 128 * P + 128]
    d["l0w"] = l0w
    d["l0b"] = l0b
    d["l0proj"] = np.ascontiguousarray(
        np.asarray(inputs["L0_attn_W_proj"])[512 * h:512 * h + 512, :])
    pb = np.asarray(inputs["L0_attn_b_proj"]).reshape(8, 128).T
    d["l0pb"] = np.ascontiguousarray(pb) if h == 0 else np.zeros((128, 8), np.float32)

    d["table_own"] = np.ascontiguousarray(
        np.asarray(inputs["L0_attn_bias_table"])[:, 8 * h:8 * h + 8])
    bk = _bucket_np()
    oh = np.zeros((32, T), np.float32)
    oh[bk, np.arange(T)] = 1.0
    d["oh0"] = oh

    d["wg_t"] = np.ascontiguousarray(np.asarray(inputs["L0_ffn_w_gate"]).T)
    d["wu_t"] = np.ascontiguousarray(np.asarray(inputs["L0_ffn_w_up"]).T)
    d["wd_t"] = np.ascontiguousarray(np.asarray(inputs["L0_ffn_w_down"]).T)

    l1w = np.empty((4, 1024, 384), np.float32)
    for p in range(4):
        P = 4 * h + p
        l1w[p, :, 0:128] = np.asarray(inputs["L1_attn_Wq"])[128 * P:128 * P + 128, :].T
        l1w[p, :, 128:256] = np.asarray(inputs["L1_attn_Wk"])[128 * P:128 * P + 128, :].T
        l1w[p, :, 256:384] = np.asarray(inputs["L1_attn_Wv"])[128 * P:128 * P + 128, :].T
    d["l1w"] = l1w
    d["l1proj"] = np.ascontiguousarray(
        np.asarray(inputs["L1_attn_Wo"]).T[512 * h:512 * h + 512, :])
    d["l1bias_own"] = np.ascontiguousarray(
        np.asarray(inputs["L1_attn_bias"])[8 * h:8 * h + 8, :])
    d["tau_own"] = np.ascontiguousarray(
        np.asarray(inputs["L1_attn_tau"])[8 * h:8 * h + 8].reshape(1, 8))
    d["riot"] = (1.0 / np.arange(1, T + 1, dtype=np.float64)).astype(np.float32).reshape(1, T)

    Wg = np.asarray(inputs["L1_ffn_Wg"])
    d["wg8"] = np.ascontiguousarray(
        Wg.T.reshape(8, 128, 8).transpose(1, 0, 2).reshape(128, 64))
    d["w1t"] = np.ascontiguousarray(np.transpose(np.asarray(inputs["L1_ffn_W1"]), (0, 2, 1)))
    d["w3t"] = np.ascontiguousarray(np.transpose(np.asarray(inputs["L1_ffn_W3"]), (0, 2, 1)))
    d["w2t"] = np.ascontiguousarray(np.transpose(np.asarray(inputs["L1_ffn_W2"]), (0, 2, 1)))
    d["ident"] = np.eye(128, dtype=np.float32)
    return d



class PB:
    ENGS = ("t", "v", "s", "g", "cc")
    NSLOT = 12

    def __init__(self, nc):
        self.nc = nc
        self.sched = []
        self.cnt = {}
        self.hdl = {}
        self.lastwait = {}
        self.dmaq = {}

    def sem(self, ctx, name):
        self.hdl[name] = ctx.enter_context(self.nc.semaphore(name))
        self.cnt[name] = 0

    def dsem(self, ctx, q):
        self.dmaq[q] = 0
        for i in range(self.NSLOT):
            n = f"{q}{i}"
            self.hdl[n] = ctx.enter_context(self.nc.semaphore(n))
            self.cnt[n] = 0

    def op(self, eng, build_fn, waits=(), incs=()):
        if eng in ("v", "s", "g"):
            waits = list(waits) + [(eng, self.cnt[eng])]
        w = []
        for sname, v in waits:
            assert v <= self.cnt[sname], (sname, v, self.cnt[sname])
            if v <= 0:
                continue
            key = (eng, sname)
            if self.lastwait.get(key, -1) >= v:
                continue
            self.lastwait[key] = v
            w.append((self.hdl[sname], v))
        iz = []
        for sname, amt in incs:
            self.cnt[sname] += amt
            iz.append((self.hdl[sname], amt))

        def fn(e, w=w, iz=iz, build_fn=build_fn):
            for hh, v in w:
                e.wait_ge(hh, v)
            ins = build_fn(e)
            for hh, amt in iz:
                ins.then_inc(hh, amt)
        self.sched.append((eng, fn))

    def bar(self, *exclude):
        ex = set(exclude)
        w = [(n, self.cnt[n]) for n in self.ENGS if n not in ex and self.cnt[n] > 0]
        for q in self.dmaq:
            if q in ex:
                continue
            for i in range(self.NSLOT):
                n = f"{q}{i}"
                if self.cnt[n] > 0:
                    w.append((n, self.cnt[n]))
        return w

    def dma(self, q, out, in_, waits=()):
        j = self.dmaq[q]
        self.dmaq[q] = j + 1
        n = f"{q}{j % self.NSLOT}"
        w = list(waits)
        if self.cnt[n] > 0:
            w.append((n, self.cnt[n]))
        self.op(q, lambda e: e.dma_start(out=out, in_=in_), w, ((n, 16),))
        return (n, self.cnt[n])

    def mm(self, out, lhsT, rhs, start, stop, tp=None, waits=(), fin=False):
        incs = (("t", 1),) if fin else ()
        self.op("t", lambda e: e.matmul(out, lhsT, rhs, start=start, stop=stop,
                                        tile_position=tp), waits, incs)
        return ("t", self.cnt["t"])

    def tr(self, out, in_, ident, waits=()):
        self.op("t", lambda e: e.transpose(out, in_, ident), waits, (("t", 1),))
        return ("t", self.cnt["t"])

    def act(self, out, in_, func, bias=0.0, scale=1.0, waits=()):
        self.op("s", lambda e: e.activation(out, in_, func, bias=bias, scale=scale),
                waits, (("s", 1),))
        return ("s", self.cnt["s"])

    def scopy(self, out, in_, waits=()):
        self.op("s", lambda e: e.copy(out, in_), waits, (("s", 1),))
        return ("s", self.cnt["s"])

    def tt(self, out, in0, in1, alu, waits=()):
        self.op("v", lambda e: e.tensor_tensor(out, in0, in1, alu), waits, (("v", 1),))
        return ("v", self.cnt["v"])

    def ts(self, out, in0, s1, s2, op0, op1=None, waits=()):
        if op1 is None:
            self.op("v", lambda e: e.tensor_scalar(out, in0, s1, s2, op0),
                    waits, (("v", 1),))
        else:
            self.op("v", lambda e: e.tensor_scalar(out, in0, s1, s2, op0, op1),
                    waits, (("v", 1),))
        return ("v", self.cnt["v"])

    def stt(self, out, in0, scalar, in1, op0, op1, waits=()):
        self.op("v", lambda e: e.scalar_tensor_tensor(out, in0, scalar, in1, op0, op1),
                waits, (("v", 1),))
        return ("v", self.cnt["v"])

    def vred(self, out, in_, alu, negate=False, waits=()):
        self.op("v", lambda e: e.tensor_reduce(out, in_, AX.X, alu, negate=negate),
                waits, (("v", 1),))
        return ("v", self.cnt["v"])

    def vcopy(self, out, in_, waits=()):
        self.op("v", lambda e: e.tensor_copy(out, in_), waits, (("v", 1),))
        return ("v", self.cnt["v"])

    def vrecip(self, out, in_, waits=()):
        self.op("v", lambda e: e.reciprocal(out, in_), waits, (("v", 1),))
        return ("v", self.cnt["v"])

    def recip_pos(self, out, in_, tmp, waits=()):
        self.act_ln = None
        self.op("s", lambda e: e.activation(tmp, in_, AF.Ln), waits, (("s", 1),))
        self.op("s", lambda e: e.activation(out, tmp, AF.Exp, scale=-1.0), (),
                (("s", 1),))
        return ("s", self.cnt["s"])

    def vmemset(self, ap, val, waits=()):
        self.op("v", lambda e: e.memset(ap, val), waits, (("v", 1),))
        return ("v", self.cnt["v"])



def build_program(mode, race_detect=False, skip=()):
    nc = bass.Bass("TRN2", target_bir_lowering=False, num_devices=8,
                   detect_race_conditions=race_detect)
    L0 = mode in ("L0", "FUSED")
    L1 = mode in ("L1", "FUSED")

    input_names = []

    def din(name, shape, dt):
        input_names.append(name)
        return nc.dram_tensor(name, shape, dt, kind="ExternalInput")

    ident_d = din("ident", [128, 128], F32)
    riot_d = din("riot", [1, T], F32)
    if L0:
        x_fm = din("x_fm", [DM, T], F32)
        x_own = din("x_own", [DM, OWN], F32)
        l0w = din("l0w", [4, DM, 384], F32)
        l0b = din("l0b", [4, 128, 3], F32)
        l0proj = din("l0proj", [512, DM], F32)
        l0pb = din("l0pb", [128, 8], F32)
        table_own = din("table_own", [32, 8], F32)
        oh0 = din("oh0", [32, T], F32)
        wg_t = din("wg_t", [DM, FH], I32)
        wu_t = din("wu_t", [DM, FH], I32)
        wd_t = din("wd_t", [FH, DM], I32)
    if L1:
        l1w = din("l1w", [4, DM, 384], F32)
        l1proj = din("l1proj", [512, DM], F32)
        l1bias_own = din("l1bias_own", [8, 2048], F32)
        tau_own = din("tau_own", [1, 8], F32)
        wg8 = din("wg8", [128, 64], F32)
        w1t = din("w1t", [NE, DM, MH], F32)
        w3t = din("w3t", [NE, DM, MH], F32)
        w2t = din("w2t", [NE, MH, DM], F32)
        if mode == "L1":
            x1_in = din("x1_in", [DM, OWN], F32)

    if mode == "L0":
        x1_out = nc.dram_tensor("x1_out", [DM, OWN], F32, kind="ExternalOutput")
    else:
        y_out = nc.dram_tensor("y_out", [DM, OWN], F32, kind="ExternalOutput")
    dbg_out = (nc.dram_tensor("dbg_out", [24, 2048], F32, kind="ExternalOutput")
               if "dbg" in skip else None)
    dbg2_out = (nc.dram_tensor("dbg2_out", [128, 128], F32, kind="ExternalOutput")
                if "dbg" in skip else None)

    F0d = nc.dram_tensor("F0d", [8, FLEN], BF16) if L0 else None
    F1d = nc.dram_tensor("F1d", [8, FLEN], BF16) if L1 else None
    xn0_d = nc.dram_tensor("xn0_d", [DM, T], BF16) if L0 else None
    x1_d = nc.dram_tensor("x1_d", [DM, OWN], F32)
    pp_in = nc.dram_tensor("pp_in", [2, DM, OWN], BF16)
    pp_out = nc.dram_tensor("pp_out", [DM, OWN], BF16)
    pp_in1 = nc.dram_tensor("pp_in1", [2, DM, OWN], BF16)
    pp_out1 = nc.dram_tensor("pp_out1", [DM, OWN], BF16)
    if L1:
        xn1_in = nc.dram_tensor("xn1_in", [DM, OWN], BF16)
        xn1_g = nc.dram_tensor("xn1_g", [2, DM, OWN], BF16)

    p = PB(nc)
    with ExitStack() as ctx:
        en = ctx.enter_context
        XN = en(nc.sbuf_tensor("XN", [128, 8 * 2048], BF16))
        Y = en(nc.sbuf_tensor("Y", [128, 4 * 4096], BF16))
        M1 = en(nc.sbuf_tensor("M1", [128, 24576], BF16))
        M2 = en(nc.sbuf_tensor("M2", [128, 20480], BF16))
        WINB = en(nc.sbuf_tensor("WINB", [128, WINW], BF16))
        QP = en(nc.sbuf_tensor("QP", [128, T], BF16))
        EB = en(nc.sbuf_tensor("EB", [128, 2048], BF16))
        PBUF = en(nc.sbuf_tensor("PBUF", [128, 1024], BF16))
        CWB = en(nc.sbuf_tensor("CWB", [128, 1024], BF16))
        STAT = en(nc.sbuf_tensor("STAT", [128, 2048], F32))
        SCR = en(nc.sbuf_tensor("SCR", [128, 3072], F32))
        SMB = en(nc.sbuf_tensor("SMB", [128, 2048], BF16))
        ROWS = en(nc.sbuf_tensor("ROWS", [128, 1664], BF16))
        IDT = en(nc.sbuf_tensor("IDT", [128, 128], BF16))
        BIA = en(nc.sbuf_tensor("BIA", [128, 32], F32))
        PS = [en(nc.psum_tensor(f"PS{i}", [128, 512], F32)) for i in range(7)]
        PSB = en(nc.psum_tensor("PSB", [128, 1024], BF16))

        for s in PB.ENGS:
            p.sem(ctx, s)
        p.dsem(ctx, "dg")
        p.dsem(ctx, "ds")
        block = en(nc.Block())

        KR0, VR0, PTA0 = 16384, 20480, 0
        VT0, PTB0 = 16384, 0

        ONES1 = ROWS[0:1, 0:128]
        ONES128 = ROWS[0:128, 0:1]
        ONESF = BIA[0:128, 26:27]

        m_id = p.dma("dg", IDT[:, :], ident_d[:, :])
        p.vmemset(ROWS[0:128, 0:128], 1.0)
        p.vmemset(ONESF, 1.0)
        p.vmemset(BIA[0:128, 24:25], -SHIFT)
        p.vmemset(BIA[0:128, 25:26], EPS)

        def build_F(Fd, is_l0):
            zt = SMB[0:16, 256:768]
            p.vmemset(zt, 0.0)
            for i in range(9):
                w = min(512, FPAD - 512 * i)
                p.dma("ds", bass.AP(Fd, 512 * i, [[FLEN, 8], [1, w]]),
                      zt[0:8, 0:w], waits=p.bar("ds"))
            if is_l0:
                tb = SMB[0:32, 0:16]
                p.dma("dg", tb[0:32, 0:8], table_own[:, :])
                ohc = SMB[0:32, 768:1280]
                for i in range(8):
                    p.dma("dg", ohc[:, :], oh0[:, 512 * i:512 * i + 512],
                          waits=p.bar("dg"))
                    bank = PS[0]
                    mmf = p.mm(bank[0:8, :], tb[0:32, 0:8], ohc[:, :], True, True,
                               waits=p.bar("t"), fin=True)
                    ev = p.act(SMB[0:8, 1280:1792], bank[0:8, :], AF.Exp,
                               bias=BIA[0:8, 24:25], waits=p.bar("s"))
                    p.dma("ds", bass.AP(Fd, FPAD + 512 * i, [[FLEN, 8], [1, 512]]),
                          SMB[0:8, 1280:1792], waits=[ev])
                p.dma("ds", bass.AP(Fd, FPAD + 4096, [[FLEN, 8], [1, FLEN - FPAD - 4096]]),
                      zt[0:8, 0:FLEN - FPAD - 4096], waits=p.bar("ds"))
            else:
                bc = SMB[0:8, 768:1280]
                for i in range(4):
                    p.dma("ds", SCR[0:8, 2048:2560],
                          l1bias_own[:, 512 * i:512 * i + 512], waits=p.bar("ds"))
                    ev = p.act(bc, SCR[0:8, 2048:2560], AF.Exp,
                               bias=BIA[0:8, 24:25], waits=p.bar("s"))
                    p.dma("ds", bass.AP(Fd, FPAD + 512 * i, [[FLEN, 8], [1, 512]]),
                          bc, waits=[ev])
                et = SMB[0:16, 1280:1792]
                p.vmemset(et, math.exp(-SHIFT), waits=p.bar("v"))
                off = FPAD + 2048
                rem = FLEN - off
                for i in range((rem + 511) // 512):
                    w = min(512, rem - 512 * i)
                    p.dma("ds", bass.AP(Fd, off + 512 * i, [[FLEN, 8], [1, w]]),
                          et[0:8, 0:w], waits=p.bar("ds"))

        if L0:
            build_F(F0d, True)
        if L1:
            build_F(F1d, False)

        def layernorm(src_d, ntok, out_sbuf, out_dram):
            stride = int(src_d.shape[1])
            nts = ntok // 512
            for tsl in range(nts):
                c0 = 512 * tsl
                for dchunk in range(8):
                    xa = SCR[0:128, 2048:2560]
                    p.dma("ds", xa,
                          bass.AP(src_d, (128 * dchunk) * stride + c0,
                                  [[stride, 128], [1, 512]]),
                          waits=p.bar("ds"))
                    p.mm(PS[0][0:1, :], ONESF, xa, dchunk == 0, dchunk == 7,
                         waits=p.bar("t"), fin=(dchunk == 7))
                    p.act(SCR[0:128, 2560:3072], xa, AF.Square, waits=p.bar("s"))
                    p.mm(PS[1][0:1, :], ONESF, SCR[0:128, 2560:3072],
                         dchunk == 0, dchunk == 7, waits=p.bar("t"),
                         fin=True)
                sA = STAT[0:1, 0:512]
                sB = STAT[0:1, 512:1024]
                sC = STAT[0:1, 1024:1536]
                sD = STAT[0:1, 1536:2048]
                p.ts(sA, PS[0][0:1, :], 1.0 / DM, None, AL.mult, waits=p.bar("v"))
                p.ts(sB, PS[1][0:1, :], 1.0 / DM, None, AL.mult, waits=p.bar("v"))
                p.tt(sC, sA, sA, AL.mult, waits=p.bar("v"))
                p.tt(sD, sB, sC, AL.subtract, waits=p.bar("v"))
                p.act(sC, sD, AF.Ln, bias=BIA[0:1, 25:26], waits=p.bar("s"))
                p.act(sD, sC, AF.Exp, scale=-0.5, waits=p.bar("s"))
                p.tt(sC, sA, sD, AL.mult, waits=p.bar("v"))
                ab = ROWS[0:1, 128:640]
                bb = ROWS[0:1, 640:1152]
                p.vcopy(ab, sD, waits=p.bar("v"))
                p.vcopy(bb, sC, waits=p.bar("v"))
                p.mm(PS[2][0:128, :], ONES1, ab, True, True,
                     waits=p.bar("t"), fin=True)
                p.mm(PS[3][0:128, :], ONES1, bb, True, True,
                     waits=p.bar("t"), fin=True)
                p.scopy(SCR[0:128, 2048:2560], PS[2][:, :], waits=p.bar("s"))
                p.scopy(SCR[0:128, 2560:3072], PS[3][:, :], waits=p.bar("s"))
                for dchunk in range(8):
                    xb = SCR[0:128, 0:512]
                    p.dma("ds", xb,
                          bass.AP(src_d, (128 * dchunk) * stride + c0,
                                  [[stride, 128], [1, 512]]),
                          waits=p.bar("ds"))
                    t1 = STAT[0:128, 0:512]
                    p.tt(t1, xb, SCR[0:128, 2048:2560], AL.mult, waits=p.bar("v"))
                    if out_sbuf is not None:
                        dst = out_sbuf[0:128, 2048 * dchunk + c0:
                                       2048 * dchunk + c0 + 512]
                    else:
                        dst = EB[0:128, 0:512]
                    p.tt(dst, t1, SCR[0:128, 2560:3072], AL.subtract,
                         waits=p.bar("v"))
                    if out_dram is not None:
                        p.dma("ds",
                              bass.AP(out_dram, (128 * dchunk) * int(out_dram.shape[1]) + c0,
                                      [[int(out_dram.shape[1]), 128], [1, 512]]),
                              dst, waits=p.bar("ds"))

        def attention(layer):
            wsrc = l0w if layer == 0 else l1w
            Fd = F0d if layer == 0 else F1d
            if layer == 1:
                p.dma("ds", BIA[0:1, 16:24], tau_own[:, :], waits=p.bar("ds"))
            for pr in range(4):
                WPS = WINB[0:128, 0:3072]
                mws = []
                for c in range(8):
                    mws.append(p.dma("dg", WPS[:, 384 * c:384 * c + 384],
                                     bass.AP(wsrc, pr * DM * 384 + (128 * c) * 384,
                                             [[384, 128], [1, 384]]),
                                     waits=p.bar("dg")))
                if layer == 0:
                    mws.append(p.dma("dg", BIA[0:128, 0:3],
                                     bass.AP(l0b, pr * 128 * 3, [[3, 128], [1, 3]]),
                                     waits=p.bar("dg")))
                    p.ts(BIA[0:128, 4:5], BIA[0:128, 0:1], QS, None, AL.mult,
                         waits=mws)
                evtok = {}
                rbtok = {}
                mmv_tok = {}
                for tsl in range(8):
                    par = tsl % 2
                    B0 = PS[3 * par]
                    B1 = PS[3 * par + 1]
                    B2 = PS[3 * par + 2]
                    mm_q = mm_k = mm_v = None
                    for dchunk in range(8):
                        rb = EB[0:128, 512 * (dchunk % 2):512 * (dchunk % 2) + 512]
                        if layer == 0:
                            srcp = bass.AP(xn0_d, (128 * dchunk) * T + 512 * tsl,
                                           [[T, 128], [1, 512]])
                        else:
                            gg = tsl // 4
                            lc = 512 * (tsl % 4)
                            srcp = bass.AP(xn1_g, gg * DM * OWN + (128 * dchunk) * OWN + lc,
                                           [[OWN, 128], [1, 512]])
                        key = (tsl * 8 + dchunk) % 2
                        prev = mmv_tok.get((tsl * 8 + dchunk) - 2)
                        mrb = p.dma("dg", rb, srcp,
                                    waits=([x for x in (prev,) if x]
                                           + (p.bar("dg") if tsl == 0 and dchunk < 2
                                              else [])))
                        st_, sp = dchunk == 0, dchunk == 7
                        wl = [mrb]
                        if st_:
                            wl += mws
                            for role in range(3):
                                tok = evtok.get((par, role))
                                if tok:
                                    wl.append(tok)
                        mm_q = p.mm(B0[:, :], WPS[:, 384 * dchunk:384 * dchunk + 128],
                                    rb, st_, sp, waits=wl, fin=sp)
                        mm_k = p.mm(B1[:, :], WPS[:, 384 * dchunk + 128:384 * dchunk + 256],
                                    rb, st_, sp, waits=[], fin=sp)
                        mm_v = p.mm(B2[:, :], WPS[:, 384 * dchunk + 256:384 * dchunk + 384],
                                    rb, st_, sp, waits=[], fin=True)
                        mmv_tok[tsl * 8 + dchunk] = ("t", p.cnt["t"])
                    if layer == 0:
                        evtok[(par, 0)] = p.act(QP[:, 512 * tsl:512 * tsl + 512],
                                                B0[:, :], AF.Identity,
                                                bias=BIA[0:128, 4:5], scale=QS,
                                                waits=[mm_q])
                        evtok[(par, 1)] = p.ts(
                            bass.AP(M1, KR0 + (T - 1 - 512 * tsl),
                                    [[24576, 128], [-1, 512]]),
                            B1[:, :], BIA[0:128, 1:2], None, AL.add, waits=[mm_k])
                        evtok[(par, 2)] = p.ts(
                            bass.AP(M1, VR0 + (T - 1 - 512 * tsl),
                                    [[24576, 128], [-1, 512]]),
                            B2[:, :], BIA[0:128, 2:3], None, AL.add, waits=[mm_v])
                    else:
                        evtok[(par, 0)] = p.act(QP[:, 512 * tsl:512 * tsl + 512],
                                                B0[:, :], AF.Identity, scale=QS,
                                                waits=[mm_q])
                        evtok[(par, 1)] = p.vcopy(
                            bass.AP(M1, KR0 + (T - 1 - 512 * tsl),
                                    [[24576, 128], [-1, 512]]),
                            B1[:, :], waits=[mm_k])
                        evtok[(par, 2)] = p.vcopy(
                            bass.AP(M1, VR0 + (T - 1 - 512 * tsl),
                                    [[24576, 128], [-1, 512]]),
                            B2[:, :], waits=[mm_v])
                kvdone = [evtok[(q, r)] for q in range(2) for r in range(3)]
                trtok = {}
                cptok = {}
                for kc in range(NKC):
                    pdst = (PSB[0:128, 0:128] if kc % 2 == 0
                            else PS[3].ap().bitcast(BF16)[0:128, 0:128])
                    wl = [x for x in (cptok.get(kc - 2),) if x]
                    if kc <= 1:
                        wl = wl + kvdone
                    trtok[kc] = p.tr(pdst, M1[0:128, VR0 + 128 * kc:VR0 + 128 * kc + 128],
                                     IDT[:, :], waits=wl)
                    cptok[kc] = p.vcopy(M2[0:128, VT0 + 128 * kc:VT0 + 128 * kc + 128],
                                        pdst, waits=[trtok[kc]])
                vt_done = [cptok[NKC - 1], cptok[NKC - 2]]
                extok = {}
                multok = {}
                sgb_free = None
                pvb_free = None
                er_prev = None
                for hh in range(2):
                    head = 2 * pr + hh
                    rb0 = 64 * hh
                    tp_row = (rb0, 0)
                    tp_col = None if hh == 0 else (0, 64)
                    ob0 = 64 * hh
                    mwin = p.dma("ds", WINB[:, :],
                                 bass.AP(Fd, head * FLEN + WINBASE,
                                         [[1, 128], [1, WINW]]),
                                 waits=list(multok.values()) + kvdone)
                    sqbanks = ([PS[0], PS[4], PS[3], PS[1]] if layer == 0
                               else [PS[0], PS[4], PS[3]])
                    npar = len(sqbanks)
                    for g in range(8):
                        q0 = 512 * g
                        kmin = KCMIN[g]
                        nk = NKC - kmin
                        PVb = PS[2]
                        SGb = PS[6]
                        slab = M1 if hh == 0 else M2
                        slab0 = PTA0 if hh == 0 else PTB0
                        for i, kc in enumerate(range(kmin, NKC)):
                            par = i % npar
                            sq = sqbanks[par]
                            lhs = M1[rb0:rb0 + 64, KR0 + 128 * kc:KR0 + 128 * kc + 128]
                            rhs = QP[rb0:rb0 + 64, q0:q0 + 512]
                            wl = [mwin] + [x for x in (extok.get(par),) if x]
                            if g == 0 and i < npar and hh == 0:
                                wl += kvdone + vt_done
                            mmS = p.mm(sq[:, :], lhs, rhs, True, True, tp=tp_row,
                                       waits=wl, fin=True)
                            ebuf = EB[0:128, 512 * par:512 * par + 512]
                            ex = p.act(ebuf, sq[:, :], AF.Exp,
                                       waits=[mmS] + [x for x in (multok.get(par),) if x])
                            extok[par] = ex
                            u0 = q0 + 128 * kc - 3456
                            if layer == 0:
                                pt = slab[0:128, slab0 + 512 * par:
                                          slab0 + 512 * par + 512]
                            else:
                                pt = slab[0:128, slab0 + 512 * i:slab0 + 512 * i + 512]
                            mul = p.tt(pt, ebuf, WINB[0:128, u0:u0 + 512], AL.mult,
                                       waits=[ex])
                            multok[par] = mul
                            p.mm(SGb[0:1, :], ONES128, pt, i == 0, i == nk - 1,
                                 waits=[mul] + ([x for x in (sgb_free,) if x]
                                                if i == 0 else []),
                                 fin=(i == nk - 1))
                            if layer == 0:
                                vts = M2[0:128, VT0 + 128 * kc + 64 * hh:
                                         VT0 + 128 * kc + 64 * hh + 64]
                                p.mm(PVb[ob0:ob0 + 64, :], vts, pt,
                                     i == 0, i == nk - 1, tp=tp_col,
                                     waits=[mul] + ([x for x in (pvb_free,) if x]
                                                    if i == 0 else []),
                                     fin=(i == nk - 1))
                        sg_done = ("t", p.cnt["t"])
                        if layer == 1:
                            rio = STAT[0:1, 0:512]
                            mrio = p.dma("ds", rio, riot_d[0:1, q0:q0 + 512],
                                         waits=p.bar("ds"))
                            cr0 = STAT[0:1, 512:1024]
                            t1_ = p.tt(cr0, SGb[0:1, :], rio, AL.mult,
                                       waits=[mrio, sg_done])
                            cr = STAT[0:1, 1024:1536]
                            t2_ = p.ts(cr, cr0, BIA[0:1, 16 + head:17 + head], None,
                                       AL.mult, waits=[t1_])
                            crb = ROWS[0:1, 128:640]
                            c1_ = p.vcopy(crb, cr, waits=[t2_])
                            mmB = p.mm(PS[1][:, :], ONES1, crb, True, True,
                                       waits=[c1_] + [x for x in (er_prev,) if x],
                                       fin=True)
                            tb2 = CWB[0:128, 0:512]
                            etb = p.scopy(tb2, PS[1][:, :], waits=[mmB])
                            for i in range(nk):
                                kc = kmin + i
                                pt = slab[0:128, slab0 + 512 * i:slab0 + 512 * i + 512]
                                pb1 = PBUF[0:128, 0:512]
                                pb2 = PBUF[0:128, 512:1024]
                                ad = p.tt(pb1, pt, tb2, AL.add, waits=[etb])
                                mx = p.ts(pb2, pb1, 0.0, None, AL.max,
                                          waits=[ad, ("t", p.cnt["t"])])
                                vts = M2[0:128, VT0 + 128 * kc + 64 * hh:
                                         VT0 + 128 * kc + 64 * hh + 64]
                                p.mm(PVb[ob0:ob0 + 64, :], vts, pb2,
                                     i == 0, i == nk - 1, tp=tp_col,
                                     waits=[mx] + ([x for x in (pvb_free,) if x]
                                                   if i == 0 else []),
                                     fin=True)
                        pv_done = ("t", p.cnt["t"])
                        rc = STAT[0:1, 1536:2048]
                        rctok = p.recip_pos(rc, SGb[0:1, :], STAT[0:1, 0:512]
                                            if layer == 0 else STAT[0:1, 512:1024],
                                            waits=[sg_done])
                        sgb_free = rctok
                        rcb = ROWS[0:1, 640:1152]
                        c2_ = p.vcopy(rcb, rc, waits=[rctok])
                        mmR = p.mm(PS[5][ob0:ob0 + 64, :], ONES1[0:1, 0:64], rcb,
                                   True, True, tp=tp_col,
                                   waits=[c2_] + [x for x in (er_prev,) if x],
                                   fin=True)
                        rbuf = CWB[0:128, 512:1024]
                        er = p.scopy(rbuf[ob0:ob0 + 64, 0:512], PS[5][ob0:ob0 + 64, :],
                                     waits=[mmR])
                        er_prev = er
                        ycol = 4096 * pr + 512 * g
                        pvb_free = p.tt(Y[rb0:rb0 + 64, ycol:ycol + 512],
                                        PVb[ob0:ob0 + 64, :],
                                        rbuf[ob0:ob0 + 64, 0:512], AL.mult,
                                        waits=[er, pv_done])

        def proj_partial(layer):
            wsrc = l0proj if layer == 0 else l1proj
            dst = pp_in if layer == 0 else pp_in1
            cout = pp_out if layer == 0 else pp_out1
            Wp = M1[0:128, 0:4096]
            mload = []
            for c in range(4):
                mload.append(p.dma("dg", Wp[:, 1024 * c:1024 * c + 1024],
                                   bass.AP(wsrc, (128 * c) * DM, [[DM, 128], [1, 1024]]),
                                   waits=p.bar("dg")))
            if layer == 0:
                mload.append(p.dma("dg", BIA[0:128, 8:16], l0pb[:, :],
                                   waits=p.bar("dg")))
            yready = [("v", p.cnt["v"]), ("s", p.cnt["s"])]
            bfreeP = {0: None, 1: None, 2: None, 3: None}
            obtok = {}
            first = True
            for od in range(8):
                for tsl in range(8):
                    bi = tsl % 4
                    bank = PS[bi]
                    for c in range(4):
                        wl = []
                        if c == 0:
                            wl = list(mload) + [x for x in (bfreeP[bi],) if x]
                            if first:
                                wl += yready
                        p.mm(bank[:, :], Wp[:, 1024 * c + 128 * od:
                                            1024 * c + 128 * od + 128],
                             Y[:, 4096 * c + 512 * tsl:4096 * c + 512 * tsl + 512],
                             c == 0, c == 3, waits=wl, fin=(c == 3))
                    first = False
                    mmf = ("t", p.cnt["t"])
                    oi = tsl % 2
                    ob = EB[0:128, 512 * oi:512 * oi + 512]
                    gg2 = tsl // 4
                    lc = 512 * (tsl % 4)
                    wl_ev = [mmf] + [x for x in (obtok.get(oi),) if x]
                    if layer == 0:
                        ev = p.act(ob, bank[:, :], AF.Identity,
                                   bias=BIA[0:128, 8 + od:9 + od], waits=wl_ev)
                    else:
                        ev = p.scopy(ob, bank[:, :], waits=wl_ev)
                    bfreeP[bi] = ev
                    obtok[oi] = p.dma("ds",
                                      bass.AP(dst, gg2 * DM * OWN + (128 * od) * OWN + lc,
                                              [[OWN, 128], [1, 512]]),
                                      ob, waits=[ev])
            p.op("g", lambda e: e.collective_compute(
                "ReduceScatter", AL.add,
                replica_groups=[[0, 1], [2, 3], [4, 5], [6, 7]],
                ins=[dst.ap().opt()], outs=[cout.ap().opt()]),
                waits=p.bar("cc"), incs=(("cc", 1),))
            return ("cc", p.cnt["cc"])

        def gate_up_h(w_gate_d, w_up_d, nhh, tcols, tw, scale_g, scale_u,
                      e_idx=None):
            cap = 20480 // tw

            def slot(i):
                if i < 0:
                    return WINB[0:128, 2 * tw:3 * tw]
                if i < cap:
                    return M2[0:128, tw * i:tw * i + tw]
                return WINB[0:128, tw * (i - cap):tw * (i - cap) + tw]

            NC_ = nhh * 128
            wstride = int(w_gate_d.shape[-1])

            def wslab(src_d):
                toks = []
                for c in range(8):
                    base = (e_idx * DM * MH if e_idx is not None else 0)
                    toks.append(p.dma("dg", M1[0:128, NC_ * c:NC_ * (c + 1)],
                                      bass.AP(src_d, base + (128 * c) * wstride,
                                              [[wstride, 128], [1, NC_]]),
                                      waits=p.bar("dg")))
                return toks

            nts = tw // 512
            nbank = 4
            bfree = {b: None for b in range(nbank)}
            mw = wslab(w_gate_d)
            for hh2 in range(nhh):
                for tsub in range(nts):
                    bi = (hh2 * nts + tsub) % nbank
                    bank = PS[bi]
                    for c in range(8):
                        wl = []
                        if c == 0:
                            wl = list(mw)
                            if bfree[bi] is not None:
                                wl.append(bfree[bi])
                        p.mm(bank[:, :],
                             M1[0:128, NC_ * c + 128 * hh2:NC_ * c + 128 * hh2 + 128],
                             XN[0:128, 2048 * c + tcols + 512 * tsub:
                                2048 * c + tcols + 512 * tsub + 512],
                             c == 0, c == 7, waits=wl, fin=(c == 7))
                    mmt = ("t", p.cnt["t"])
                    bfree[bi] = p.act(slot(hh2)[:, 512 * tsub:512 * tsub + 512],
                                      bank[:, :], AF.Silu, scale=scale_g,
                                      waits=[mmt])
            gate_done = list(bfree.values())
            bfree = {b: None for b in range(nbank)}
            mw = wslab(w_up_d)
            htok = {}
            uetok = {}
            for hh2 in range(nhh):
                for tsub in range(nts):
                    bi = (hh2 * nts + tsub) % nbank
                    bank = PS[bi]
                    for c in range(8):
                        wl = []
                        if c == 0:
                            wl = list(mw) + [x for x in (bfree[bi],) if x]
                            if hh2 == 0:
                                wl += [x for x in gate_done if x]
                        p.mm(bank[:, :],
                             M1[0:128, NC_ * c + 128 * hh2:NC_ * c + 128 * hh2 + 128],
                             XN[0:128, 2048 * c + tcols + 512 * tsub:
                                2048 * c + tcols + 512 * tsub + 512],
                             c == 0, c == 7, waits=wl, fin=(c == 7))
                    mmt = ("t", p.cnt["t"])
                    ui = (hh2 * nts + tsub) % 2
                    ue = EB[0:128, 512 * ui:512 * ui + 512]
                    ev = p.act(ue, bank[:, :], AF.Copy, scale=scale_u,
                               waits=[mmt] + [x for x in (uetok.get(ui),) if x])
                    bfree[bi] = ev
                    tk = p.tt(slot(hh2 - 1)[:, 512 * tsub:512 * tsub + 512],
                              ue, slot(hh2)[:, 512 * tsub:512 * tsub + 512],
                              AL.mult, waits=[ev])
                    htok[(hh2, tsub)] = tk
                    uetok[ui] = tk
            return slot, htok

        if L0:
            layernorm(x_fm, T, None, xn0_d)
            layernorm(x_own, OWN, XN, None)
            attention(0)
            cc_l0 = proj_partial(0)
            sG = 1.0 / 127.0
            sU = sG * sG
            for tb in range(4):
                slot, htok = gate_up_h(wg_t, wu_t, 24, 512 * tb, 512, sG, sU)
                msd = []
                for c in range(24):
                    msd.append(p.dma("dg", M1[0:128, 1024 * c:1024 * c + 1024],
                                     bass.AP(wd_t, (128 * c) * DM, [[DM, 128], [1, 1024]]),
                                     waits=p.bar("dg")))
                hall = [t for t in htok.values()]
                bfree2 = {0: None, 1: None, 2: None, 3: None}
                xbtok, rbtok, t1tok, t2tok = {}, {}, {}, {}
                for od in range(8):
                    bi = od % 4
                    bank = PS[bi]
                    for c in range(24):
                        wl = []
                        if c == 0:
                            wl = list(msd) + [x for x in (bfree2[bi],) if x]
                            if od == 0:
                                wl += hall
                        p.mm(bank[:, :],
                             M1[0:128, 1024 * c + 128 * od:1024 * c + 128 * od + 128],
                             slot(c - 1)[:, 0:512], c == 0, c == 23,
                             waits=wl, fin=(c == 23))
                    mmf = ("t", p.cnt["t"])
                    pi = od % 2
                    xb = SCR[0:128, 1024 + 512 * pi:1536 + 512 * pi]
                    mx_ = p.dma("ds", xb,
                                bass.AP(x_own, (128 * od) * OWN + 512 * tb,
                                        [[OWN, 128], [1, 512]]),
                                waits=[x for x in (xbtok.get(pi),) if x])
                    rb = EB[0:128, 1024 + 512 * pi:1536 + 512 * pi]
                    mr = p.dma("ds", rb,
                               bass.AP(pp_out, (128 * od) * OWN + 512 * tb,
                                       [[OWN, 128], [1, 512]]),
                               waits=[cc_l0] + [x for x in (rbtok.get(pi),) if x])
                    t1 = SCR[0:128, 512 * pi:512 * pi + 512]
                    tk1 = p.tt(t1, bank[:, :], rb, AL.add,
                               waits=[mmf, mr] + [x for x in (t1tok.get(pi),) if x])
                    bfree2[bi] = tk1
                    rbtok[pi] = tk1
                    t2 = SCR[0:128, 2048 + 512 * pi:2560 + 512 * pi]
                    tk2 = p.tt(t2, t1, xb, AL.add,
                               waits=[mx_] + [x for x in (t2tok.get(pi),) if x])
                    xbtok[pi] = tk2
                    t1tok[pi] = tk2
                    dd = x1_out if mode == "L0" else x1_d
                    t2tok[pi] = p.dma("ds", bass.AP(dd, (128 * od) * OWN + 512 * tb,
                                                    [[OWN, 128], [1, 512]]),
                                      t2, waits=[tk2])

        if L1:
            x1src = x1_in if mode == "L1" else x1_d
            layernorm(x1src, OWN, XN, xn1_in)
            p.op("g", lambda e: e.collective_compute(
                "AllGather", AL.bypass,
                replica_groups=[[0, 1], [2, 3], [4, 5], [6, 7]],
                ins=[xn1_in.ap().opt()], outs=[xn1_g.ap().opt()]),
                waits=p.bar("cc"), incs=(("cc", 1),))
            if "l1attn" not in skip:
                attention(1)
            cc_l1 = proj_partial(1)
            mwg = p.dma("dg", ROWS[0:128, 1152:1216], wg8[:, :], waits=p.bar("dg"))
            for tsl in range(4):
                for c in range(8):
                    p.mm(PS[tsl % 4][0:8, :], ROWS[0:128, 1152 + 8 * c:1160 + 8 * c],
                         XN[0:128, 2048 * c + 512 * tsl:2048 * c + 512 * tsl + 512],
                         c == 0, c == 7,
                         waits=([mwg, ("s", p.cnt["s"]), ("v", p.cnt["v"])]
                                if c == 0 else []),
                         fin=(c == 7))
                p.scopy(SCR[0:8, 512 * tsl:512 * tsl + 512], PS[tsl % 4][0:8, :],
                        waits=p.bar("s"))
            lg_done = ("s", p.cnt["s"])
            for tc in range(16):
                lgb = ROWS[0:8, 1280:1408]
                p.vcopy(lgb, SCR[0:8, 128 * tc:128 * tc + 128],
                        waits=[lg_done] + p.bar("v"))
                p.tr(PSB[0:128, 0:8], lgb, IDT[0:8, 0:8],
                     waits=p.bar("t"))
                ltm = ROWS[0:128, 1216:1224]
                p.vcopy(ltm, PSB[0:128, 0:8], waits=p.bar("v"))
                TRE = ROWS[0:128, 1408:1424]
                TREF = SCR[0:128, 2048:2072]

                def tree8(dst_col, tile, alu):
                    p.tt(TRE[0:128, 0:4], tile[0:128, 0:4], tile[0:128, 4:8], alu,
                         waits=p.bar("v"))
                    p.tt(TRE[0:128, 4:6], TRE[0:128, 0:2], TRE[0:128, 2:4], alu,
                         waits=p.bar("v"))
                    p.tt(dst_col, TRE[0:128, 4:5], TRE[0:128, 5:6], alu,
                         waits=p.bar("v"))

                m0 = TREF[0:128, 0:1]
                tree8(m0, ltm, AL.max)
                nm = TREF[0:128, 1:2]
                p.ts(nm, m0, -1.0, None, AL.mult, waits=p.bar("v"))
                e8 = ROWS[0:128, 1224:1232]
                p.act(e8, ltm, AF.Exp, bias=nm, waits=p.bar("s"))
                s8 = TREF[0:128, 2:3]
                tree8(s8, e8, AL.add)
                ri = TREF[0:128, 3:4]
                p.recip_pos(ri, s8, TREF[0:128, 4:5], waits=p.bar("s"))
                p8 = ROWS[0:128, 1232:1240]
                p.ts(p8, e8, ri, None, AL.mult, waits=p.bar("v"))
                m1 = TREF[0:128, 5:6]
                tree8(m1, p8, AL.max)
                eq = TREF[0:128, 6:14]
                p.ts(eq, p8, m1, None, AL.is_equal, waits=p.bar("v"))
                pm = ROWS[0:128, 1248:1256]
                p.stt(pm, eq, -2.0, p8, AL.mult, AL.add, waits=p.bar("v"))
                m2 = TREF[0:128, 14:15]
                tree8(m2, pm, AL.max)
                keep = ROWS[0:128, 1256:1264]
                p.ts(keep, p8, m2, None, AL.is_ge, waits=p.bar("v"))
                cwu = ROWS[0:128, 1264:1272]
                p.tt(cwu, p8, keep, AL.mult, waits=p.bar("v"))
                s2 = TREF[0:128, 15:16]
                tree8(s2, cwu, AL.add)
                ri2 = TREF[0:128, 16:17]
                p.recip_pos(ri2, s2, TREF[0:128, 4:5], waits=p.bar("s"))
                cwt = ROWS[0:128, 1272:1280]
                p.ts(cwt, cwu, ri2, None, AL.mult, waits=p.bar("v"))
                if dbg_out is not None and tc == 0:
                    for k, tile in enumerate([ltm, e8, p8, pm, keep, cwu, cwt]):
                        p.vcopy(PBUF[0:128, 8 * k:8 * k + 8], tile, waits=p.bar("v"))
                    p.vcopy(PBUF[0:128, 56:64], eq[0:128, 0:8], waits=p.bar("v"))
                    for k in range(8):
                        p.vcopy(PBUF[0:128, 64 + k:65 + k],
                                TREF[0:128, k:k + 1], waits=p.bar("v"))
                p.tr(PSB[0:8, 128:256], cwt, IDT[:, :], waits=p.bar("t"))
                p.vcopy(SMB[0:8, 128 * tc:128 * tc + 128], PSB[0:8, 128:256],
                        waits=p.bar("v"))
            cw_done = ("v", p.cnt["v"])
            for tb in range(2):
                for e_i in ([] if "moe" in skip else range(NE)):
                    slot, htok = gate_up_h(w1t, w3t, 22, 1024 * tb, 1024,
                                           1.0, 1.0, e_idx=e_i)
                    msd = []
                    for c in range(22):
                        msd.append(p.dma("dg", M1[0:128, 1024 * c:1024 * c + 1024],
                                         bass.AP(w2t, e_i * MH * DM + (128 * c) * DM,
                                                 [[DM, 128], [1, 1024]]),
                                         waits=p.bar("dg")))
                    if "cw" in skip:
                        p.vmemset(EB[0:128, 1024:1536], 0.125, waits=p.bar("v"))
                        p.vmemset(EB[0:128, 1536:2048], 0.125, waits=p.bar("v"))
                        ecw = ("v", p.cnt["v"])
                    else:
                        mcw = p.dma("dg", ROWS[0:1, 128:1152],
                                    bass.AP(SMB, e_i * 2048 + 1024 * tb,
                                            [[2048, 1], [1, 1024]]),
                                    waits=[cw_done] + p.bar("dg"))
                        p.mm(PS[2][:, :], ONES1, ROWS[0:1, 128:640], True, True,
                             waits=[mcw, ("s", p.cnt["s"])], fin=True)
                        p.mm(PS[3][:, :], ONES1, ROWS[0:1, 640:1152], True, True,
                             waits=[mcw], fin=True)
                        p.scopy(EB[0:128, 1024:1536], PS[2][:, :], waits=p.bar("s"))
                        p.scopy(EB[0:128, 1536:2048], PS[3][:, :], waits=p.bar("s"))
                        ecw = ("s", p.cnt["s"])
                    hall = list(htok.values())
                    bfree3 = {0: None, 1: None, 2: None, 3: None}
                    first = True
                    for od in range(8):
                        for tsub in range(2):
                            bi = (od * 2 + tsub) % 4
                            bank = PS[bi]
                            for c in range(22):
                                wl = []
                                if c == 0:
                                    wl = list(msd) + [x for x in (bfree3[bi],) if x]
                                    if first:
                                        wl += hall
                                p.mm(bank[:, :],
                                     M1[0:128, 1024 * c + 128 * od:
                                        1024 * c + 128 * od + 128],
                                     slot(c - 1)[:, 512 * tsub:512 * tsub + 512],
                                     c == 0, c == 21, waits=wl, fin=(c == 21))
                            first = False
                            mmf = ("t", p.cnt["t"])
                            cwb = (EB[0:128, 1024:1536] if tsub == 0
                                   else EB[0:128, 1536:2048])
                            ycol = 8192 * (e_i % 2) + 1024 * od + 512 * tsub
                            yprev = 8192 * ((e_i + 1) % 2) + 1024 * od + 512 * tsub
                            if e_i == 0:
                                bfree3[bi] = p.tt(Y[0:128, ycol:ycol + 512],
                                                  bank[:, :], cwb, AL.mult,
                                                  waits=[mmf, ecw])
                            else:
                                tmp = PBUF[0:128, 0:512]
                                bfree3[bi] = p.tt(tmp, bank[:, :], cwb, AL.mult,
                                                  waits=[mmf, ecw])
                                p.tt(Y[0:128, ycol:ycol + 512],
                                     Y[0:128, yprev:yprev + 512], tmp, AL.add,
                                     waits=[])
                ylast = 8192 * ((NE - 1) % 2)
                for od in range(8):
                    for tsub in range(2):
                        xb = SCR[0:128, 0:512]
                        mx_ = p.dma("ds", xb,
                                    bass.AP(x1src, (128 * od) * OWN + 1024 * tb
                                            + 512 * tsub,
                                            [[OWN, 128], [1, 512]]),
                                    waits=p.bar("ds"))
                        rb = EB[0:128, 0:512]
                        mr = p.dma("ds", rb,
                                   bass.AP(pp_out1, (128 * od) * OWN + 1024 * tb
                                           + 512 * tsub,
                                           [[OWN, 128], [1, 512]]),
                                   waits=[cc_l1] + p.bar("ds"))
                        t1 = SCR[0:128, 512:1024]
                        p.tt(t1, xb, rb, AL.add, waits=[mr, mx_, ("v", p.cnt["v"]),
                                                        ("t", p.cnt["t"])])
                        ycol = ylast + 1024 * od + 512 * tsub
                        t2 = SCR[0:128, 2048:2560]
                        p.tt(t2, t1, Y[0:128, ycol:ycol + 512], AL.add,
                             waits=p.bar("v"))
                        p.dma("ds", bass.AP(y_out, (128 * od) * OWN + 1024 * tb
                                            + 512 * tsub,
                                            [[OWN, 128], [1, 512]]),
                              t2, waits=p.bar("ds"))

        if dbg_out is not None:
            p.dma("dg", dbg2_out[:, 0:72], PBUF[0:128, 0:72], waits=p.bar("dg"))
            p.dma("dg", bass.AP(dbg_out, 0, [[2048, 8], [1, 2048]]),
                  SMB[0:8, 0:2048], waits=p.bar("dg"))
            p.dma("dg", bass.AP(dbg_out, 8 * 2048, [[2048, 8], [1, 2048]]),
                  SCR[0:8, 0:2048], waits=p.bar("dg"))
            p.dma("dg", bass.AP(dbg_out, 16 * 2048, [[2048, 8], [1, 2048]]),
                  XN[0:8, 0:2048], waits=p.bar("dg"))
        p.op("ds", lambda e: e.nop(), waits=p.bar(), incs=())

        @block.tensor
        def _(t):
            p.replay_eng = None
            for e, fn in p.sched:
                if e == "t":
                    fn(t)

        @block.vector
        def _(v):
            for e, fn in p.sched:
                if e == "v":
                    fn(v)

        @block.scalar
        def _(s):
            for e, fn in p.sched:
                if e == "s":
                    fn(s)

        @block.gpsimd
        def _(g):
            for e, fn in p.sched:
                if e in ("dg", "g", "cc"):
                    fn(g)

        @block.sync
        def _(s):
            for e, fn in p.sched:
                if e == "ds":
                    fn(s)

    nc._input_names = input_names
    return nc



def _run(mode, in_maps):
    nc = build_program(mode)
    names = set(nc._input_names)
    in_maps = [{k: v for k, v in m.items() if k in names} for m in in_maps]
    return run_bass_kernel_spmd(nc, in_maps, core_ids=list(range(8)))


def kernel(**inputs):
    in_maps = [stage_core(inputs, c) for c in range(8)]
    res = _run("FUSED", in_maps)
    out = np.empty((4, T, DM), np.float32)
    for c in range(8):
        b, h = c // 2, c % 2
        y = np.asarray(res.results[c]["y_out"], np.float32)
        out[b, 2048 * h:2048 * h + 2048, :] = y.T
    return out


# version 05
# speedup vs baseline: 1.4326x; 1.2765x over previous
import math
from contextlib import ExitStack

import numpy as np

import concourse.bass as bass
import concourse.mybir as mybir
from concourse.bass_utils import run_bass_kernel_spmd
from concourse.alu_op_type import AluOpType as AL

F32 = mybir.dt.float32
BF16 = mybir.dt.bfloat16
I32 = mybir.dt.int32
AF = mybir.ActivationFunctionType
AX = mybir.AxisListType

T = 4096
DM = 1024
FH = 3072
MH = 2816
NE = 8
FLEN = 8448
FPAD = 4223
WINW = 4608
WINBASE = 3584
SHIFT = 16.0
QS = 0.125
OWN = 2048
EPS = 1e-5
NKC = 32

KCMIN = [max(0, (3457 - 512 * g + 127) // 128) for g in range(8)]



def _bucket_np():
    d = np.arange(T)
    rp = np.maximum(d, 1).astype(np.float32)
    large = 16 + (np.log(rp / np.float32(16.0))
                  / np.float32(math.log(8.0)) * 16).astype(np.int32)
    large = np.minimum(large, 31)
    return np.where(d < 16, d, large)


def stage_core(inputs, core):
    b, h = core // 2, core % 2
    x = np.ascontiguousarray(np.asarray(inputs["L0_attn_x"][b]).T.astype(np.float32, copy=False))
    d = {}
    d["x_fm"] = x
    d["x_own"] = np.ascontiguousarray(x[:, 2048 * h:2048 * h + 2048])

    Wa = np.asarray(inputs["L0_attn_W_attn"])
    ba = np.asarray(inputs["L0_attn_b_attn"])
    l0w = np.empty((4, 1024, 384), np.float32)
    l0b = np.empty((4, 128, 3), np.float32)
    for p in range(4):
        P = 4 * h + p
        l0w[p, :, 0:128] = Wa[:, 128 * P:128 * P + 128]
        l0w[p, :, 128:256] = Wa[:, 1024 + 128 * P:1024 + 128 * P + 128]
        l0w[p, :, 256:384] = Wa[:, 2048 + 128 * P:2048 + 128 * P + 128]
        l0b[p, :, 0] = ba[128 * P:128 * P + 128]
        l0b[p, :, 1] = ba[1024 + 128 * P:1024 + 128 * P + 128]
        l0b[p, :, 2] = ba[2048 + 128 * P:2048 + 128 * P + 128]
    d["l0w"] = l0w
    d["l0b"] = l0b
    d["l0proj"] = np.ascontiguousarray(
        np.asarray(inputs["L0_attn_W_proj"])[512 * h:512 * h + 512, :])
    pb = np.asarray(inputs["L0_attn_b_proj"]).reshape(8, 128).T
    d["l0pb"] = np.ascontiguousarray(pb) if h == 0 else np.zeros((128, 8), np.float32)

    d["table_own"] = np.ascontiguousarray(
        np.asarray(inputs["L0_attn_bias_table"])[:, 8 * h:8 * h + 8])
    bk = _bucket_np()
    oh = np.zeros((32, T), np.float32)
    oh[bk, np.arange(T)] = 1.0
    d["oh0"] = oh

    d["wg_t"] = np.ascontiguousarray(np.asarray(inputs["L0_ffn_w_gate"]).T)
    d["wu_t"] = np.ascontiguousarray(np.asarray(inputs["L0_ffn_w_up"]).T)
    d["wd_t"] = np.ascontiguousarray(np.asarray(inputs["L0_ffn_w_down"]).T)

    l1w = np.empty((4, 1024, 384), np.float32)
    for p in range(4):
        P = 4 * h + p
        l1w[p, :, 0:128] = np.asarray(inputs["L1_attn_Wq"])[128 * P:128 * P + 128, :].T
        l1w[p, :, 128:256] = np.asarray(inputs["L1_attn_Wk"])[128 * P:128 * P + 128, :].T
        l1w[p, :, 256:384] = np.asarray(inputs["L1_attn_Wv"])[128 * P:128 * P + 128, :].T
    d["l1w"] = l1w
    d["l1proj"] = np.ascontiguousarray(
        np.asarray(inputs["L1_attn_Wo"]).T[512 * h:512 * h + 512, :])
    d["l1bias_own"] = np.ascontiguousarray(
        np.asarray(inputs["L1_attn_bias"])[8 * h:8 * h + 8, :])
    d["tau_own"] = np.ascontiguousarray(
        np.asarray(inputs["L1_attn_tau"])[8 * h:8 * h + 8].reshape(1, 8))
    d["riot"] = (1.0 / np.arange(1, T + 1, dtype=np.float64)).astype(np.float32).reshape(1, T)

    Wg = np.asarray(inputs["L1_ffn_Wg"])
    d["wg8"] = np.ascontiguousarray(
        Wg.T.reshape(8, 128, 8).transpose(1, 0, 2).reshape(128, 64))
    d["w1t"] = np.ascontiguousarray(np.transpose(np.asarray(inputs["L1_ffn_W1"]), (0, 2, 1)))
    d["w3t"] = np.ascontiguousarray(np.transpose(np.asarray(inputs["L1_ffn_W3"]), (0, 2, 1)))
    d["w2t"] = np.ascontiguousarray(np.transpose(np.asarray(inputs["L1_ffn_W2"]), (0, 2, 1)))
    d["ident"] = np.eye(128, dtype=np.float32)
    return d



class PB:
    ENGS = ("t", "v", "s", "g", "cc")
    NSLOT = 12

    def __init__(self, nc):
        self.nc = nc
        self.sched = []
        self.cnt = {}
        self.hdl = {}
        self.lastwait = {}
        self.dmaq = {}

    def sem(self, ctx, name):
        self.hdl[name] = ctx.enter_context(self.nc.semaphore(name))
        self.cnt[name] = 0

    def dsem(self, ctx, q):
        self.dmaq[q] = 0
        for i in range(self.NSLOT):
            n = f"{q}{i}"
            self.hdl[n] = ctx.enter_context(self.nc.semaphore(n))
            self.cnt[n] = 0

    def op(self, eng, build_fn, waits=(), incs=()):
        if eng in ("v", "s", "g"):
            waits = list(waits) + [(eng, self.cnt[eng])]
        w = []
        for sname, v in waits:
            assert v <= self.cnt[sname], (sname, v, self.cnt[sname])
            if v <= 0:
                continue
            key = (eng, sname)
            if self.lastwait.get(key, -1) >= v:
                continue
            self.lastwait[key] = v
            w.append((self.hdl[sname], v))
        iz = []
        for sname, amt in incs:
            self.cnt[sname] += amt
            iz.append((self.hdl[sname], amt))

        def fn(e, w=w, iz=iz, build_fn=build_fn):
            for hh, v in w:
                e.wait_ge(hh, v)
            ins = build_fn(e)
            for hh, amt in iz:
                ins.then_inc(hh, amt)
        self.sched.append((eng, fn))

    def bar(self, *exclude):
        ex = set(exclude)
        w = [(n, self.cnt[n]) for n in self.ENGS if n not in ex and self.cnt[n] > 0]
        for q in self.dmaq:
            if q in ex:
                continue
            for i in range(self.NSLOT):
                n = f"{q}{i}"
                if self.cnt[n] > 0:
                    w.append((n, self.cnt[n]))
        return w

    def dma(self, q, out, in_, waits=()):
        j = self.dmaq[q]
        self.dmaq[q] = j + 1
        n = f"{q}{j % self.NSLOT}"
        w = list(waits)
        if self.cnt[n] > 0:
            w.append((n, self.cnt[n]))
        self.op(q, lambda e: e.dma_start(out=out, in_=in_), w, ((n, 16),))
        return (n, self.cnt[n])

    def mm(self, out, lhsT, rhs, start, stop, tp=None, waits=(), fin=False):
        incs = (("t", 1),) if fin else ()
        self.op("t", lambda e: e.matmul(out, lhsT, rhs, start=start, stop=stop,
                                        tile_position=tp), waits, incs)
        return ("t", self.cnt["t"])

    def tr(self, out, in_, ident, waits=()):
        self.op("t", lambda e: e.transpose(out, in_, ident), waits, (("t", 1),))
        return ("t", self.cnt["t"])

    def act(self, out, in_, func, bias=0.0, scale=1.0, waits=()):
        self.op("s", lambda e: e.activation(out, in_, func, bias=bias, scale=scale),
                waits, (("s", 1),))
        return ("s", self.cnt["s"])

    def scopy(self, out, in_, waits=()):
        self.op("s", lambda e: e.copy(out, in_), waits, (("s", 1),))
        return ("s", self.cnt["s"])

    def tt(self, out, in0, in1, alu, waits=()):
        self.op("v", lambda e: e.tensor_tensor(out, in0, in1, alu), waits, (("v", 1),))
        return ("v", self.cnt["v"])

    def ts(self, out, in0, s1, s2, op0, op1=None, waits=()):
        if op1 is None:
            self.op("v", lambda e: e.tensor_scalar(out, in0, s1, s2, op0),
                    waits, (("v", 1),))
        else:
            self.op("v", lambda e: e.tensor_scalar(out, in0, s1, s2, op0, op1),
                    waits, (("v", 1),))
        return ("v", self.cnt["v"])

    def stt(self, out, in0, scalar, in1, op0, op1, waits=()):
        self.op("v", lambda e: e.scalar_tensor_tensor(out, in0, scalar, in1, op0, op1),
                waits, (("v", 1),))
        return ("v", self.cnt["v"])

    def vred(self, out, in_, alu, negate=False, waits=()):
        self.op("v", lambda e: e.tensor_reduce(out, in_, AX.X, alu, negate=negate),
                waits, (("v", 1),))
        return ("v", self.cnt["v"])

    def vcopy(self, out, in_, waits=()):
        self.op("v", lambda e: e.tensor_copy(out, in_), waits, (("v", 1),))
        return ("v", self.cnt["v"])

    def vrecip(self, out, in_, waits=()):
        self.op("v", lambda e: e.reciprocal(out, in_), waits, (("v", 1),))
        return ("v", self.cnt["v"])

    def recip_pos(self, out, in_, tmp, waits=()):
        self.act_ln = None
        self.op("s", lambda e: e.activation(tmp, in_, AF.Ln), waits, (("s", 1),))
        self.op("s", lambda e: e.activation(out, tmp, AF.Exp, scale=-1.0), (),
                (("s", 1),))
        return ("s", self.cnt["s"])

    def vmemset(self, ap, val, waits=()):
        self.op("v", lambda e: e.memset(ap, val), waits, (("v", 1),))
        return ("v", self.cnt["v"])



def build_program(mode, race_detect=False, skip=()):
    nc = bass.Bass("TRN2", target_bir_lowering=False, num_devices=8,
                   detect_race_conditions=race_detect)
    L0 = mode in ("L0", "FUSED")
    L1 = mode in ("L1", "FUSED")

    input_names = []

    def din(name, shape, dt):
        input_names.append(name)
        return nc.dram_tensor(name, shape, dt, kind="ExternalInput")

    ident_d = din("ident", [128, 128], F32)
    riot_d = din("riot", [1, T], F32)
    if L0:
        x_fm = din("x_fm", [DM, T], F32)
        x_own = din("x_own", [DM, OWN], F32)
        l0w = din("l0w", [4, DM, 384], F32)
        l0b = din("l0b", [4, 128, 3], F32)
        l0proj = din("l0proj", [512, DM], F32)
        l0pb = din("l0pb", [128, 8], F32)
        table_own = din("table_own", [32, 8], F32)
        oh0 = din("oh0", [32, T], F32)
        wg_t = din("wg_t", [DM, FH], I32)
        wu_t = din("wu_t", [DM, FH], I32)
        wd_t = din("wd_t", [FH, DM], I32)
    if L1:
        l1w = din("l1w", [4, DM, 384], F32)
        l1proj = din("l1proj", [512, DM], F32)
        l1bias_own = din("l1bias_own", [8, 2048], F32)
        tau_own = din("tau_own", [1, 8], F32)
        wg8 = din("wg8", [128, 64], F32)
        w1t = din("w1t", [NE, DM, MH], F32)
        w3t = din("w3t", [NE, DM, MH], F32)
        w2t = din("w2t", [NE, MH, DM], F32)
        if mode == "L1":
            x1_in = din("x1_in", [DM, OWN], F32)

    if mode == "L0":
        x1_out = nc.dram_tensor("x1_out", [DM, OWN], F32, kind="ExternalOutput")
    else:
        y_out = nc.dram_tensor("y_out", [DM, OWN], F32, kind="ExternalOutput")
    dbg_out = (nc.dram_tensor("dbg_out", [24, 2048], F32, kind="ExternalOutput")
               if "dbg" in skip else None)
    dbg2_out = (nc.dram_tensor("dbg2_out", [128, 128], F32, kind="ExternalOutput")
                if "dbg" in skip else None)

    F0d = nc.dram_tensor("F0d", [8, FLEN], BF16) if L0 else None
    F1d = nc.dram_tensor("F1d", [8, FLEN], BF16) if L1 else None
    xn0_d = nc.dram_tensor("xn0_d", [DM, T], BF16) if L0 else None
    x1_d = nc.dram_tensor("x1_d", [DM, OWN], F32)
    pp_in = nc.dram_tensor("pp_in", [2, DM, OWN], BF16)
    pp_out = nc.dram_tensor("pp_out", [DM, OWN], BF16)
    pp_in1 = nc.dram_tensor("pp_in1", [2, DM, OWN], BF16)
    pp_out1 = nc.dram_tensor("pp_out1", [DM, OWN], BF16)
    if L1:
        xn1_in = nc.dram_tensor("xn1_in", [DM, OWN], BF16)
        xn1_g = nc.dram_tensor("xn1_g", [2, DM, OWN], BF16)

    p = PB(nc)
    with ExitStack() as ctx:
        en = ctx.enter_context
        XN = en(nc.sbuf_tensor("XN", [128, 8 * 2048], BF16))
        Y = en(nc.sbuf_tensor("Y", [128, 4 * 4096], BF16))
        M1 = en(nc.sbuf_tensor("M1", [128, 24576], BF16))
        M2 = en(nc.sbuf_tensor("M2", [128, 20480], BF16))
        WINB = en(nc.sbuf_tensor("WINB", [128, WINW], BF16))
        QP = en(nc.sbuf_tensor("QP", [128, T], BF16))
        EB = en(nc.sbuf_tensor("EB", [128, 2048], BF16))
        PBUF = en(nc.sbuf_tensor("PBUF", [128, 1024], BF16))
        CWB = en(nc.sbuf_tensor("CWB", [128, 1024], BF16))
        STAT = en(nc.sbuf_tensor("STAT", [128, 2048], F32))
        SCR = en(nc.sbuf_tensor("SCR", [128, 3072], F32))
        SMB = en(nc.sbuf_tensor("SMB", [128, 2048], BF16))
        ROWS = en(nc.sbuf_tensor("ROWS", [128, 1664], BF16))
        IDT = en(nc.sbuf_tensor("IDT", [128, 128], BF16))
        BIA = en(nc.sbuf_tensor("BIA", [128, 32], F32))
        QA = en(nc.psum_tensor("QA", [128, 1024], F32))
        QB = en(nc.psum_tensor("QB", [128, 1024], F32))
        QC = en(nc.psum_tensor("QC", [128, 1024], F32))
        QD = en(nc.psum_tensor("QD", [128, 512], F32))
        PSB = en(nc.psum_tensor("PSB", [128, 1024], BF16))
        PS = [QA[0:128, 0:512], QA[0:128, 512:1024],
              QB[0:128, 0:512], QB[0:128, 512:1024],
              QC[0:128, 0:512], QC[0:128, 512:1024],
              QD[0:128, 0:512]]

        for s in PB.ENGS:
            p.sem(ctx, s)
        p.dsem(ctx, "dg")
        p.dsem(ctx, "ds")
        block = en(nc.Block())

        KR0, VR0, PTA0 = 16384, 20480, 0
        VT0, PTB0 = 16384, 0

        ONES1 = ROWS[0:1, 0:128]
        ONES128 = ROWS[0:128, 0:1]
        ONESF = BIA[0:128, 26:27]

        m_id = p.dma("dg", IDT[:, :], ident_d[:, :])
        p.vmemset(ROWS[0:128, 0:128], 1.0)
        p.vmemset(ONESF, 1.0)
        p.vmemset(BIA[0:128, 24:25], -SHIFT)
        p.vmemset(BIA[0:128, 25:26], EPS)

        def build_F(Fd, is_l0):
            zt = SMB[0:16, 256:768]
            p.vmemset(zt, 0.0)
            for i in range(9):
                w = min(512, FPAD - 512 * i)
                p.dma("ds", bass.AP(Fd, 512 * i, [[FLEN, 8], [1, w]]),
                      zt[0:8, 0:w], waits=p.bar("ds"))
            if is_l0:
                tb = SMB[0:32, 0:16]
                p.dma("dg", tb[0:32, 0:8], table_own[:, :])
                ohc = SMB[0:32, 768:1280]
                for i in range(8):
                    p.dma("dg", ohc[:, :], oh0[:, 512 * i:512 * i + 512],
                          waits=p.bar("dg"))
                    bank = PS[0]
                    mmf = p.mm(bank[0:8, :], tb[0:32, 0:8], ohc[:, :], True, True,
                               waits=p.bar("t"), fin=True)
                    ev = p.act(SMB[0:8, 1280:1792], bank[0:8, :], AF.Exp,
                               bias=BIA[0:8, 24:25], waits=p.bar("s"))
                    p.dma("ds", bass.AP(Fd, FPAD + 512 * i, [[FLEN, 8], [1, 512]]),
                          SMB[0:8, 1280:1792], waits=[ev])
                p.dma("ds", bass.AP(Fd, FPAD + 4096, [[FLEN, 8], [1, FLEN - FPAD - 4096]]),
                      zt[0:8, 0:FLEN - FPAD - 4096], waits=p.bar("ds"))
            else:
                bc = SMB[0:8, 768:1280]
                for i in range(4):
                    p.dma("ds", SCR[0:8, 2048:2560],
                          l1bias_own[:, 512 * i:512 * i + 512], waits=p.bar("ds"))
                    ev = p.act(bc, SCR[0:8, 2048:2560], AF.Exp,
                               bias=BIA[0:8, 24:25], waits=p.bar("s"))
                    p.dma("ds", bass.AP(Fd, FPAD + 512 * i, [[FLEN, 8], [1, 512]]),
                          bc, waits=[ev])
                et = SMB[0:16, 1280:1792]
                p.vmemset(et, math.exp(-SHIFT), waits=p.bar("v"))
                off = FPAD + 2048
                rem = FLEN - off
                for i in range((rem + 511) // 512):
                    w = min(512, rem - 512 * i)
                    p.dma("ds", bass.AP(Fd, off + 512 * i, [[FLEN, 8], [1, w]]),
                          et[0:8, 0:w], waits=p.bar("ds"))

        if L0:
            build_F(F0d, True)
        if L1:
            build_F(F1d, False)

        def layernorm(src_d, ntok, out_sbuf, out_dram):
            stride = int(src_d.shape[1])
            nts = ntok // 512
            for tsl in range(nts):
                c0 = 512 * tsl
                for dchunk in range(8):
                    xa = SCR[0:128, 2048:2560]
                    p.dma("ds", xa,
                          bass.AP(src_d, (128 * dchunk) * stride + c0,
                                  [[stride, 128], [1, 512]]),
                          waits=p.bar("ds"))
                    p.mm(PS[0][0:1, :], ONESF, xa, dchunk == 0, dchunk == 7,
                         waits=p.bar("t"), fin=(dchunk == 7))
                    p.act(SCR[0:128, 2560:3072], xa, AF.Square, waits=p.bar("s"))
                    p.mm(PS[1][0:1, :], ONESF, SCR[0:128, 2560:3072],
                         dchunk == 0, dchunk == 7, waits=p.bar("t"),
                         fin=True)
                sA = STAT[0:1, 0:512]
                sB = STAT[0:1, 512:1024]
                sC = STAT[0:1, 1024:1536]
                sD = STAT[0:1, 1536:2048]
                p.ts(sA, PS[0][0:1, :], 1.0 / DM, None, AL.mult, waits=p.bar("v"))
                p.ts(sB, PS[1][0:1, :], 1.0 / DM, None, AL.mult, waits=p.bar("v"))
                p.tt(sC, sA, sA, AL.mult, waits=p.bar("v"))
                p.tt(sD, sB, sC, AL.subtract, waits=p.bar("v"))
                p.act(sC, sD, AF.Ln, bias=BIA[0:1, 25:26], waits=p.bar("s"))
                p.act(sD, sC, AF.Exp, scale=-0.5, waits=p.bar("s"))
                p.tt(sC, sA, sD, AL.mult, waits=p.bar("v"))
                ab = ROWS[0:1, 128:640]
                bb = ROWS[0:1, 640:1152]
                p.vcopy(ab, sD, waits=p.bar("v"))
                p.vcopy(bb, sC, waits=p.bar("v"))
                p.mm(PS[2][0:128, :], ONES1, ab, True, True,
                     waits=p.bar("t"), fin=True)
                p.mm(PS[3][0:128, :], ONES1, bb, True, True,
                     waits=p.bar("t"), fin=True)
                p.scopy(SCR[0:128, 2048:2560], PS[2][:, :], waits=p.bar("s"))
                p.scopy(SCR[0:128, 2560:3072], PS[3][:, :], waits=p.bar("s"))
                for dchunk in range(8):
                    xb = SCR[0:128, 0:512]
                    p.dma("ds", xb,
                          bass.AP(src_d, (128 * dchunk) * stride + c0,
                                  [[stride, 128], [1, 512]]),
                          waits=p.bar("ds"))
                    t1 = STAT[0:128, 0:512]
                    p.tt(t1, xb, SCR[0:128, 2048:2560], AL.mult, waits=p.bar("v"))
                    if out_sbuf is not None:
                        dst = out_sbuf[0:128, 2048 * dchunk + c0:
                                       2048 * dchunk + c0 + 512]
                    else:
                        dst = EB[0:128, 0:512]
                    p.tt(dst, t1, SCR[0:128, 2560:3072], AL.subtract,
                         waits=p.bar("v"))
                    if out_dram is not None:
                        p.dma("ds",
                              bass.AP(out_dram, (128 * dchunk) * int(out_dram.shape[1]) + c0,
                                      [[int(out_dram.shape[1]), 128], [1, 512]]),
                              dst, waits=p.bar("ds"))

        def attention(layer):
            wsrc = l0w if layer == 0 else l1w
            Fd = F0d if layer == 0 else F1d
            if layer == 1:
                p.dma("ds", BIA[0:1, 16:24], tau_own[:, :], waits=p.bar("ds"))
            for pr in range(4):
                WPS = WINB[0:128, 0:3072]
                mws = []
                for c in range(8):
                    mws.append(p.dma("dg", WPS[:, 384 * c:384 * c + 384],
                                     bass.AP(wsrc, pr * DM * 384 + (128 * c) * 384,
                                             [[384, 128], [1, 384]]),
                                     waits=p.bar("dg")))
                if layer == 0:
                    mws.append(p.dma("dg", BIA[0:128, 0:3],
                                     bass.AP(l0b, pr * 128 * 3, [[3, 128], [1, 3]]),
                                     waits=p.bar("dg")))
                    p.ts(BIA[0:128, 4:5], BIA[0:128, 0:1], QS, None, AL.mult,
                         waits=mws)
                evtok = {}
                rbtok = {}
                mmv_tok = {}
                for tsl in range(8):
                    par = tsl % 2
                    B0 = PS[3 * par]
                    B1 = PS[3 * par + 1]
                    B2 = PS[3 * par + 2]
                    mm_q = mm_k = mm_v = None
                    for dchunk in range(8):
                        rb = EB[0:128, 512 * (dchunk % 2):512 * (dchunk % 2) + 512]
                        if layer == 0:
                            srcp = bass.AP(xn0_d, (128 * dchunk) * T + 512 * tsl,
                                           [[T, 128], [1, 512]])
                        else:
                            gg = tsl // 4
                            lc = 512 * (tsl % 4)
                            srcp = bass.AP(xn1_g, gg * DM * OWN + (128 * dchunk) * OWN + lc,
                                           [[OWN, 128], [1, 512]])
                        key = (tsl * 8 + dchunk) % 2
                        prev = mmv_tok.get((tsl * 8 + dchunk) - 2)
                        mrb = p.dma("dg", rb, srcp,
                                    waits=([x for x in (prev,) if x]
                                           + (p.bar("dg") if tsl == 0 and dchunk < 2
                                              else [])))
                        st_, sp = dchunk == 0, dchunk == 7
                        wl = [mrb]
                        if st_:
                            wl += mws
                            for role in range(3):
                                tok = evtok.get((par, role))
                                if tok:
                                    wl.append(tok)
                        mm_q = p.mm(B0[:, :], WPS[:, 384 * dchunk:384 * dchunk + 128],
                                    rb, st_, sp, waits=wl, fin=sp)
                        mm_k = p.mm(B1[:, :], WPS[:, 384 * dchunk + 128:384 * dchunk + 256],
                                    rb, st_, sp, waits=[], fin=sp)
                        mm_v = p.mm(B2[:, :], WPS[:, 384 * dchunk + 256:384 * dchunk + 384],
                                    rb, st_, sp, waits=[], fin=True)
                        mmv_tok[tsl * 8 + dchunk] = ("t", p.cnt["t"])
                    if layer == 0:
                        evtok[(par, 0)] = p.act(QP[:, 512 * tsl:512 * tsl + 512],
                                                B0[:, :], AF.Identity,
                                                bias=BIA[0:128, 4:5], scale=QS,
                                                waits=[mm_q])
                        evtok[(par, 1)] = p.ts(
                            bass.AP(M1, KR0 + (T - 1 - 512 * tsl),
                                    [[24576, 128], [-1, 512]]),
                            B1[:, :], BIA[0:128, 1:2], None, AL.add, waits=[mm_k])
                        evtok[(par, 2)] = p.ts(
                            bass.AP(M1, VR0 + (T - 1 - 512 * tsl),
                                    [[24576, 128], [-1, 512]]),
                            B2[:, :], BIA[0:128, 2:3], None, AL.add, waits=[mm_v])
                    else:
                        evtok[(par, 0)] = p.act(QP[:, 512 * tsl:512 * tsl + 512],
                                                B0[:, :], AF.Identity, scale=QS,
                                                waits=[mm_q])
                        evtok[(par, 1)] = p.vcopy(
                            bass.AP(M1, KR0 + (T - 1 - 512 * tsl),
                                    [[24576, 128], [-1, 512]]),
                            B1[:, :], waits=[mm_k])
                        evtok[(par, 2)] = p.vcopy(
                            bass.AP(M1, VR0 + (T - 1 - 512 * tsl),
                                    [[24576, 128], [-1, 512]]),
                            B2[:, :], waits=[mm_v])
                kvdone = [evtok[(q, r)] for q in range(2) for r in range(3)]
                trtok = {}
                cptok = {}
                for kc in range(NKC):
                    pdst = (PSB[0:128, 0:128] if kc % 2 == 0
                            else QD.ap().bitcast(BF16)[0:128, 0:128])
                    wl = [x for x in (cptok.get(kc - 2),) if x]
                    if kc <= 1:
                        wl = wl + kvdone
                    trtok[kc] = p.tr(pdst, M1[0:128, VR0 + 128 * kc:VR0 + 128 * kc + 128],
                                     IDT[:, :], waits=wl)
                    cptok[kc] = p.vcopy(M2[0:128, VT0 + 128 * kc:VT0 + 128 * kc + 128],
                                        pdst, waits=[trtok[kc]])
                vt_done = [cptok[NKC - 1], cptok[NKC - 2]]
                extok = {}
                multok = {}
                sgb_free = None
                pvb_free = None
                er_prev = None
                for hh in range(2):
                    head = 2 * pr + hh
                    rb0 = 64 * hh
                    tp_row = (rb0, 0)
                    tp_col = None if hh == 0 else (0, 64)
                    ob0 = 64 * hh
                    mwin = p.dma("ds", WINB[:, :],
                                 bass.AP(Fd, head * FLEN + WINBASE,
                                         [[1, 128], [1, WINW]]),
                                 waits=list(multok.values()) + kvdone)
                    sqbanks = [QA, QB]
                    npar = 2
                    for g in range(8):
                        q0 = 512 * g
                        kmin = KCMIN[g]
                        nk = NKC - kmin
                        npair = nk // 2
                        PVb = QC[0:128, 0:512]
                        SGb = QD[0:128, 0:512]
                        slab = M1 if hh == 0 else M2
                        slab0 = PTA0 if hh == 0 else PTB0
                        for j in range(npair):
                            kc = kmin + 2 * j
                            par = j % npar
                            sq = sqbanks[par]
                            rhs = QP[rb0:rb0 + 64, q0:q0 + 512]
                            wl = [mwin] + [x for x in (extok.get(par),) if x]
                            if g == 0 and j < npar and hh == 0:
                                wl += kvdone + vt_done
                            p.mm(sq[0:128, 0:512],
                                 M1[rb0:rb0 + 64, KR0 + 128 * kc:KR0 + 128 * kc + 128],
                                 rhs, True, True, tp=tp_row, waits=wl, fin=True)
                            mmS = p.mm(sq[0:128, 512:1024],
                                       M1[rb0:rb0 + 64,
                                          KR0 + 128 * kc + 128:KR0 + 128 * kc + 256],
                                       rhs, True, True, tp=tp_row, waits=[], fin=True)
                            ebuf = EB[0:128, 1024 * par:1024 * par + 1024]
                            ex = p.act(ebuf, sq[0:128, 0:1024], AF.Exp,
                                       waits=[mmS] + [x for x in (multok.get(par),) if x])
                            extok[par] = ex
                            u0 = q0 + 128 * kc - 3456
                            if layer == 0:
                                pt = slab[0:128, slab0 + 1024 * par:
                                          slab0 + 1024 * par + 1024]
                            else:
                                pt = slab[0:128, slab0 + 1024 * j:slab0 + 1024 * j + 1024]
                            win2 = bass.AP(WINB, u0, [[WINW, 128], [128, 2], [1, 512]])
                            mul = p.tt(pt, ebuf, win2, AL.mult, waits=[ex])
                            multok[par] = mul
                            p.mm(SGb[0:1, :], ONES128, pt[0:128, 0:512],
                                 j == 0, False,
                                 waits=[mul] + ([x for x in (sgb_free,) if x]
                                                if j == 0 else []),
                                 fin=False)
                            p.mm(SGb[0:1, :], ONES128, pt[0:128, 512:1024],
                                 False, j == npair - 1, waits=[],
                                 fin=(j == npair - 1))
                            if layer == 0:
                                for u in range(2):
                                    vts = M2[0:128, VT0 + 128 * (kc + u) + 64 * hh:
                                             VT0 + 128 * (kc + u) + 64 * hh + 64]
                                    p.mm(PVb[ob0:ob0 + 64, 0:512], vts,
                                         pt[0:128, 512 * u:512 * u + 512],
                                         j == 0 and u == 0, j == npair - 1 and u == 1,
                                         tp=tp_col,
                                         waits=([x for x in (pvb_free,) if x]
                                                if j == 0 and u == 0 else []),
                                         fin=(j == npair - 1 and u == 1))
                        sg_done = ("t", p.cnt["t"])
                        if layer == 1:
                            rio = STAT[0:1, 0:512]
                            mrio = p.dma("ds", rio, riot_d[0:1, q0:q0 + 512],
                                         waits=p.bar("ds"))
                            cr0 = STAT[0:1, 512:1024]
                            t1_ = p.tt(cr0, SGb[0:1, :], rio, AL.mult,
                                       waits=[mrio, sg_done])
                            cr = STAT[0:1, 1024:1536]
                            t2_ = p.ts(cr, cr0, BIA[0:1, 16 + head:17 + head], None,
                                       AL.mult, waits=[t1_])
                            crb = ROWS[0:1, 128:640]
                            c1_ = p.vcopy(crb, cr, waits=[t2_])
                            PCh = QC[0:128, 512:1024]
                            mmB = p.mm(PCh, ONES1, crb, True, True,
                                       waits=[c1_] + [x for x in (er_prev,) if x],
                                       fin=True)
                            tb2 = CWB[0:128, 0:1024]
                            e1_ = p.scopy(CWB[0:128, 0:512], PCh, waits=[mmB])
                            etb = p.scopy(CWB[0:128, 512:1024], PCh, waits=[e1_])
                            for j in range(npair):
                                kc = kmin + 2 * j
                                pt = slab[0:128, slab0 + 1024 * j:slab0 + 1024 * j + 1024]
                                pb1 = PBUF[0:128, 0:1024]
                                pbm = EB[0:128, 0:1024]
                                ad = p.tt(pb1, pt, tb2, AL.add, waits=[etb])
                                mx = p.ts(pbm, pb1, 0.0, None, AL.max, waits=[ad])
                                for u in range(2):
                                    vts = M2[0:128, VT0 + 128 * (kc + u) + 64 * hh:
                                             VT0 + 128 * (kc + u) + 64 * hh + 64]
                                    p.mm(PVb[ob0:ob0 + 64, 0:512], vts,
                                         pbm[0:128, 512 * u:512 * u + 512],
                                         j == 0 and u == 0, j == npair - 1 and u == 1,
                                         tp=tp_col,
                                         waits=([mx] if u == 0 else [])
                                         + ([x for x in (pvb_free,) if x]
                                            if j == 0 and u == 0 else []),
                                         fin=(j == npair - 1 and u == 1))
                        pv_done = ("t", p.cnt["t"])
                        rc = STAT[0:1, 1536:2048]
                        rctok = p.recip_pos(rc, SGb[0:1, :], STAT[0:1, 0:512]
                                            if layer == 0 else STAT[0:1, 512:1024],
                                            waits=[sg_done])
                        sgb_free = rctok
                        rcb = ROWS[0:1, 640:1152]
                        c2_ = p.vcopy(rcb, rc, waits=[rctok])
                        mmR = p.mm(PS[5][ob0:ob0 + 64, :], ONES1[0:1, 0:64], rcb,
                                   True, True, tp=tp_col,
                                   waits=[c2_, ("s", p.cnt["s"])]
                                   + [x for x in (er_prev,) if x],
                                   fin=True)
                        rbuf = CWB[0:128, 512:1024]
                        er = p.scopy(rbuf[ob0:ob0 + 64, 0:512], PS[5][ob0:ob0 + 64, :],
                                     waits=[mmR])
                        er_prev = er
                        ycol = 4096 * pr + 512 * g
                        pvb_free = p.tt(Y[rb0:rb0 + 64, ycol:ycol + 512],
                                        PVb[ob0:ob0 + 64, :],
                                        rbuf[ob0:ob0 + 64, 0:512], AL.mult,
                                        waits=[er, pv_done])

        def proj_partial(layer):
            wsrc = l0proj if layer == 0 else l1proj
            dst = pp_in if layer == 0 else pp_in1
            cout = pp_out if layer == 0 else pp_out1
            Wp = M1[0:128, 0:4096]
            mload = []
            for c in range(4):
                mload.append(p.dma("dg", Wp[:, 1024 * c:1024 * c + 1024],
                                   bass.AP(wsrc, (128 * c) * DM, [[DM, 128], [1, 1024]]),
                                   waits=p.bar("dg")))
            if layer == 0:
                mload.append(p.dma("dg", BIA[0:128, 8:16], l0pb[:, :],
                                   waits=p.bar("dg")))
            yready = [("v", p.cnt["v"]), ("s", p.cnt["s"])]
            bfreeP = {0: None, 1: None, 2: None, 3: None}
            obtok = {}
            first = True
            for od in range(8):
                for tsl in range(8):
                    bi = tsl % 4
                    bank = PS[bi]
                    for c in range(4):
                        wl = []
                        if c == 0:
                            wl = list(mload) + [x for x in (bfreeP[bi],) if x]
                            if first:
                                wl += yready
                        p.mm(bank[:, :], Wp[:, 1024 * c + 128 * od:
                                            1024 * c + 128 * od + 128],
                             Y[:, 4096 * c + 512 * tsl:4096 * c + 512 * tsl + 512],
                             c == 0, c == 3, waits=wl, fin=(c == 3))
                    first = False
                    mmf = ("t", p.cnt["t"])
                    oi = tsl % 2
                    ob = EB[0:128, 512 * oi:512 * oi + 512]
                    gg2 = tsl // 4
                    lc = 512 * (tsl % 4)
                    wl_ev = [mmf] + [x for x in (obtok.get(oi),) if x]
                    if layer == 0:
                        ev = p.act(ob, bank[:, :], AF.Identity,
                                   bias=BIA[0:128, 8 + od:9 + od], waits=wl_ev)
                    else:
                        ev = p.scopy(ob, bank[:, :], waits=wl_ev)
                    bfreeP[bi] = ev
                    obtok[oi] = p.dma("ds",
                                      bass.AP(dst, gg2 * DM * OWN + (128 * od) * OWN + lc,
                                              [[OWN, 128], [1, 512]]),
                                      ob, waits=[ev])
            p.op("g", lambda e: e.collective_compute(
                "ReduceScatter", AL.add,
                replica_groups=[[0, 1], [2, 3], [4, 5], [6, 7]],
                ins=[dst.ap().opt()], outs=[cout.ap().opt()]),
                waits=p.bar("cc"), incs=(("cc", 1),))
            return ("cc", p.cnt["cc"])

        def gate_up_h(w_gate_d, w_up_d, nhh, tcols, tw, scale_g, scale_u,
                      e_idx=None):
            cap = 20480 // tw

            def slot(i):
                if i < 0:
                    return WINB[0:128, 2 * tw:3 * tw]
                if i < cap:
                    return M2[0:128, tw * i:tw * i + tw]
                return WINB[0:128, tw * (i - cap):tw * (i - cap) + tw]

            NC_ = nhh * 128
            wstride = int(w_gate_d.shape[-1])

            def wslab(src_d):
                toks = []
                for c in range(8):
                    base = (e_idx * DM * MH if e_idx is not None else 0)
                    toks.append(p.dma("dg", M1[0:128, NC_ * c:NC_ * (c + 1)],
                                      bass.AP(src_d, base + (128 * c) * wstride,
                                              [[wstride, 128], [1, NC_]]),
                                      waits=p.bar("dg")))
                return toks

            nts = tw // 512
            nbank = 4
            bfree = {b: None for b in range(nbank)}
            mw = wslab(w_gate_d)
            for hh2 in range(nhh):
                for tsub in range(nts):
                    bi = (hh2 * nts + tsub) % nbank
                    bank = PS[bi]
                    for c in range(8):
                        wl = []
                        if c == 0:
                            wl = list(mw)
                            if bfree[bi] is not None:
                                wl.append(bfree[bi])
                        p.mm(bank[:, :],
                             M1[0:128, NC_ * c + 128 * hh2:NC_ * c + 128 * hh2 + 128],
                             XN[0:128, 2048 * c + tcols + 512 * tsub:
                                2048 * c + tcols + 512 * tsub + 512],
                             c == 0, c == 7, waits=wl, fin=(c == 7))
                    mmt = ("t", p.cnt["t"])
                    bfree[bi] = p.act(slot(hh2)[:, 512 * tsub:512 * tsub + 512],
                                      bank[:, :], AF.Silu, scale=scale_g,
                                      waits=[mmt])
            gate_done = list(bfree.values())
            bfree = {b: None for b in range(nbank)}
            mw = wslab(w_up_d)
            htok = {}
            uetok = {}
            for hh2 in range(nhh):
                for tsub in range(nts):
                    bi = (hh2 * nts + tsub) % nbank
                    bank = PS[bi]
                    for c in range(8):
                        wl = []
                        if c == 0:
                            wl = list(mw) + [x for x in (bfree[bi],) if x]
                            if hh2 == 0:
                                wl += [x for x in gate_done if x]
                        p.mm(bank[:, :],
                             M1[0:128, NC_ * c + 128 * hh2:NC_ * c + 128 * hh2 + 128],
                             XN[0:128, 2048 * c + tcols + 512 * tsub:
                                2048 * c + tcols + 512 * tsub + 512],
                             c == 0, c == 7, waits=wl, fin=(c == 7))
                    mmt = ("t", p.cnt["t"])
                    ui = (hh2 * nts + tsub) % 2
                    ue = EB[0:128, 512 * ui:512 * ui + 512]
                    ev = p.act(ue, bank[:, :], AF.Copy, scale=scale_u,
                               waits=[mmt] + [x for x in (uetok.get(ui),) if x])
                    bfree[bi] = ev
                    tk = p.tt(slot(hh2 - 1)[:, 512 * tsub:512 * tsub + 512],
                              ue, slot(hh2)[:, 512 * tsub:512 * tsub + 512],
                              AL.mult, waits=[ev])
                    htok[(hh2, tsub)] = tk
                    uetok[ui] = tk
            return slot, htok

        if L0:
            layernorm(x_fm, T, None, xn0_d)
            layernorm(x_own, OWN, XN, None)
            attention(0)
            cc_l0 = proj_partial(0)
            sG = 1.0 / 127.0
            sU = sG * sG
            for tb in range(4):
                slot, htok = gate_up_h(wg_t, wu_t, 24, 512 * tb, 512, sG, sU)
                msd = []
                for c in range(24):
                    msd.append(p.dma("dg", M1[0:128, 1024 * c:1024 * c + 1024],
                                     bass.AP(wd_t, (128 * c) * DM, [[DM, 128], [1, 1024]]),
                                     waits=p.bar("dg")))
                hall = [t for t in htok.values()]
                bfree2 = {0: None, 1: None, 2: None, 3: None}
                xbtok, rbtok, t1tok, t2tok = {}, {}, {}, {}
                for od in range(8):
                    bi = od % 4
                    bank = PS[bi]
                    for c in range(24):
                        wl = []
                        if c == 0:
                            wl = list(msd) + [x for x in (bfree2[bi],) if x]
                            if od == 0:
                                wl += hall
                        p.mm(bank[:, :],
                             M1[0:128, 1024 * c + 128 * od:1024 * c + 128 * od + 128],
                             slot(c - 1)[:, 0:512], c == 0, c == 23,
                             waits=wl, fin=(c == 23))
                    mmf = ("t", p.cnt["t"])
                    pi = od % 2
                    xb = SCR[0:128, 1024 + 512 * pi:1536 + 512 * pi]
                    mx_ = p.dma("ds", xb,
                                bass.AP(x_own, (128 * od) * OWN + 512 * tb,
                                        [[OWN, 128], [1, 512]]),
                                waits=[x for x in (xbtok.get(pi),) if x])
                    rb = EB[0:128, 1024 + 512 * pi:1536 + 512 * pi]
                    mr = p.dma("ds", rb,
                               bass.AP(pp_out, (128 * od) * OWN + 512 * tb,
                                       [[OWN, 128], [1, 512]]),
                               waits=[cc_l0] + [x for x in (rbtok.get(pi),) if x])
                    t1 = SCR[0:128, 512 * pi:512 * pi + 512]
                    tk1 = p.tt(t1, bank[:, :], rb, AL.add,
                               waits=[mmf, mr] + [x for x in (t1tok.get(pi),) if x])
                    bfree2[bi] = tk1
                    rbtok[pi] = tk1
                    t2 = SCR[0:128, 2048 + 512 * pi:2560 + 512 * pi]
                    tk2 = p.tt(t2, t1, xb, AL.add,
                               waits=[mx_] + [x for x in (t2tok.get(pi),) if x])
                    xbtok[pi] = tk2
                    t1tok[pi] = tk2
                    dd = x1_out if mode == "L0" else x1_d
                    t2tok[pi] = p.dma("ds", bass.AP(dd, (128 * od) * OWN + 512 * tb,
                                                    [[OWN, 128], [1, 512]]),
                                      t2, waits=[tk2])

        if L1:
            x1src = x1_in if mode == "L1" else x1_d
            layernorm(x1src, OWN, XN, xn1_in)
            p.op("g", lambda e: e.collective_compute(
                "AllGather", AL.bypass,
                replica_groups=[[0, 1], [2, 3], [4, 5], [6, 7]],
                ins=[xn1_in.ap().opt()], outs=[xn1_g.ap().opt()]),
                waits=p.bar("cc"), incs=(("cc", 1),))
            if "l1attn" not in skip:
                attention(1)
            cc_l1 = proj_partial(1)
            mwg = p.dma("dg", ROWS[0:128, 1152:1216], wg8[:, :], waits=p.bar("dg"))
            for tsl in range(4):
                for c in range(8):
                    p.mm(PS[tsl % 4][0:8, :], ROWS[0:128, 1152 + 8 * c:1160 + 8 * c],
                         XN[0:128, 2048 * c + 512 * tsl:2048 * c + 512 * tsl + 512],
                         c == 0, c == 7,
                         waits=([mwg, ("s", p.cnt["s"]), ("v", p.cnt["v"])]
                                if c == 0 else []),
                         fin=(c == 7))
                p.scopy(SCR[0:8, 512 * tsl:512 * tsl + 512], PS[tsl % 4][0:8, :],
                        waits=p.bar("s"))
            lg_done = ("s", p.cnt["s"])
            for tc in range(16):
                lgb = ROWS[0:8, 1280:1408]
                p.vcopy(lgb, SCR[0:8, 128 * tc:128 * tc + 128],
                        waits=[lg_done] + p.bar("v"))
                p.tr(PSB[0:128, 0:8], lgb, IDT[0:8, 0:8],
                     waits=p.bar("t"))
                ltm = ROWS[0:128, 1216:1224]
                p.vcopy(ltm, PSB[0:128, 0:8], waits=p.bar("v"))
                TRE = ROWS[0:128, 1408:1424]
                TREF = SCR[0:128, 2048:2072]

                def tree8(dst_col, tile, alu):
                    p.tt(TRE[0:128, 0:4], tile[0:128, 0:4], tile[0:128, 4:8], alu,
                         waits=p.bar("v"))
                    p.tt(TRE[0:128, 4:6], TRE[0:128, 0:2], TRE[0:128, 2:4], alu,
                         waits=p.bar("v"))
                    p.tt(dst_col, TRE[0:128, 4:5], TRE[0:128, 5:6], alu,
                         waits=p.bar("v"))

                m0 = TREF[0:128, 0:1]
                tree8(m0, ltm, AL.max)
                nm = TREF[0:128, 1:2]
                p.ts(nm, m0, -1.0, None, AL.mult, waits=p.bar("v"))
                e8 = ROWS[0:128, 1224:1232]
                p.act(e8, ltm, AF.Exp, bias=nm, waits=p.bar("s"))
                s8 = TREF[0:128, 2:3]
                tree8(s8, e8, AL.add)
                ri = TREF[0:128, 3:4]
                p.recip_pos(ri, s8, TREF[0:128, 4:5], waits=p.bar("s"))
                p8 = ROWS[0:128, 1232:1240]
                p.ts(p8, e8, ri, None, AL.mult, waits=p.bar("v"))
                m1 = TREF[0:128, 5:6]
                tree8(m1, p8, AL.max)
                eq = TREF[0:128, 6:14]
                p.ts(eq, p8, m1, None, AL.is_equal, waits=p.bar("v"))
                pm = ROWS[0:128, 1248:1256]
                p.stt(pm, eq, -2.0, p8, AL.mult, AL.add, waits=p.bar("v"))
                m2 = TREF[0:128, 14:15]
                tree8(m2, pm, AL.max)
                keep = ROWS[0:128, 1256:1264]
                p.ts(keep, p8, m2, None, AL.is_ge, waits=p.bar("v"))
                cwu = ROWS[0:128, 1264:1272]
                p.tt(cwu, p8, keep, AL.mult, waits=p.bar("v"))
                s2 = TREF[0:128, 15:16]
                tree8(s2, cwu, AL.add)
                ri2 = TREF[0:128, 16:17]
                p.recip_pos(ri2, s2, TREF[0:128, 4:5], waits=p.bar("s"))
                cwt = ROWS[0:128, 1272:1280]
                p.ts(cwt, cwu, ri2, None, AL.mult, waits=p.bar("v"))
                if dbg_out is not None and tc == 0:
                    for k, tile in enumerate([ltm, e8, p8, pm, keep, cwu, cwt]):
                        p.vcopy(PBUF[0:128, 8 * k:8 * k + 8], tile, waits=p.bar("v"))
                    p.vcopy(PBUF[0:128, 56:64], eq[0:128, 0:8], waits=p.bar("v"))
                    for k in range(8):
                        p.vcopy(PBUF[0:128, 64 + k:65 + k],
                                TREF[0:128, k:k + 1], waits=p.bar("v"))
                p.tr(PSB[0:8, 128:256], cwt, IDT[:, :], waits=p.bar("t"))
                p.vcopy(SMB[0:8, 128 * tc:128 * tc + 128], PSB[0:8, 128:256],
                        waits=p.bar("v"))
            cw_done = ("v", p.cnt["v"])
            for tb in range(2):
                for e_i in ([] if "moe" in skip else range(NE)):
                    slot, htok = gate_up_h(w1t, w3t, 22, 1024 * tb, 1024,
                                           1.0, 1.0, e_idx=e_i)
                    msd = []
                    for c in range(22):
                        msd.append(p.dma("dg", M1[0:128, 1024 * c:1024 * c + 1024],
                                         bass.AP(w2t, e_i * MH * DM + (128 * c) * DM,
                                                 [[DM, 128], [1, 1024]]),
                                         waits=p.bar("dg")))
                    if "cw" in skip:
                        p.vmemset(EB[0:128, 1024:1536], 0.125, waits=p.bar("v"))
                        p.vmemset(EB[0:128, 1536:2048], 0.125, waits=p.bar("v"))
                        ecw = ("v", p.cnt["v"])
                    else:
                        mcw = p.dma("dg", ROWS[0:1, 128:1152],
                                    bass.AP(SMB, e_i * 2048 + 1024 * tb,
                                            [[2048, 1], [1, 1024]]),
                                    waits=[cw_done] + p.bar("dg"))
                        p.mm(PS[2][:, :], ONES1, ROWS[0:1, 128:640], True, True,
                             waits=[mcw, ("s", p.cnt["s"])], fin=True)
                        p.mm(PS[3][:, :], ONES1, ROWS[0:1, 640:1152], True, True,
                             waits=[mcw], fin=True)
                        p.scopy(EB[0:128, 1024:1536], PS[2][:, :], waits=p.bar("s"))
                        p.scopy(EB[0:128, 1536:2048], PS[3][:, :], waits=p.bar("s"))
                        ecw = ("s", p.cnt["s"])
                    hall = list(htok.values())
                    bfree3 = {0: None, 1: None, 2: None, 3: None}
                    first = True
                    for od in range(8):
                        for tsub in range(2):
                            bi = (od * 2 + tsub) % 4
                            bank = PS[bi]
                            for c in range(22):
                                wl = []
                                if c == 0:
                                    wl = list(msd) + [x for x in (bfree3[bi],) if x]
                                    if first:
                                        wl += hall
                                p.mm(bank[:, :],
                                     M1[0:128, 1024 * c + 128 * od:
                                        1024 * c + 128 * od + 128],
                                     slot(c - 1)[:, 512 * tsub:512 * tsub + 512],
                                     c == 0, c == 21, waits=wl, fin=(c == 21))
                            first = False
                            mmf = ("t", p.cnt["t"])
                            cwb = (EB[0:128, 1024:1536] if tsub == 0
                                   else EB[0:128, 1536:2048])
                            ycol = 8192 * (e_i % 2) + 1024 * od + 512 * tsub
                            yprev = 8192 * ((e_i + 1) % 2) + 1024 * od + 512 * tsub
                            if e_i == 0:
                                bfree3[bi] = p.tt(Y[0:128, ycol:ycol + 512],
                                                  bank[:, :], cwb, AL.mult,
                                                  waits=[mmf, ecw])
                            else:
                                tmp = PBUF[0:128, 0:512]
                                bfree3[bi] = p.tt(tmp, bank[:, :], cwb, AL.mult,
                                                  waits=[mmf, ecw])
                                p.tt(Y[0:128, ycol:ycol + 512],
                                     Y[0:128, yprev:yprev + 512], tmp, AL.add,
                                     waits=[])
                ylast = 8192 * ((NE - 1) % 2)
                for od in range(8):
                    for tsub in range(2):
                        xb = SCR[0:128, 0:512]
                        mx_ = p.dma("ds", xb,
                                    bass.AP(x1src, (128 * od) * OWN + 1024 * tb
                                            + 512 * tsub,
                                            [[OWN, 128], [1, 512]]),
                                    waits=p.bar("ds"))
                        rb = EB[0:128, 0:512]
                        mr = p.dma("ds", rb,
                                   bass.AP(pp_out1, (128 * od) * OWN + 1024 * tb
                                           + 512 * tsub,
                                           [[OWN, 128], [1, 512]]),
                                   waits=[cc_l1] + p.bar("ds"))
                        t1 = SCR[0:128, 512:1024]
                        p.tt(t1, xb, rb, AL.add, waits=[mr, mx_, ("v", p.cnt["v"]),
                                                        ("t", p.cnt["t"])])
                        ycol = ylast + 1024 * od + 512 * tsub
                        t2 = SCR[0:128, 2048:2560]
                        p.tt(t2, t1, Y[0:128, ycol:ycol + 512], AL.add,
                             waits=p.bar("v"))
                        p.dma("ds", bass.AP(y_out, (128 * od) * OWN + 1024 * tb
                                            + 512 * tsub,
                                            [[OWN, 128], [1, 512]]),
                              t2, waits=p.bar("ds"))

        if dbg_out is not None:
            p.dma("dg", dbg2_out[:, 0:72], PBUF[0:128, 0:72], waits=p.bar("dg"))
            p.dma("dg", bass.AP(dbg_out, 0, [[2048, 8], [1, 2048]]),
                  SMB[0:8, 0:2048], waits=p.bar("dg"))
            p.dma("dg", bass.AP(dbg_out, 8 * 2048, [[2048, 8], [1, 2048]]),
                  SCR[0:8, 0:2048], waits=p.bar("dg"))
            p.dma("dg", bass.AP(dbg_out, 16 * 2048, [[2048, 8], [1, 2048]]),
                  XN[0:8, 0:2048], waits=p.bar("dg"))
        p.op("ds", lambda e: e.nop(), waits=p.bar(), incs=())

        @block.tensor
        def _(t):
            p.replay_eng = None
            for e, fn in p.sched:
                if e == "t":
                    fn(t)

        @block.vector
        def _(v):
            for e, fn in p.sched:
                if e == "v":
                    fn(v)

        @block.scalar
        def _(s):
            for e, fn in p.sched:
                if e == "s":
                    fn(s)

        @block.gpsimd
        def _(g):
            for e, fn in p.sched:
                if e in ("dg", "g", "cc"):
                    fn(g)

        @block.sync
        def _(s):
            for e, fn in p.sched:
                if e == "ds":
                    fn(s)

    nc._input_names = input_names
    return nc



def _run(mode, in_maps):
    nc = build_program(mode)
    names = set(nc._input_names)
    in_maps = [{k: v for k, v in m.items() if k in names} for m in in_maps]
    return run_bass_kernel_spmd(nc, in_maps, core_ids=list(range(8)))


def kernel(**inputs):
    in_maps = [stage_core(inputs, c) for c in range(8)]
    res = _run("FUSED", in_maps)
    out = np.empty((4, T, DM), np.float32)
    for c in range(8):
        b, h = c // 2, c % 2
        y = np.asarray(res.results[c]["y_out"], np.float32)
        out[b, 2048 * h:2048 * h + 2048, :] = y.T
    return out


# version 08
# speedup vs baseline: 1.5336x; 1.0705x over previous
import math
from contextlib import ExitStack

import numpy as np

import concourse.bass as bass
import concourse.mybir as mybir
from concourse.bass_utils import run_bass_kernel_spmd
from concourse.alu_op_type import AluOpType as AL

F32 = mybir.dt.float32
BF16 = mybir.dt.bfloat16
I32 = mybir.dt.int32
AF = mybir.ActivationFunctionType
AX = mybir.AxisListType

T = 4096
DM = 1024
FH = 3072
MH = 2816
NE = 8
FLEN = 8448
FPAD = 4223
WINW = 4608
WINBASE = 3584
SHIFT = 16.0
QS = 0.125
OWN = 2048
EPS = 1e-5
NKC = 32

KCMIN = [max(0, (3457 - 512 * g + 127) // 128) for g in range(8)]



def _bucket_np():
    d = np.arange(T)
    rp = np.maximum(d, 1).astype(np.float32)
    large = 16 + (np.log(rp / np.float32(16.0))
                  / np.float32(math.log(8.0)) * 16).astype(np.int32)
    large = np.minimum(large, 31)
    return np.where(d < 16, d, large)


def stage_core(inputs, core):
    b, h = core // 2, core % 2
    x = np.ascontiguousarray(np.asarray(inputs["L0_attn_x"][b]).T.astype(np.float32, copy=False))
    d = {}
    d["x_fm"] = x
    d["x_own"] = np.ascontiguousarray(x[:, 2048 * h:2048 * h + 2048])

    Wa = np.asarray(inputs["L0_attn_W_attn"])
    ba = np.asarray(inputs["L0_attn_b_attn"])
    l0w = np.empty((4, 1024, 384), np.float32)
    l0b = np.empty((4, 128, 3), np.float32)
    for p in range(4):
        P = 4 * h + p
        l0w[p, :, 0:128] = Wa[:, 128 * P:128 * P + 128]
        l0w[p, :, 128:256] = Wa[:, 1024 + 128 * P:1024 + 128 * P + 128]
        l0w[p, :, 256:384] = Wa[:, 2048 + 128 * P:2048 + 128 * P + 128]
        l0b[p, :, 0] = ba[128 * P:128 * P + 128]
        l0b[p, :, 1] = ba[1024 + 128 * P:1024 + 128 * P + 128]
        l0b[p, :, 2] = ba[2048 + 128 * P:2048 + 128 * P + 128]
    d["l0w"] = l0w
    d["l0b"] = l0b
    d["l0proj"] = np.ascontiguousarray(
        np.asarray(inputs["L0_attn_W_proj"])[512 * h:512 * h + 512, :])
    pb = np.asarray(inputs["L0_attn_b_proj"]).reshape(8, 128).T
    d["l0pb"] = np.ascontiguousarray(pb) if h == 0 else np.zeros((128, 8), np.float32)

    d["table_own"] = np.ascontiguousarray(
        np.asarray(inputs["L0_attn_bias_table"])[:, 8 * h:8 * h + 8])
    bk = _bucket_np()
    oh = np.zeros((32, T), np.float32)
    oh[bk, np.arange(T)] = 1.0
    d["oh0"] = oh

    d["wg_t"] = np.ascontiguousarray(np.asarray(inputs["L0_ffn_w_gate"]).T)
    d["wu_t"] = np.ascontiguousarray(np.asarray(inputs["L0_ffn_w_up"]).T)
    d["wd_t"] = np.ascontiguousarray(np.asarray(inputs["L0_ffn_w_down"]).T)

    l1w = np.empty((4, 1024, 384), np.float32)
    for p in range(4):
        P = 4 * h + p
        l1w[p, :, 0:128] = np.asarray(inputs["L1_attn_Wq"])[128 * P:128 * P + 128, :].T
        l1w[p, :, 128:256] = np.asarray(inputs["L1_attn_Wk"])[128 * P:128 * P + 128, :].T
        l1w[p, :, 256:384] = np.asarray(inputs["L1_attn_Wv"])[128 * P:128 * P + 128, :].T
    d["l1w"] = l1w
    d["l1proj"] = np.ascontiguousarray(
        np.asarray(inputs["L1_attn_Wo"]).T[512 * h:512 * h + 512, :])
    d["l1bias_own"] = np.ascontiguousarray(
        np.asarray(inputs["L1_attn_bias"])[8 * h:8 * h + 8, :])
    d["tau_own"] = np.ascontiguousarray(
        np.asarray(inputs["L1_attn_tau"])[8 * h:8 * h + 8].reshape(1, 8))
    d["riot"] = (1.0 / np.arange(1, T + 1, dtype=np.float64)).astype(np.float32).reshape(1, T)

    Wg = np.asarray(inputs["L1_ffn_Wg"])
    d["wg8"] = np.ascontiguousarray(
        Wg.T.reshape(8, 128, 8).transpose(1, 0, 2).reshape(128, 64))
    d["w1t"] = np.ascontiguousarray(np.transpose(np.asarray(inputs["L1_ffn_W1"]), (0, 2, 1)))
    d["w3t"] = np.ascontiguousarray(np.transpose(np.asarray(inputs["L1_ffn_W3"]), (0, 2, 1)))
    d["w2t"] = np.ascontiguousarray(np.transpose(np.asarray(inputs["L1_ffn_W2"]), (0, 2, 1)))
    d["ident"] = np.eye(128, dtype=np.float32)
    return d



class PB:
    ENGS = ("t", "v", "s", "g", "cc")
    NSLOT = 12

    def __init__(self, nc):
        self.nc = nc
        self.sched = []
        self.cnt = {}
        self.hdl = {}
        self.lastwait = {}
        self.dmaq = {}

    def sem(self, ctx, name):
        self.hdl[name] = ctx.enter_context(self.nc.semaphore(name))
        self.cnt[name] = 0

    def dsem(self, ctx, q):
        self.dmaq[q] = 0
        for i in range(self.NSLOT):
            n = f"{q}{i}"
            self.hdl[n] = ctx.enter_context(self.nc.semaphore(n))
            self.cnt[n] = 0

    def op(self, eng, build_fn, waits=(), incs=()):
        if eng in ("v", "s", "g"):
            waits = list(waits) + [(eng, self.cnt[eng])]
        w = []
        for sname, v in waits:
            assert v <= self.cnt[sname], (sname, v, self.cnt[sname])
            if v <= 0:
                continue
            key = (eng, sname)
            if self.lastwait.get(key, -1) >= v:
                continue
            self.lastwait[key] = v
            w.append((self.hdl[sname], v))
        iz = []
        for sname, amt in incs:
            self.cnt[sname] += amt
            iz.append((self.hdl[sname], amt))

        def fn(e, w=w, iz=iz, build_fn=build_fn):
            for hh, v in w:
                e.wait_ge(hh, v)
            ins = build_fn(e)
            for hh, amt in iz:
                ins.then_inc(hh, amt)
        self.sched.append((eng, fn))

    def bar(self, *exclude):
        ex = set(exclude)
        w = [(n, self.cnt[n]) for n in self.ENGS if n not in ex and self.cnt[n] > 0]
        for q in self.dmaq:
            if q in ex:
                continue
            for i in range(self.NSLOT):
                n = f"{q}{i}"
                if self.cnt[n] > 0:
                    w.append((n, self.cnt[n]))
        return w

    def dma(self, q, out, in_, waits=()):
        j = self.dmaq[q]
        self.dmaq[q] = j + 1
        n = f"{q}{j % self.NSLOT}"
        w = list(waits)
        if self.cnt[n] > 0:
            w.append((n, self.cnt[n]))
        self.op(q, lambda e: e.dma_start(out=out, in_=in_), w, ((n, 16),))
        return (n, self.cnt[n])

    def mm(self, out, lhsT, rhs, start, stop, tp=None, waits=(), fin=False):
        incs = (("t", 1),) if fin else ()
        self.op("t", lambda e: e.matmul(out, lhsT, rhs, start=start, stop=stop,
                                        tile_position=tp), waits, incs)
        return ("t", self.cnt["t"])

    def tr(self, out, in_, ident, waits=()):
        self.op("t", lambda e: e.transpose(out, in_, ident), waits, (("t", 1),))
        return ("t", self.cnt["t"])

    def act(self, out, in_, func, bias=0.0, scale=1.0, waits=()):
        self.op("s", lambda e: e.activation(out, in_, func, bias=bias, scale=scale),
                waits, (("s", 1),))
        return ("s", self.cnt["s"])

    def scopy(self, out, in_, waits=()):
        self.op("s", lambda e: e.copy(out, in_), waits, (("s", 1),))
        return ("s", self.cnt["s"])

    def tt(self, out, in0, in1, alu, waits=()):
        self.op("v", lambda e: e.tensor_tensor(out, in0, in1, alu), waits, (("v", 1),))
        return ("v", self.cnt["v"])

    def ts(self, out, in0, s1, s2, op0, op1=None, waits=()):
        if op1 is None:
            self.op("v", lambda e: e.tensor_scalar(out, in0, s1, s2, op0),
                    waits, (("v", 1),))
        else:
            self.op("v", lambda e: e.tensor_scalar(out, in0, s1, s2, op0, op1),
                    waits, (("v", 1),))
        return ("v", self.cnt["v"])

    def stt(self, out, in0, scalar, in1, op0, op1, waits=()):
        self.op("v", lambda e: e.scalar_tensor_tensor(out, in0, scalar, in1, op0, op1),
                waits, (("v", 1),))
        return ("v", self.cnt["v"])

    def vred(self, out, in_, alu, negate=False, waits=()):
        self.op("v", lambda e: e.tensor_reduce(out, in_, AX.X, alu, negate=negate),
                waits, (("v", 1),))
        return ("v", self.cnt["v"])

    def vcopy(self, out, in_, waits=()):
        self.op("v", lambda e: e.tensor_copy(out, in_), waits, (("v", 1),))
        return ("v", self.cnt["v"])

    def vrecip(self, out, in_, waits=()):
        self.op("v", lambda e: e.reciprocal(out, in_), waits, (("v", 1),))
        return ("v", self.cnt["v"])

    def recip_pos(self, out, in_, tmp, waits=()):
        self.act_ln = None
        self.op("s", lambda e: e.activation(tmp, in_, AF.Ln), waits, (("s", 1),))
        self.op("s", lambda e: e.activation(out, tmp, AF.Exp, scale=-1.0), (),
                (("s", 1),))
        return ("s", self.cnt["s"])

    def vmemset(self, ap, val, waits=()):
        self.op("v", lambda e: e.memset(ap, val), waits, (("v", 1),))
        return ("v", self.cnt["v"])



def build_program(mode, race_detect=False, skip=()):
    nc = bass.Bass("TRN2", target_bir_lowering=False, num_devices=8,
                   detect_race_conditions=race_detect)
    L0 = mode in ("L0", "FUSED")
    L1 = mode in ("L1", "FUSED")

    input_names = []

    def din(name, shape, dt):
        input_names.append(name)
        return nc.dram_tensor(name, shape, dt, kind="ExternalInput")

    ident_d = din("ident", [128, 128], F32)
    riot_d = din("riot", [1, T], F32)
    if L0:
        x_fm = din("x_fm", [DM, T], F32)
        x_own = din("x_own", [DM, OWN], F32)
        l0w = din("l0w", [4, DM, 384], F32)
        l0b = din("l0b", [4, 128, 3], F32)
        l0proj = din("l0proj", [512, DM], F32)
        l0pb = din("l0pb", [128, 8], F32)
        table_own = din("table_own", [32, 8], F32)
        oh0 = din("oh0", [32, T], F32)
        wg_t = din("wg_t", [DM, FH], I32)
        wu_t = din("wu_t", [DM, FH], I32)
        wd_t = din("wd_t", [FH, DM], I32)
    if L1:
        l1w = din("l1w", [4, DM, 384], F32)
        l1proj = din("l1proj", [512, DM], F32)
        l1bias_own = din("l1bias_own", [8, 2048], F32)
        tau_own = din("tau_own", [1, 8], F32)
        wg8 = din("wg8", [128, 64], F32)
        w1t = din("w1t", [NE, DM, MH], F32)
        w3t = din("w3t", [NE, DM, MH], F32)
        w2t = din("w2t", [NE, MH, DM], F32)
        if mode == "L1":
            x1_in = din("x1_in", [DM, OWN], F32)

    if mode == "L0":
        x1_out = nc.dram_tensor("x1_out", [DM, OWN], F32, kind="ExternalOutput")
    else:
        y_out = nc.dram_tensor("y_out", [DM, OWN], F32, kind="ExternalOutput")
    dbg_out = (nc.dram_tensor("dbg_out", [24, 2048], F32, kind="ExternalOutput")
               if "dbg" in skip else None)
    dbg2_out = (nc.dram_tensor("dbg2_out", [128, 128], F32, kind="ExternalOutput")
                if "dbg" in skip else None)

    F0d = nc.dram_tensor("F0d", [8, FLEN], BF16) if L0 else None
    F1d = nc.dram_tensor("F1d", [8, FLEN], BF16) if L1 else None
    xn0_d = nc.dram_tensor("xn0_d", [DM, T], BF16) if L0 else None
    x1_d = nc.dram_tensor("x1_d", [DM, OWN], F32)
    pp_in = nc.dram_tensor("pp_in", [2, DM, OWN], BF16)
    pp_out = nc.dram_tensor("pp_out", [DM, OWN], BF16)
    pp_in1 = nc.dram_tensor("pp_in1", [2, DM, OWN], BF16)
    pp_out1 = nc.dram_tensor("pp_out1", [DM, OWN], BF16)
    if L1:
        xn1_in = nc.dram_tensor("xn1_in", [DM, OWN], BF16)
        xn1_g = nc.dram_tensor("xn1_g", [2, DM, OWN], BF16)

    p = PB(nc)
    with ExitStack() as ctx:
        en = ctx.enter_context
        XN = en(nc.sbuf_tensor("XN", [128, 8 * 2048], BF16))
        Y = en(nc.sbuf_tensor("Y", [128, 4 * 4096], BF16))
        M1 = en(nc.sbuf_tensor("M1", [128, 24576], BF16))
        M2 = en(nc.sbuf_tensor("M2", [128, 20480], BF16))
        WINB = en(nc.sbuf_tensor("WINB", [128, WINW], BF16))
        QP = en(nc.sbuf_tensor("QP", [128, T], BF16))
        EB = en(nc.sbuf_tensor("EB", [128, 2048], BF16))
        PBUF = en(nc.sbuf_tensor("PBUF", [128, 1024], BF16))
        CWB = en(nc.sbuf_tensor("CWB", [128, 1024], BF16))
        STAT = en(nc.sbuf_tensor("STAT", [128, 2048], F32))
        SCR = en(nc.sbuf_tensor("SCR", [128, 3072], F32))
        SMB = en(nc.sbuf_tensor("SMB", [128, 2048], BF16))
        ROWS = en(nc.sbuf_tensor("ROWS", [128, 1664], BF16))
        IDT = en(nc.sbuf_tensor("IDT", [128, 128], BF16))
        BIA = en(nc.sbuf_tensor("BIA", [128, 32], F32))
        QA = en(nc.psum_tensor("QA", [128, 1024], F32))
        QB = en(nc.psum_tensor("QB", [128, 1024], F32))
        QC = en(nc.psum_tensor("QC", [128, 1024], F32))
        QD = en(nc.psum_tensor("QD", [128, 512], F32))
        PSB = en(nc.psum_tensor("PSB", [128, 1024], BF16))
        PS = [QA[0:128, 0:512], QA[0:128, 512:1024],
              QB[0:128, 0:512], QB[0:128, 512:1024],
              QC[0:128, 0:512], QC[0:128, 512:1024],
              QD[0:128, 0:512]]

        for s in PB.ENGS:
            p.sem(ctx, s)
        p.dsem(ctx, "dg")
        p.dsem(ctx, "ds")
        block = en(nc.Block())

        KR0, VR0, PTA0 = 16384, 20480, 0
        VT0, PTB0 = 16384, 0

        ONES1 = ROWS[0:1, 0:128]
        ONES128 = ROWS[0:128, 0:1]
        ONESF = BIA[0:128, 26:27]

        m_id = p.dma("dg", IDT[:, :], ident_d[:, :])
        p.vmemset(ROWS[0:128, 0:128], 1.0)
        p.vmemset(ONESF, 1.0)
        p.vmemset(BIA[0:128, 24:25], -SHIFT)
        p.vmemset(BIA[0:128, 25:26], EPS)

        def build_F(Fd, is_l0):
            zt = SMB[0:16, 256:768]
            p.vmemset(zt, 0.0)
            for i in range(9):
                w = min(512, FPAD - 512 * i)
                p.dma("ds", bass.AP(Fd, 512 * i, [[FLEN, 8], [1, w]]),
                      zt[0:8, 0:w], waits=p.bar("ds"))
            if is_l0:
                tb = SMB[0:32, 0:16]
                p.dma("dg", tb[0:32, 0:8], table_own[:, :])
                ohc = SMB[0:32, 768:1280]
                for i in range(8):
                    p.dma("dg", ohc[:, :], oh0[:, 512 * i:512 * i + 512],
                          waits=p.bar("dg"))
                    bank = PS[0]
                    mmf = p.mm(bank[0:8, :], tb[0:32, 0:8], ohc[:, :], True, True,
                               waits=p.bar("t"), fin=True)
                    ev = p.act(SMB[0:8, 1280:1792], bank[0:8, :], AF.Exp,
                               bias=BIA[0:8, 24:25], waits=p.bar("s"))
                    p.dma("ds", bass.AP(Fd, FPAD + 512 * i, [[FLEN, 8], [1, 512]]),
                          SMB[0:8, 1280:1792], waits=[ev])
                p.dma("ds", bass.AP(Fd, FPAD + 4096, [[FLEN, 8], [1, FLEN - FPAD - 4096]]),
                      zt[0:8, 0:FLEN - FPAD - 4096], waits=p.bar("ds"))
            else:
                bc = SMB[0:8, 768:1280]
                for i in range(4):
                    p.dma("ds", SCR[0:8, 2048:2560],
                          l1bias_own[:, 512 * i:512 * i + 512], waits=p.bar("ds"))
                    ev = p.act(bc, SCR[0:8, 2048:2560], AF.Exp,
                               bias=BIA[0:8, 24:25], waits=p.bar("s"))
                    p.dma("ds", bass.AP(Fd, FPAD + 512 * i, [[FLEN, 8], [1, 512]]),
                          bc, waits=[ev])
                et = SMB[0:16, 1280:1792]
                p.vmemset(et, math.exp(-SHIFT), waits=p.bar("v"))
                off = FPAD + 2048
                rem = FLEN - off
                for i in range((rem + 511) // 512):
                    w = min(512, rem - 512 * i)
                    p.dma("ds", bass.AP(Fd, off + 512 * i, [[FLEN, 8], [1, w]]),
                          et[0:8, 0:w], waits=p.bar("ds"))

        if L0:
            build_F(F0d, True)
        if L1:
            build_F(F1d, False)

        def layernorm(src_d, ntok, out_sbuf, out_dram):
            stride = int(src_d.shape[1])
            nts = ntok // 512
            for tsl in range(nts):
                c0 = 512 * tsl
                for dchunk in range(8):
                    xa = SCR[0:128, 2048:2560]
                    p.dma("ds", xa,
                          bass.AP(src_d, (128 * dchunk) * stride + c0,
                                  [[stride, 128], [1, 512]]),
                          waits=p.bar("ds"))
                    p.mm(PS[0][0:1, :], ONESF, xa, dchunk == 0, dchunk == 7,
                         waits=p.bar("t"), fin=(dchunk == 7))
                    p.act(SCR[0:128, 2560:3072], xa, AF.Square, waits=p.bar("s"))
                    p.mm(PS[1][0:1, :], ONESF, SCR[0:128, 2560:3072],
                         dchunk == 0, dchunk == 7, waits=p.bar("t"),
                         fin=True)
                sA = STAT[0:1, 0:512]
                sB = STAT[0:1, 512:1024]
                sC = STAT[0:1, 1024:1536]
                sD = STAT[0:1, 1536:2048]
                p.ts(sA, PS[0][0:1, :], 1.0 / DM, None, AL.mult, waits=p.bar("v"))
                p.ts(sB, PS[1][0:1, :], 1.0 / DM, None, AL.mult, waits=p.bar("v"))
                p.tt(sC, sA, sA, AL.mult, waits=p.bar("v"))
                p.tt(sD, sB, sC, AL.subtract, waits=p.bar("v"))
                p.act(sC, sD, AF.Ln, bias=BIA[0:1, 25:26], waits=p.bar("s"))
                p.act(sD, sC, AF.Exp, scale=-0.5, waits=p.bar("s"))
                p.tt(sC, sA, sD, AL.mult, waits=p.bar("v"))
                ab = ROWS[0:1, 128:640]
                bb = ROWS[0:1, 640:1152]
                p.vcopy(ab, sD, waits=p.bar("v"))
                p.vcopy(bb, sC, waits=p.bar("v"))
                p.mm(PS[2][0:128, :], ONES1, ab, True, True,
                     waits=p.bar("t"), fin=True)
                p.mm(PS[3][0:128, :], ONES1, bb, True, True,
                     waits=p.bar("t"), fin=True)
                p.scopy(SCR[0:128, 2048:2560], PS[2][:, :], waits=p.bar("s"))
                p.scopy(SCR[0:128, 2560:3072], PS[3][:, :], waits=p.bar("s"))
                for dchunk in range(8):
                    xb = SCR[0:128, 0:512]
                    p.dma("ds", xb,
                          bass.AP(src_d, (128 * dchunk) * stride + c0,
                                  [[stride, 128], [1, 512]]),
                          waits=p.bar("ds"))
                    t1 = STAT[0:128, 0:512]
                    p.tt(t1, xb, SCR[0:128, 2048:2560], AL.mult, waits=p.bar("v"))
                    if out_sbuf is not None:
                        dst = out_sbuf[0:128, 2048 * dchunk + c0:
                                       2048 * dchunk + c0 + 512]
                    else:
                        dst = EB[0:128, 0:512]
                    p.tt(dst, t1, SCR[0:128, 2560:3072], AL.subtract,
                         waits=p.bar("v"))
                    if out_dram is not None:
                        p.dma("ds",
                              bass.AP(out_dram, (128 * dchunk) * int(out_dram.shape[1]) + c0,
                                      [[int(out_dram.shape[1]), 128], [1, 512]]),
                              dst, waits=p.bar("ds"))

        def attention(layer):
            wsrc = l0w if layer == 0 else l1w
            Fd = F0d if layer == 0 else F1d
            if layer == 1:
                p.dma("ds", BIA[0:1, 16:24], tau_own[:, :], waits=p.bar("ds"))
            for pr in range(4):
                WPS = WINB[0:128, 0:3072]
                mws = []
                tw_ = [("t", p.cnt["t"]), ("s", p.cnt["s"]), ("v", p.cnt["v"])]
                for c in range(8):
                    mws.append(p.dma("dg", WPS[:, 384 * c:384 * c + 384],
                                     bass.AP(wsrc, pr * DM * 384 + (128 * c) * 384,
                                             [[384, 128], [1, 384]]),
                                     waits=tw_))
                if layer == 0:
                    mws.append(p.dma("dg", BIA[0:128, 0:3],
                                     bass.AP(l0b, pr * 128 * 3, [[3, 128], [1, 3]]),
                                     waits=p.bar("dg")))
                    p.ts(BIA[0:128, 4:5], BIA[0:128, 0:1], QS, None, AL.mult,
                         waits=mws)
                evtok = {}
                rbtok = {}
                mmv_tok = {}
                for tsl in range(8):
                    par = tsl % 2
                    B0 = PS[3 * par]
                    B1 = PS[3 * par + 1]
                    B2 = PS[3 * par + 2]
                    mm_q = mm_k = mm_v = None
                    for dchunk in range(8):
                        rb = EB[0:128, 512 * (dchunk % 2):512 * (dchunk % 2) + 512]
                        if layer == 0:
                            srcp = bass.AP(xn0_d, (128 * dchunk) * T + 512 * tsl,
                                           [[T, 128], [1, 512]])
                        else:
                            gg = tsl // 4
                            lc = 512 * (tsl % 4)
                            srcp = bass.AP(xn1_g, gg * DM * OWN + (128 * dchunk) * OWN + lc,
                                           [[OWN, 128], [1, 512]])
                        key = (tsl * 8 + dchunk) % 2
                        prev = mmv_tok.get((tsl * 8 + dchunk) - 2)
                        mrb = p.dma("dg", rb, srcp,
                                    waits=([x for x in (prev,) if x]
                                           + (p.bar("dg") if tsl == 0 and dchunk < 2
                                              else [])))
                        st_, sp = dchunk == 0, dchunk == 7
                        wl = [mrb]
                        if st_:
                            wl += mws
                            for role in range(3):
                                tok = evtok.get((par, role))
                                if tok:
                                    wl.append(tok)
                        mm_q = p.mm(B0[:, :], WPS[:, 384 * dchunk:384 * dchunk + 128],
                                    rb, st_, sp, waits=wl, fin=sp)
                        mm_k = p.mm(B1[:, :], WPS[:, 384 * dchunk + 128:384 * dchunk + 256],
                                    rb, st_, sp, waits=[], fin=sp)
                        mm_v = p.mm(B2[:, :], WPS[:, 384 * dchunk + 256:384 * dchunk + 384],
                                    rb, st_, sp, waits=[], fin=True)
                        mmv_tok[tsl * 8 + dchunk] = ("t", p.cnt["t"])
                    if layer == 0:
                        evtok[(par, 0)] = p.act(QP[:, 512 * tsl:512 * tsl + 512],
                                                B0[:, :], AF.Identity,
                                                bias=BIA[0:128, 4:5], scale=QS,
                                                waits=[mm_q])
                        evtok[(par, 1)] = p.ts(
                            bass.AP(M1, KR0 + (T - 1 - 512 * tsl),
                                    [[24576, 128], [-1, 512]]),
                            B1[:, :], BIA[0:128, 1:2], None, AL.add, waits=[mm_k])
                        evtok[(par, 2)] = p.ts(
                            bass.AP(M1, VR0 + (T - 1 - 512 * tsl),
                                    [[24576, 128], [-1, 512]]),
                            B2[:, :], BIA[0:128, 2:3], None, AL.add, waits=[mm_v])
                    else:
                        evtok[(par, 0)] = p.act(QP[:, 512 * tsl:512 * tsl + 512],
                                                B0[:, :], AF.Identity, scale=QS,
                                                waits=[mm_q])
                        evtok[(par, 1)] = p.vcopy(
                            bass.AP(M1, KR0 + (T - 1 - 512 * tsl),
                                    [[24576, 128], [-1, 512]]),
                            B1[:, :], waits=[mm_k])
                        evtok[(par, 2)] = p.vcopy(
                            bass.AP(M1, VR0 + (T - 1 - 512 * tsl),
                                    [[24576, 128], [-1, 512]]),
                            B2[:, :], waits=[mm_v])
                kvdone = [evtok[(q, r)] for q in range(2) for r in range(3)]
                trtok = {}
                cptok = {}
                for kc in range(NKC):
                    pdst = (PSB[0:128, 0:128] if kc % 2 == 0
                            else QD.ap().bitcast(BF16)[0:128, 0:128])
                    wl = [x for x in (cptok.get(kc - 2),) if x]
                    if kc <= 1:
                        wl = wl + kvdone
                    trtok[kc] = p.tr(pdst, M1[0:128, VR0 + 128 * kc:VR0 + 128 * kc + 128],
                                     IDT[:, :], waits=wl)
                    cptok[kc] = p.vcopy(M2[0:128, VT0 + 128 * kc:VT0 + 128 * kc + 128],
                                        pdst, waits=[trtok[kc]])
                vt_done = [cptok[NKC - 1], cptok[NKC - 2]]
                extok = {}
                multok = {}
                sgb_free = None
                pvb_free = None
                er_prev = None
                for hh in range(2):
                    head = 2 * pr + hh
                    rb0 = 64 * hh
                    tp_row = (rb0, 0)
                    tp_col = None if hh == 0 else (0, 64)
                    ob0 = 64 * hh
                    mwin = p.dma("ds", WINB[:, :],
                                 bass.AP(Fd, head * FLEN + WINBASE,
                                         [[1, 128], [1, WINW]]),
                                 waits=list(multok.values()) + kvdone)
                    sqbanks = [QA, QB]
                    npar = 2
                    for g in range(8):
                        q0 = 512 * g
                        kmin = KCMIN[g]
                        nk = NKC - kmin
                        npair = nk // 2
                        PVb = QC[0:128, 0:512]
                        SGb = QD[0:128, 0:512]
                        slab = M1 if hh == 0 else M2
                        slab0 = PTA0 if hh == 0 else PTB0
                        for j in range(npair):
                            kc = kmin + 2 * j
                            par = j % npar
                            sq = sqbanks[par]
                            rhs = QP[rb0:rb0 + 64, q0:q0 + 512]
                            wl = [mwin] + [x for x in (extok.get(par),) if x]
                            if g == 0 and j < npar and hh == 0:
                                wl += kvdone + vt_done
                            p.mm(sq[0:128, 0:512],
                                 M1[rb0:rb0 + 64, KR0 + 128 * kc:KR0 + 128 * kc + 128],
                                 rhs, True, True, tp=tp_row, waits=wl, fin=True)
                            mmS = p.mm(sq[0:128, 512:1024],
                                       M1[rb0:rb0 + 64,
                                          KR0 + 128 * kc + 128:KR0 + 128 * kc + 256],
                                       rhs, True, True, tp=tp_row, waits=[], fin=True)
                            ebuf = EB[0:128, 1024 * par:1024 * par + 1024]
                            ex = p.act(ebuf, sq[0:128, 0:1024], AF.Exp,
                                       waits=[mmS] + [x for x in (multok.get(par),) if x])
                            extok[par] = ex
                            u0 = q0 + 128 * kc - 3456
                            if layer == 0:
                                pt = slab[0:128, slab0 + 1024 * par:
                                          slab0 + 1024 * par + 1024]
                            else:
                                pt = slab[0:128, slab0 + 1024 * j:slab0 + 1024 * j + 1024]
                            win2 = bass.AP(WINB, u0, [[WINW, 128], [128, 2], [1, 512]])
                            mul = p.tt(pt, ebuf, win2, AL.mult, waits=[ex])
                            multok[par] = mul
                            p.mm(SGb[0:1, :], ONES128, pt[0:128, 0:512],
                                 j == 0, False,
                                 waits=[mul] + ([x for x in (sgb_free,) if x]
                                                if j == 0 else []),
                                 fin=False)
                            p.mm(SGb[0:1, :], ONES128, pt[0:128, 512:1024],
                                 False, j == npair - 1, waits=[],
                                 fin=(j == npair - 1))
                            if layer == 0:
                                for u in range(2):
                                    vts = M2[0:128, VT0 + 128 * (kc + u) + 64 * hh:
                                             VT0 + 128 * (kc + u) + 64 * hh + 64]
                                    p.mm(PVb[ob0:ob0 + 64, 0:512], vts,
                                         pt[0:128, 512 * u:512 * u + 512],
                                         j == 0 and u == 0, j == npair - 1 and u == 1,
                                         tp=tp_col,
                                         waits=([x for x in (pvb_free,) if x]
                                                if j == 0 and u == 0 else []),
                                         fin=(j == npair - 1 and u == 1))
                        sg_done = ("t", p.cnt["t"])
                        if layer == 1:
                            rio = STAT[0:1, 0:512]
                            mrio = p.dma("ds", rio, riot_d[0:1, q0:q0 + 512],
                                         waits=p.bar("ds"))
                            cr0 = STAT[0:1, 512:1024]
                            t1_ = p.tt(cr0, SGb[0:1, :], rio, AL.mult,
                                       waits=[mrio, sg_done])
                            cr = STAT[0:1, 1024:1536]
                            t2_ = p.ts(cr, cr0, BIA[0:1, 16 + head:17 + head], None,
                                       AL.mult, waits=[t1_])
                            crb = ROWS[0:1, 128:640]
                            c1_ = p.vcopy(crb, cr, waits=[t2_])
                            PCh = QC[0:128, 512:1024]
                            mmB = p.mm(PCh, ONES1, crb, True, True,
                                       waits=[c1_] + [x for x in (er_prev,) if x],
                                       fin=True)
                            tb2 = CWB[0:128, 0:1024]
                            e1_ = p.scopy(CWB[0:128, 0:512], PCh, waits=[mmB])
                            etb = p.scopy(CWB[0:128, 512:1024], PCh, waits=[e1_])
                            for j in range(npair):
                                kc = kmin + 2 * j
                                pt = slab[0:128, slab0 + 1024 * j:slab0 + 1024 * j + 1024]
                                pb1 = PBUF[0:128, 0:1024]
                                pbm = EB[0:128, 0:1024]
                                ad = p.tt(pb1, pt, tb2, AL.add, waits=[etb])
                                mx = p.ts(pbm, pb1, 0.0, None, AL.max, waits=[ad])
                                for u in range(2):
                                    vts = M2[0:128, VT0 + 128 * (kc + u) + 64 * hh:
                                             VT0 + 128 * (kc + u) + 64 * hh + 64]
                                    p.mm(PVb[ob0:ob0 + 64, 0:512], vts,
                                         pbm[0:128, 512 * u:512 * u + 512],
                                         j == 0 and u == 0, j == npair - 1 and u == 1,
                                         tp=tp_col,
                                         waits=([mx] if u == 0 else [])
                                         + ([x for x in (pvb_free,) if x]
                                            if j == 0 and u == 0 else []),
                                         fin=(j == npair - 1 and u == 1))
                        pv_done = ("t", p.cnt["t"])
                        rc = STAT[0:1, 1536:2048]
                        rctok = p.recip_pos(rc, SGb[0:1, :], STAT[0:1, 0:512]
                                            if layer == 0 else STAT[0:1, 512:1024],
                                            waits=[sg_done])
                        sgb_free = rctok
                        rcb = ROWS[0:1, 640:1152]
                        c2_ = p.vcopy(rcb, rc, waits=[rctok])
                        mmR = p.mm(PS[5][ob0:ob0 + 64, :], ONES1[0:1, 0:64], rcb,
                                   True, True, tp=tp_col,
                                   waits=[c2_, ("s", p.cnt["s"])]
                                   + [x for x in (er_prev,) if x],
                                   fin=True)
                        rbuf = CWB[0:128, 512:1024]
                        er = p.scopy(rbuf[ob0:ob0 + 64, 0:512], PS[5][ob0:ob0 + 64, :],
                                     waits=[mmR])
                        er_prev = er
                        ycol = 4096 * pr + 512 * g
                        pvb_free = p.tt(Y[rb0:rb0 + 64, ycol:ycol + 512],
                                        PVb[ob0:ob0 + 64, :],
                                        rbuf[ob0:ob0 + 64, 0:512], AL.mult,
                                        waits=[er, pv_done])

        def proj_partial(layer):
            wsrc = l0proj if layer == 0 else l1proj
            dst = pp_in if layer == 0 else pp_in1
            cout = pp_out if layer == 0 else pp_out1
            Wp = M1[0:128, 0:4096]
            mload = []
            tw_ = p.bar("dg")
            for c in range(4):
                mload.append(p.dma("dg", Wp[:, 1024 * c:1024 * c + 1024],
                                   bass.AP(wsrc, (128 * c) * DM, [[DM, 128], [1, 1024]]),
                                   waits=tw_))
            if layer == 0:
                mload.append(p.dma("dg", BIA[0:128, 8:16], l0pb[:, :],
                                   waits=p.bar("dg")))
            yready = [("v", p.cnt["v"]), ("s", p.cnt["s"])]
            bfreeP = {0: None, 1: None, 2: None, 3: None}
            obtok = {}
            first = True
            for od in range(8):
                for tsl in range(8):
                    bi = tsl % 4
                    bank = PS[bi]
                    for c in range(4):
                        wl = []
                        if c == 0:
                            wl = list(mload) + [x for x in (bfreeP[bi],) if x]
                            if first:
                                wl += yready
                        p.mm(bank[:, :], Wp[:, 1024 * c + 128 * od:
                                            1024 * c + 128 * od + 128],
                             Y[:, 4096 * c + 512 * tsl:4096 * c + 512 * tsl + 512],
                             c == 0, c == 3, waits=wl, fin=(c == 3))
                    first = False
                    mmf = ("t", p.cnt["t"])
                    oi = tsl % 2
                    ob = EB[0:128, 512 * oi:512 * oi + 512]
                    gg2 = tsl // 4
                    lc = 512 * (tsl % 4)
                    wl_ev = [mmf] + [x for x in (obtok.get(oi),) if x]
                    if layer == 0:
                        ev = p.act(ob, bank[:, :], AF.Identity,
                                   bias=BIA[0:128, 8 + od:9 + od], waits=wl_ev)
                    else:
                        ev = p.scopy(ob, bank[:, :], waits=wl_ev)
                    bfreeP[bi] = ev
                    obtok[oi] = p.dma("ds",
                                      bass.AP(dst, gg2 * DM * OWN + (128 * od) * OWN + lc,
                                              [[OWN, 128], [1, 512]]),
                                      ob, waits=[ev])
            p.op("g", lambda e: e.collective_compute(
                "ReduceScatter", AL.add,
                replica_groups=[[0, 1], [2, 3], [4, 5], [6, 7]],
                ins=[dst.ap().opt()], outs=[cout.ap().opt()]),
                waits=p.bar("cc"), incs=(("cc", 1),))
            return ("cc", p.cnt["cc"])

        def gate_up_h(w_gate_d, w_up_d, nhh, tcols, tw, scale_g, scale_u,
                      e_idx=None):
            cap = 20480 // tw

            def slot(i):
                if i < 0:
                    return WINB[0:128, 2 * tw:3 * tw]
                if i < cap:
                    return M2[0:128, tw * i:tw * i + tw]
                return WINB[0:128, tw * (i - cap):tw * (i - cap) + tw]

            NC_ = nhh * 128
            wstride = int(w_gate_d.shape[-1])

            nh2 = nhh // 2
            NH2 = nh2 * 128

            def wslab(src_d, half):
                toks = []
                tw_ = p.bar("dg")
                for c in range(8):
                    base = (e_idx * DM * MH if e_idx is not None else 0)
                    toks.append(p.dma("dg",
                                      M1[0:128, NC_ * c + NH2 * half:
                                         NC_ * c + NH2 * half + NH2],
                                      bass.AP(src_d, base + (128 * c) * wstride
                                              + NH2 * half,
                                              [[wstride, 128], [1, NH2]]),
                                      waits=tw_))
                return toks

            nts = tw // 512
            nbank = 4
            bfree = {b: None for b in range(nbank)}
            mwh = {0: wslab(w_gate_d, 0), 1: wslab(w_gate_d, 1)}
            m3h = {}
            for hh2 in range(nhh):
                if hh2 == nh2:
                    m3h[0] = wslab(w_up_d, 0)
                for tsub in range(nts):
                    bi = (hh2 * nts + tsub) % nbank
                    bank = PS[bi]
                    for c in range(8):
                        wl = []
                        if c == 0:
                            wl = list(mwh[min(1, hh2 // nh2)])
                            if bfree[bi] is not None:
                                wl.append(bfree[bi])
                        p.mm(bank[:, :],
                             M1[0:128, NC_ * c + 128 * hh2:NC_ * c + 128 * hh2 + 128],
                             XN[0:128, 2048 * c + tcols + 512 * tsub:
                                2048 * c + tcols + 512 * tsub + 512],
                             c == 0, c == 7, waits=wl, fin=(c == 7))
                    mmt = ("t", p.cnt["t"])
                    bfree[bi] = p.act(slot(hh2)[:, 512 * tsub:512 * tsub + 512],
                                      bank[:, :], AF.Silu, scale=scale_g,
                                      waits=[mmt])
            gate_done = list(bfree.values())
            m3h[1] = wslab(w_up_d, 1)
            bfree = {b: None for b in range(nbank)}
            htok = {}
            uetok = {}
            for hh2 in range(nhh):
                for tsub in range(nts):
                    bi = (hh2 * nts + tsub) % nbank
                    bank = PS[bi]
                    for c in range(8):
                        wl = []
                        if c == 0:
                            wl = (list(m3h[min(1, hh2 // nh2)])
                                  + [x for x in (bfree[bi],) if x])
                            if hh2 == 0:
                                wl += [x for x in gate_done if x]
                        p.mm(bank[:, :],
                             M1[0:128, NC_ * c + 128 * hh2:NC_ * c + 128 * hh2 + 128],
                             XN[0:128, 2048 * c + tcols + 512 * tsub:
                                2048 * c + tcols + 512 * tsub + 512],
                             c == 0, c == 7, waits=wl, fin=(c == 7))
                    mmt = ("t", p.cnt["t"])
                    ui = (hh2 * nts + tsub) % 2
                    ue = EB[0:128, 512 * ui:512 * ui + 512]
                    ev = p.act(ue, bank[:, :], AF.Copy, scale=scale_u,
                               waits=[mmt] + [x for x in (uetok.get(ui),) if x])
                    bfree[bi] = ev
                    tk = p.tt(slot(hh2 - 1)[:, 512 * tsub:512 * tsub + 512],
                              ue, slot(hh2)[:, 512 * tsub:512 * tsub + 512],
                              AL.mult, waits=[ev])
                    htok[(hh2, tsub)] = tk
                    uetok[ui] = tk
            return slot, htok

        if L0:
            layernorm(x_fm, T, None, xn0_d)
            layernorm(x_own, OWN, XN, None)
            attention(0)
            cc_l0 = proj_partial(0)
            sG = 1.0 / 127.0
            sU = sG * sG
            for tb in range(4):
                slot, htok = gate_up_h(wg_t, wu_t, 24, 512 * tb, 512, sG, sU)
                msdh = {0: [], 1: []}
                tw_ = p.bar("dg")
                for c in range(24):
                    msdh[c // 12].append(
                        p.dma("dg", M1[0:128, 1024 * c:1024 * c + 1024],
                              bass.AP(wd_t, (128 * c) * DM, [[DM, 128], [1, 1024]]),
                              waits=tw_))
                hall = [t for t in htok.values()]
                bfree2 = {0: None, 1: None, 2: None, 3: None}
                xbtok, rbtok, t1tok, t2tok = {}, {}, {}, {}
                for od in range(8):
                    bi = od % 4
                    bank = PS[bi]
                    for c in range(24):
                        wl = []
                        if c == 0:
                            wl = list(msdh[0]) + [x for x in (bfree2[bi],) if x]
                            if od == 0:
                                wl += hall
                        elif c == 12 and od == 0:
                            wl = list(msdh[1])
                        p.mm(bank[:, :],
                             M1[0:128, 1024 * c + 128 * od:1024 * c + 128 * od + 128],
                             slot(c - 1)[:, 0:512], c == 0, c == 23,
                             waits=wl, fin=(c == 23))
                    mmf = ("t", p.cnt["t"])
                    pi = od % 2
                    xb = SCR[0:128, 1024 + 512 * pi:1536 + 512 * pi]
                    mx_ = p.dma("ds", xb,
                                bass.AP(x_own, (128 * od) * OWN + 512 * tb,
                                        [[OWN, 128], [1, 512]]),
                                waits=[x for x in (xbtok.get(pi),) if x])
                    rb = EB[0:128, 1024 + 512 * pi:1536 + 512 * pi]
                    mr = p.dma("ds", rb,
                               bass.AP(pp_out, (128 * od) * OWN + 512 * tb,
                                       [[OWN, 128], [1, 512]]),
                               waits=[cc_l0] + [x for x in (rbtok.get(pi),) if x])
                    t1 = SCR[0:128, 512 * pi:512 * pi + 512]
                    tk1 = p.tt(t1, bank[:, :], rb, AL.add,
                               waits=[mmf, mr] + [x for x in (t1tok.get(pi),) if x])
                    bfree2[bi] = tk1
                    rbtok[pi] = tk1
                    t2 = SCR[0:128, 2048 + 512 * pi:2560 + 512 * pi]
                    tk2 = p.tt(t2, t1, xb, AL.add,
                               waits=[mx_] + [x for x in (t2tok.get(pi),) if x])
                    xbtok[pi] = tk2
                    t1tok[pi] = tk2
                    dd = x1_out if mode == "L0" else x1_d
                    t2tok[pi] = p.dma("ds", bass.AP(dd, (128 * od) * OWN + 512 * tb,
                                                    [[OWN, 128], [1, 512]]),
                                      t2, waits=[tk2])

        if L1:
            x1src = x1_in if mode == "L1" else x1_d
            layernorm(x1src, OWN, XN, xn1_in)
            p.op("g", lambda e: e.collective_compute(
                "AllGather", AL.bypass,
                replica_groups=[[0, 1], [2, 3], [4, 5], [6, 7]],
                ins=[xn1_in.ap().opt()], outs=[xn1_g.ap().opt()]),
                waits=p.bar("cc"), incs=(("cc", 1),))
            if "l1attn" not in skip:
                attention(1)
            cc_l1 = proj_partial(1)
            mwg = p.dma("dg", ROWS[0:128, 1152:1216], wg8[:, :], waits=p.bar("dg"))
            for tsl in range(4):
                for c in range(8):
                    p.mm(PS[tsl % 4][0:8, :], ROWS[0:128, 1152 + 8 * c:1160 + 8 * c],
                         XN[0:128, 2048 * c + 512 * tsl:2048 * c + 512 * tsl + 512],
                         c == 0, c == 7,
                         waits=([mwg, ("s", p.cnt["s"]), ("v", p.cnt["v"])]
                                if c == 0 else []),
                         fin=(c == 7))
                p.scopy(SCR[0:8, 512 * tsl:512 * tsl + 512], PS[tsl % 4][0:8, :],
                        waits=p.bar("s"))
            lg_done = ("s", p.cnt["s"])
            for tc in range(16):
                lgb = ROWS[0:8, 1280:1408]
                p.vcopy(lgb, SCR[0:8, 128 * tc:128 * tc + 128],
                        waits=[lg_done] + p.bar("v"))
                p.tr(PSB[0:128, 0:8], lgb, IDT[0:8, 0:8],
                     waits=p.bar("t"))
                ltm = ROWS[0:128, 1216:1224]
                p.vcopy(ltm, PSB[0:128, 0:8], waits=p.bar("v"))
                TRE = ROWS[0:128, 1408:1424]
                TREF = SCR[0:128, 2048:2072]

                def tree8(dst_col, tile, alu):
                    p.tt(TRE[0:128, 0:4], tile[0:128, 0:4], tile[0:128, 4:8], alu,
                         waits=p.bar("v"))
                    p.tt(TRE[0:128, 4:6], TRE[0:128, 0:2], TRE[0:128, 2:4], alu,
                         waits=p.bar("v"))
                    p.tt(dst_col, TRE[0:128, 4:5], TRE[0:128, 5:6], alu,
                         waits=p.bar("v"))

                m0 = TREF[0:128, 0:1]
                tree8(m0, ltm, AL.max)
                nm = TREF[0:128, 1:2]
                p.ts(nm, m0, -1.0, None, AL.mult, waits=p.bar("v"))
                e8 = ROWS[0:128, 1224:1232]
                p.act(e8, ltm, AF.Exp, bias=nm, waits=p.bar("s"))
                s8 = TREF[0:128, 2:3]
                tree8(s8, e8, AL.add)
                ri = TREF[0:128, 3:4]
                p.recip_pos(ri, s8, TREF[0:128, 4:5], waits=p.bar("s"))
                p8 = ROWS[0:128, 1232:1240]
                p.ts(p8, e8, ri, None, AL.mult, waits=p.bar("v"))
                m1 = TREF[0:128, 5:6]
                tree8(m1, p8, AL.max)
                eq = TREF[0:128, 6:14]
                p.ts(eq, p8, m1, None, AL.is_equal, waits=p.bar("v"))
                pm = ROWS[0:128, 1248:1256]
                p.stt(pm, eq, -2.0, p8, AL.mult, AL.add, waits=p.bar("v"))
                m2 = TREF[0:128, 14:15]
                tree8(m2, pm, AL.max)
                keep = ROWS[0:128, 1256:1264]
                p.ts(keep, p8, m2, None, AL.is_ge, waits=p.bar("v"))
                cwu = ROWS[0:128, 1264:1272]
                p.tt(cwu, p8, keep, AL.mult, waits=p.bar("v"))
                s2 = TREF[0:128, 15:16]
                tree8(s2, cwu, AL.add)
                ri2 = TREF[0:128, 16:17]
                p.recip_pos(ri2, s2, TREF[0:128, 4:5], waits=p.bar("s"))
                cwt = ROWS[0:128, 1272:1280]
                p.ts(cwt, cwu, ri2, None, AL.mult, waits=p.bar("v"))
                if dbg_out is not None and tc == 0:
                    for k, tile in enumerate([ltm, e8, p8, pm, keep, cwu, cwt]):
                        p.vcopy(PBUF[0:128, 8 * k:8 * k + 8], tile, waits=p.bar("v"))
                    p.vcopy(PBUF[0:128, 56:64], eq[0:128, 0:8], waits=p.bar("v"))
                    for k in range(8):
                        p.vcopy(PBUF[0:128, 64 + k:65 + k],
                                TREF[0:128, k:k + 1], waits=p.bar("v"))
                p.tr(PSB[0:8, 128:256], cwt, IDT[:, :], waits=p.bar("t"))
                p.vcopy(SMB[0:8, 128 * tc:128 * tc + 128], PSB[0:8, 128:256],
                        waits=p.bar("v"))
            cw_done = ("v", p.cnt["v"])
            for tb in range(2):
                for e_i in ([] if "moe" in skip else range(NE)):
                    slot, htok = gate_up_h(w1t, w3t, 22, 1024 * tb, 1024,
                                           1.0, 1.0, e_idx=e_i)
                    msdh = {0: [], 1: []}
                    tw_ = p.bar("dg")
                    for c in range(22):
                        msdh[c // 11].append(
                            p.dma("dg", M1[0:128, 1024 * c:1024 * c + 1024],
                                  bass.AP(w2t, e_i * MH * DM + (128 * c) * DM,
                                          [[DM, 128], [1, 1024]]),
                                  waits=tw_))
                    if "cw" in skip:
                        p.vmemset(EB[0:128, 1024:1536], 0.125, waits=p.bar("v"))
                        p.vmemset(EB[0:128, 1536:2048], 0.125, waits=p.bar("v"))
                        ecw = ("v", p.cnt["v"])
                    else:
                        mcw = p.dma("dg", ROWS[0:1, 128:1152],
                                    bass.AP(SMB, e_i * 2048 + 1024 * tb,
                                            [[2048, 1], [1, 1024]]),
                                    waits=[cw_done] + p.bar("dg"))
                        p.mm(PS[2][:, :], ONES1, ROWS[0:1, 128:640], True, True,
                             waits=[mcw, ("s", p.cnt["s"])], fin=True)
                        p.mm(PS[3][:, :], ONES1, ROWS[0:1, 640:1152], True, True,
                             waits=[mcw], fin=True)
                        p.scopy(EB[0:128, 1024:1536], PS[2][:, :], waits=p.bar("s"))
                        p.scopy(EB[0:128, 1536:2048], PS[3][:, :], waits=p.bar("s"))
                        ecw = ("s", p.cnt["s"])
                    hall = list(htok.values())
                    bfree3 = {0: None, 1: None, 2: None, 3: None}
                    first = True
                    first2 = True
                    for od in range(8):
                        for tsub in range(2):
                            bi = (od * 2 + tsub) % 4
                            bank = PS[bi]
                            for c in range(22):
                                wl = []
                                if c == 0:
                                    wl = list(msdh[0]) + [x for x in (bfree3[bi],) if x]
                                    if first:
                                        wl += hall + [ecw]
                                elif c == 11 and first2:
                                    wl = list(msdh[1])
                                p.mm(bank[:, :],
                                     M1[0:128, 1024 * c + 128 * od:
                                        1024 * c + 128 * od + 128],
                                     slot(c - 1)[:, 512 * tsub:512 * tsub + 512],
                                     c == 0, c == 21, waits=wl, fin=(c == 21))
                            first2 = False
                            first = False
                            mmf = ("t", p.cnt["t"])
                            cwb = (EB[0:128, 1024:1536] if tsub == 0
                                   else EB[0:128, 1536:2048])
                            ycol = 8192 * (e_i % 2) + 1024 * od + 512 * tsub
                            yprev = 8192 * ((e_i + 1) % 2) + 1024 * od + 512 * tsub
                            if e_i == 0:
                                bfree3[bi] = p.tt(Y[0:128, ycol:ycol + 512],
                                                  bank[:, :], cwb, AL.mult,
                                                  waits=[mmf, ecw])
                            else:
                                tmp = PBUF[0:128, 0:512]
                                bfree3[bi] = p.tt(tmp, bank[:, :], cwb, AL.mult,
                                                  waits=[mmf, ecw])
                                p.tt(Y[0:128, ycol:ycol + 512],
                                     Y[0:128, yprev:yprev + 512], tmp, AL.add,
                                     waits=[])
                ylast = 8192 * ((NE - 1) % 2)
                for od in range(8):
                    for tsub in range(2):
                        xb = SCR[0:128, 0:512]
                        mx_ = p.dma("ds", xb,
                                    bass.AP(x1src, (128 * od) * OWN + 1024 * tb
                                            + 512 * tsub,
                                            [[OWN, 128], [1, 512]]),
                                    waits=p.bar("ds"))
                        rb = EB[0:128, 0:512]
                        mr = p.dma("ds", rb,
                                   bass.AP(pp_out1, (128 * od) * OWN + 1024 * tb
                                           + 512 * tsub,
                                           [[OWN, 128], [1, 512]]),
                                   waits=[cc_l1] + p.bar("ds"))
                        t1 = SCR[0:128, 512:1024]
                        p.tt(t1, xb, rb, AL.add, waits=[mr, mx_, ("v", p.cnt["v"]),
                                                        ("t", p.cnt["t"])])
                        ycol = ylast + 1024 * od + 512 * tsub
                        t2 = SCR[0:128, 2048:2560]
                        p.tt(t2, t1, Y[0:128, ycol:ycol + 512], AL.add,
                             waits=p.bar("v"))
                        p.dma("ds", bass.AP(y_out, (128 * od) * OWN + 1024 * tb
                                            + 512 * tsub,
                                            [[OWN, 128], [1, 512]]),
                              t2, waits=p.bar("ds"))

        if dbg_out is not None:
            p.dma("dg", dbg2_out[:, 0:72], PBUF[0:128, 0:72], waits=p.bar("dg"))
            p.dma("dg", bass.AP(dbg_out, 0, [[2048, 8], [1, 2048]]),
                  SMB[0:8, 0:2048], waits=p.bar("dg"))
            p.dma("dg", bass.AP(dbg_out, 8 * 2048, [[2048, 8], [1, 2048]]),
                  SCR[0:8, 0:2048], waits=p.bar("dg"))
            p.dma("dg", bass.AP(dbg_out, 16 * 2048, [[2048, 8], [1, 2048]]),
                  XN[0:8, 0:2048], waits=p.bar("dg"))
        p.op("ds", lambda e: e.nop(), waits=p.bar(), incs=())

        @block.tensor
        def _(t):
            p.replay_eng = None
            for e, fn in p.sched:
                if e == "t":
                    fn(t)

        @block.vector
        def _(v):
            for e, fn in p.sched:
                if e == "v":
                    fn(v)

        @block.scalar
        def _(s):
            for e, fn in p.sched:
                if e == "s":
                    fn(s)

        @block.gpsimd
        def _(g):
            for e, fn in p.sched:
                if e in ("dg", "g", "cc"):
                    fn(g)

        @block.sync
        def _(s):
            for e, fn in p.sched:
                if e == "ds":
                    fn(s)

    nc._input_names = input_names
    return nc



def _run(mode, in_maps):
    nc = build_program(mode)
    names = set(nc._input_names)
    in_maps = [{k: v for k, v in m.items() if k in names} for m in in_maps]
    return run_bass_kernel_spmd(nc, in_maps, core_ids=list(range(8)))


def kernel(**inputs):
    in_maps = [stage_core(inputs, c) for c in range(8)]
    res = _run("FUSED", in_maps)
    out = np.empty((4, T, DM), np.float32)
    for c in range(8):
        b, h = c // 2, c % 2
        y = np.asarray(res.results[c]["y_out"], np.float32)
        out[b, 2048 * h:2048 * h + 2048, :] = y.T
    return out


# version 09 (final)
# speedup vs baseline: 1.6009x; 1.0439x over previous
import math
from contextlib import ExitStack

import numpy as np

import concourse.bass as bass
import concourse.mybir as mybir
from concourse.bass_utils import run_bass_kernel_spmd
from concourse.alu_op_type import AluOpType as AL

F32 = mybir.dt.float32
BF16 = mybir.dt.bfloat16
I32 = mybir.dt.int32
AF = mybir.ActivationFunctionType
AX = mybir.AxisListType

T = 4096
DM = 1024
FH = 3072
MH = 2816
NE = 8
FLEN = 8448
FPAD = 4223
WINW = 4608
WINBASE = 3584
SHIFT = 16.0
QS = 0.125
OWN = 2048
EPS = 1e-5
NKC = 32

KCMIN = [max(0, (3457 - 512 * g + 127) // 128) for g in range(8)]



def _bucket_np():
    d = np.arange(T)
    rp = np.maximum(d, 1).astype(np.float32)
    large = 16 + (np.log(rp / np.float32(16.0))
                  / np.float32(math.log(8.0)) * 16).astype(np.int32)
    large = np.minimum(large, 31)
    return np.where(d < 16, d, large)


def stage_core(inputs, core):
    b, h = core // 2, core % 2
    x = np.ascontiguousarray(np.asarray(inputs["L0_attn_x"][b]).T.astype(np.float32, copy=False))
    d = {}
    d["x_fm"] = x
    d["x_own"] = np.ascontiguousarray(x[:, 2048 * h:2048 * h + 2048])

    Wa = np.asarray(inputs["L0_attn_W_attn"])
    ba = np.asarray(inputs["L0_attn_b_attn"])
    l0w = np.empty((4, 1024, 384), np.float32)
    l0b = np.empty((4, 128, 3), np.float32)
    for p in range(4):
        P = 4 * h + p
        l0w[p, :, 0:128] = Wa[:, 128 * P:128 * P + 128]
        l0w[p, :, 128:256] = Wa[:, 1024 + 128 * P:1024 + 128 * P + 128]
        l0w[p, :, 256:384] = Wa[:, 2048 + 128 * P:2048 + 128 * P + 128]
        l0b[p, :, 0] = ba[128 * P:128 * P + 128]
        l0b[p, :, 1] = ba[1024 + 128 * P:1024 + 128 * P + 128]
        l0b[p, :, 2] = ba[2048 + 128 * P:2048 + 128 * P + 128]
    d["l0w"] = l0w
    d["l0b"] = l0b
    d["l0proj"] = np.ascontiguousarray(
        np.asarray(inputs["L0_attn_W_proj"])[512 * h:512 * h + 512, :])
    pb = np.asarray(inputs["L0_attn_b_proj"]).reshape(8, 128).T
    d["l0pb"] = np.ascontiguousarray(pb) if h == 0 else np.zeros((128, 8), np.float32)

    d["table_own"] = np.ascontiguousarray(
        np.asarray(inputs["L0_attn_bias_table"])[:, 8 * h:8 * h + 8])
    bk = _bucket_np()
    oh = np.zeros((32, T), np.float32)
    oh[bk, np.arange(T)] = 1.0
    d["oh0"] = oh

    d["wg_t"] = np.ascontiguousarray(np.asarray(inputs["L0_ffn_w_gate"]).T)
    d["wu_t"] = np.ascontiguousarray(np.asarray(inputs["L0_ffn_w_up"]).T)
    d["wd_t"] = np.ascontiguousarray(np.asarray(inputs["L0_ffn_w_down"]).T)

    l1w = np.empty((4, 1024, 384), np.float32)
    for p in range(4):
        P = 4 * h + p
        l1w[p, :, 0:128] = np.asarray(inputs["L1_attn_Wq"])[128 * P:128 * P + 128, :].T
        l1w[p, :, 128:256] = np.asarray(inputs["L1_attn_Wk"])[128 * P:128 * P + 128, :].T
        l1w[p, :, 256:384] = np.asarray(inputs["L1_attn_Wv"])[128 * P:128 * P + 128, :].T
    d["l1w"] = l1w
    d["l1proj"] = np.ascontiguousarray(
        np.asarray(inputs["L1_attn_Wo"]).T[512 * h:512 * h + 512, :])
    d["l1bias_own"] = np.ascontiguousarray(
        np.asarray(inputs["L1_attn_bias"])[8 * h:8 * h + 8, :])
    d["tau_own"] = np.ascontiguousarray(
        np.asarray(inputs["L1_attn_tau"])[8 * h:8 * h + 8].reshape(1, 8))
    d["riot"] = (1.0 / np.arange(1, T + 1, dtype=np.float64)).astype(np.float32).reshape(1, T)

    Wg = np.asarray(inputs["L1_ffn_Wg"])
    d["wg8"] = np.ascontiguousarray(
        Wg.T.reshape(8, 128, 8).transpose(1, 0, 2).reshape(128, 64))
    d["w1t"] = np.ascontiguousarray(np.transpose(np.asarray(inputs["L1_ffn_W1"]), (0, 2, 1)))
    d["w3t"] = np.ascontiguousarray(np.transpose(np.asarray(inputs["L1_ffn_W3"]), (0, 2, 1)))
    d["w2t"] = np.ascontiguousarray(np.transpose(np.asarray(inputs["L1_ffn_W2"]), (0, 2, 1)))
    d["ident"] = np.eye(128, dtype=np.float32)
    return d



class PB:
    ENGS = ("t", "v", "s", "g", "cc")
    NSLOT = 12

    def __init__(self, nc):
        self.nc = nc
        self.sched = []
        self.cnt = {}
        self.hdl = {}
        self.lastwait = {}
        self.dmaq = {}

    def sem(self, ctx, name):
        self.hdl[name] = ctx.enter_context(self.nc.semaphore(name))
        self.cnt[name] = 0

    def dsem(self, ctx, q):
        self.dmaq[q] = 0
        for i in range(self.NSLOT):
            n = f"{q}{i}"
            self.hdl[n] = ctx.enter_context(self.nc.semaphore(n))
            self.cnt[n] = 0

    def op(self, eng, build_fn, waits=(), incs=()):
        if eng in ("v", "s", "g"):
            waits = list(waits) + [(eng, self.cnt[eng])]
        w = []
        for sname, v in waits:
            assert v <= self.cnt[sname], (sname, v, self.cnt[sname])
            if v <= 0:
                continue
            key = (eng, sname)
            if self.lastwait.get(key, -1) >= v:
                continue
            self.lastwait[key] = v
            w.append((self.hdl[sname], v))
        iz = []
        for sname, amt in incs:
            self.cnt[sname] += amt
            iz.append((self.hdl[sname], amt))

        def fn(e, w=w, iz=iz, build_fn=build_fn):
            for hh, v in w:
                e.wait_ge(hh, v)
            ins = build_fn(e)
            for hh, amt in iz:
                ins.then_inc(hh, amt)
        self.sched.append((eng, fn))

    def bar(self, *exclude):
        ex = set(exclude)
        w = [(n, self.cnt[n]) for n in self.ENGS if n not in ex and self.cnt[n] > 0]
        for q in self.dmaq:
            if q in ex:
                continue
            for i in range(self.NSLOT):
                n = f"{q}{i}"
                if self.cnt[n] > 0:
                    w.append((n, self.cnt[n]))
        return w

    def dma(self, q, out, in_, waits=()):
        j = self.dmaq[q]
        self.dmaq[q] = j + 1
        n = f"{q}{j % self.NSLOT}"
        w = list(waits)
        if self.cnt[n] > 0:
            w.append((n, self.cnt[n]))
        self.op(q, lambda e: e.dma_start(out=out, in_=in_), w, ((n, 16),))
        return (n, self.cnt[n])

    def mm(self, out, lhsT, rhs, start, stop, tp=None, waits=(), fin=False):
        incs = (("t", 1),) if fin else ()
        self.op("t", lambda e: e.matmul(out, lhsT, rhs, start=start, stop=stop,
                                        tile_position=tp), waits, incs)
        return ("t", self.cnt["t"])

    def tr(self, out, in_, ident, waits=()):
        self.op("t", lambda e: e.transpose(out, in_, ident), waits, (("t", 1),))
        return ("t", self.cnt["t"])

    def act(self, out, in_, func, bias=0.0, scale=1.0, waits=()):
        self.op("s", lambda e: e.activation(out, in_, func, bias=bias, scale=scale),
                waits, (("s", 1),))
        return ("s", self.cnt["s"])

    def scopy(self, out, in_, waits=()):
        self.op("s", lambda e: e.copy(out, in_), waits, (("s", 1),))
        return ("s", self.cnt["s"])

    def tt(self, out, in0, in1, alu, waits=()):
        self.op("v", lambda e: e.tensor_tensor(out, in0, in1, alu), waits, (("v", 1),))
        return ("v", self.cnt["v"])

    def ts(self, out, in0, s1, s2, op0, op1=None, waits=()):
        if op1 is None:
            self.op("v", lambda e: e.tensor_scalar(out, in0, s1, s2, op0),
                    waits, (("v", 1),))
        else:
            self.op("v", lambda e: e.tensor_scalar(out, in0, s1, s2, op0, op1),
                    waits, (("v", 1),))
        return ("v", self.cnt["v"])

    def stt(self, out, in0, scalar, in1, op0, op1, waits=()):
        self.op("v", lambda e: e.scalar_tensor_tensor(out, in0, scalar, in1, op0, op1),
                waits, (("v", 1),))
        return ("v", self.cnt["v"])

    def vred(self, out, in_, alu, negate=False, waits=()):
        self.op("v", lambda e: e.tensor_reduce(out, in_, AX.X, alu, negate=negate),
                waits, (("v", 1),))
        return ("v", self.cnt["v"])

    def vcopy(self, out, in_, waits=()):
        self.op("v", lambda e: e.tensor_copy(out, in_), waits, (("v", 1),))
        return ("v", self.cnt["v"])

    def vrecip(self, out, in_, waits=()):
        self.op("v", lambda e: e.reciprocal(out, in_), waits, (("v", 1),))
        return ("v", self.cnt["v"])

    def recip_pos(self, out, in_, tmp, waits=()):
        self.act_ln = None
        self.op("s", lambda e: e.activation(tmp, in_, AF.Ln), waits, (("s", 1),))
        self.op("s", lambda e: e.activation(out, tmp, AF.Exp, scale=-1.0), (),
                (("s", 1),))
        return ("s", self.cnt["s"])

    def vmemset(self, ap, val, waits=()):
        self.op("v", lambda e: e.memset(ap, val), waits, (("v", 1),))
        return ("v", self.cnt["v"])



def build_program(mode, race_detect=False, skip=()):
    nc = bass.Bass("TRN2", target_bir_lowering=False, num_devices=8,
                   detect_race_conditions=race_detect)
    L0 = mode in ("L0", "FUSED")
    L1 = mode in ("L1", "FUSED")

    input_names = []

    def din(name, shape, dt):
        input_names.append(name)
        return nc.dram_tensor(name, shape, dt, kind="ExternalInput")

    ident_d = din("ident", [128, 128], F32)
    riot_d = din("riot", [1, T], F32)
    if L0:
        x_fm = din("x_fm", [DM, T], F32)
        x_own = din("x_own", [DM, OWN], F32)
        l0w = din("l0w", [4, DM, 384], F32)
        l0b = din("l0b", [4, 128, 3], F32)
        l0proj = din("l0proj", [512, DM], F32)
        l0pb = din("l0pb", [128, 8], F32)
        table_own = din("table_own", [32, 8], F32)
        oh0 = din("oh0", [32, T], F32)
        wg_t = din("wg_t", [DM, FH], I32)
        wu_t = din("wu_t", [DM, FH], I32)
        wd_t = din("wd_t", [FH, DM], I32)
    if L1:
        l1w = din("l1w", [4, DM, 384], F32)
        l1proj = din("l1proj", [512, DM], F32)
        l1bias_own = din("l1bias_own", [8, 2048], F32)
        tau_own = din("tau_own", [1, 8], F32)
        wg8 = din("wg8", [128, 64], F32)
        w1t = din("w1t", [NE, DM, MH], F32)
        w3t = din("w3t", [NE, DM, MH], F32)
        w2t = din("w2t", [NE, MH, DM], F32)
        if mode == "L1":
            x1_in = din("x1_in", [DM, OWN], F32)

    if mode == "L0":
        x1_out = nc.dram_tensor("x1_out", [DM, OWN], F32, kind="ExternalOutput")
    else:
        y_out = nc.dram_tensor("y_out", [DM, OWN], F32, kind="ExternalOutput")
    dbg_out = (nc.dram_tensor("dbg_out", [24, 2048], F32, kind="ExternalOutput")
               if "dbg" in skip else None)
    dbg2_out = (nc.dram_tensor("dbg2_out", [128, 128], F32, kind="ExternalOutput")
                if "dbg" in skip else None)

    F0d = nc.dram_tensor("F0d", [8, FLEN], BF16) if L0 else None
    F1d = nc.dram_tensor("F1d", [8, FLEN], BF16) if L1 else None
    xn0_d = nc.dram_tensor("xn0_d", [DM, T], BF16) if L0 else None
    x1_d = nc.dram_tensor("x1_d", [DM, OWN], F32)
    pp_in = nc.dram_tensor("pp_in", [2, DM, OWN], BF16)
    pp_out = nc.dram_tensor("pp_out", [DM, OWN], BF16)
    pp_in1 = nc.dram_tensor("pp_in1", [2, DM, OWN], BF16)
    pp_out1 = nc.dram_tensor("pp_out1", [DM, OWN], BF16)
    if L1:
        xn1_in = nc.dram_tensor("xn1_in", [DM, OWN], BF16)
        xn1_g = nc.dram_tensor("xn1_g", [2, DM, OWN], BF16)

    p = PB(nc)
    with ExitStack() as ctx:
        en = ctx.enter_context
        XN = en(nc.sbuf_tensor("XN", [128, 8 * 2048], BF16))
        Y = en(nc.sbuf_tensor("Y", [128, 4 * 4096], BF16))
        M1 = en(nc.sbuf_tensor("M1", [128, 24576], BF16))
        M2 = en(nc.sbuf_tensor("M2", [128, 20480], BF16))
        WINB = en(nc.sbuf_tensor("WINB", [128, WINW], BF16))
        QP = en(nc.sbuf_tensor("QP", [128, T], BF16))
        EB = en(nc.sbuf_tensor("EB", [128, 2048], BF16))
        PBUF = en(nc.sbuf_tensor("PBUF", [128, 1024], BF16))
        CWB = en(nc.sbuf_tensor("CWB", [128, 1024], BF16))
        STAT = en(nc.sbuf_tensor("STAT", [128, 2048], F32))
        SCR = en(nc.sbuf_tensor("SCR", [128, 3072], F32))
        SMB = en(nc.sbuf_tensor("SMB", [128, 2048], BF16))
        ROWS = en(nc.sbuf_tensor("ROWS", [128, 1664], BF16))
        IDT = en(nc.sbuf_tensor("IDT", [128, 128], BF16))
        BIA = en(nc.sbuf_tensor("BIA", [128, 32], F32))
        QA = en(nc.psum_tensor("QA", [128, 1024], F32))
        QB = en(nc.psum_tensor("QB", [128, 1024], F32))
        QC = en(nc.psum_tensor("QC", [128, 1024], F32))
        QD = en(nc.psum_tensor("QD", [128, 512], F32))
        PSB = en(nc.psum_tensor("PSB", [128, 1024], BF16))
        PS = [QA[0:128, 0:512], QA[0:128, 512:1024],
              QB[0:128, 0:512], QB[0:128, 512:1024],
              QC[0:128, 0:512], QC[0:128, 512:1024],
              QD[0:128, 0:512]]

        for s in PB.ENGS:
            p.sem(ctx, s)
        p.dsem(ctx, "dg")
        p.dsem(ctx, "ds")
        block = en(nc.Block())

        KR0, VR0, PTA0 = 16384, 20480, 0
        VT0, PTB0 = 16384, 0

        ONES1 = ROWS[0:1, 0:128]
        ONES128 = ROWS[0:128, 0:1]
        ONESF = BIA[0:128, 26:27]

        m_id = p.dma("dg", IDT[:, :], ident_d[:, :])
        p.vmemset(ROWS[0:128, 0:128], 1.0)
        p.vmemset(ONESF, 1.0)
        p.vmemset(BIA[0:128, 24:25], -SHIFT)
        p.vmemset(BIA[0:128, 25:26], EPS)

        def build_F(Fd, is_l0):
            zt = SMB[0:16, 256:768]
            p.vmemset(zt, 0.0)
            for i in range(9):
                w = min(512, FPAD - 512 * i)
                p.dma("ds", bass.AP(Fd, 512 * i, [[FLEN, 8], [1, w]]),
                      zt[0:8, 0:w], waits=p.bar("ds"))
            if is_l0:
                tb = SMB[0:32, 0:16]
                p.dma("dg", tb[0:32, 0:8], table_own[:, :])
                ohc = SMB[0:32, 768:1280]
                for i in range(8):
                    p.dma("dg", ohc[:, :], oh0[:, 512 * i:512 * i + 512],
                          waits=p.bar("dg"))
                    bank = PS[0]
                    mmf = p.mm(bank[0:8, :], tb[0:32, 0:8], ohc[:, :], True, True,
                               waits=p.bar("t"), fin=True)
                    ev = p.act(SMB[0:8, 1280:1792], bank[0:8, :], AF.Exp,
                               bias=BIA[0:8, 24:25], waits=p.bar("s"))
                    p.dma("ds", bass.AP(Fd, FPAD + 512 * i, [[FLEN, 8], [1, 512]]),
                          SMB[0:8, 1280:1792], waits=[ev])
                p.dma("ds", bass.AP(Fd, FPAD + 4096, [[FLEN, 8], [1, FLEN - FPAD - 4096]]),
                      zt[0:8, 0:FLEN - FPAD - 4096], waits=p.bar("ds"))
            else:
                bc = SMB[0:8, 768:1280]
                for i in range(4):
                    p.dma("ds", SCR[0:8, 2048:2560],
                          l1bias_own[:, 512 * i:512 * i + 512], waits=p.bar("ds"))
                    ev = p.act(bc, SCR[0:8, 2048:2560], AF.Exp,
                               bias=BIA[0:8, 24:25], waits=p.bar("s"))
                    p.dma("ds", bass.AP(Fd, FPAD + 512 * i, [[FLEN, 8], [1, 512]]),
                          bc, waits=[ev])
                et = SMB[0:16, 1280:1792]
                p.vmemset(et, math.exp(-SHIFT), waits=p.bar("v"))
                off = FPAD + 2048
                rem = FLEN - off
                for i in range((rem + 511) // 512):
                    w = min(512, rem - 512 * i)
                    p.dma("ds", bass.AP(Fd, off + 512 * i, [[FLEN, 8], [1, w]]),
                          et[0:8, 0:w], waits=p.bar("ds"))

        if L0:
            build_F(F0d, True)
        if L1:
            build_F(F1d, False)

        def layernorm(src_d, ntok, out_sbuf, out_dram):
            stride = int(src_d.shape[1])
            nts = ntok // 512
            for tsl in range(nts):
                c0 = 512 * tsl
                for dchunk in range(8):
                    xa = SCR[0:128, 2048:2560]
                    p.dma("ds", xa,
                          bass.AP(src_d, (128 * dchunk) * stride + c0,
                                  [[stride, 128], [1, 512]]),
                          waits=p.bar("ds"))
                    p.mm(PS[0][0:1, :], ONESF, xa, dchunk == 0, dchunk == 7,
                         waits=p.bar("t"), fin=(dchunk == 7))
                    p.act(SCR[0:128, 2560:3072], xa, AF.Square, waits=p.bar("s"))
                    p.mm(PS[1][0:1, :], ONESF, SCR[0:128, 2560:3072],
                         dchunk == 0, dchunk == 7, waits=p.bar("t"),
                         fin=True)
                sA = STAT[0:1, 0:512]
                sB = STAT[0:1, 512:1024]
                sC = STAT[0:1, 1024:1536]
                sD = STAT[0:1, 1536:2048]
                p.ts(sA, PS[0][0:1, :], 1.0 / DM, None, AL.mult, waits=p.bar("v"))
                p.ts(sB, PS[1][0:1, :], 1.0 / DM, None, AL.mult, waits=p.bar("v"))
                p.tt(sC, sA, sA, AL.mult, waits=p.bar("v"))
                p.tt(sD, sB, sC, AL.subtract, waits=p.bar("v"))
                p.act(sC, sD, AF.Ln, bias=BIA[0:1, 25:26], waits=p.bar("s"))
                p.act(sD, sC, AF.Exp, scale=-0.5, waits=p.bar("s"))
                p.tt(sC, sA, sD, AL.mult, waits=p.bar("v"))
                ab = ROWS[0:1, 128:640]
                bb = ROWS[0:1, 640:1152]
                p.vcopy(ab, sD, waits=p.bar("v"))
                p.vcopy(bb, sC, waits=p.bar("v"))
                p.mm(PS[2][0:128, :], ONES1, ab, True, True,
                     waits=p.bar("t"), fin=True)
                p.mm(PS[3][0:128, :], ONES1, bb, True, True,
                     waits=p.bar("t"), fin=True)
                p.scopy(SCR[0:128, 2048:2560], PS[2][:, :], waits=p.bar("s"))
                p.scopy(SCR[0:128, 2560:3072], PS[3][:, :], waits=p.bar("s"))
                for dchunk in range(8):
                    xb = SCR[0:128, 0:512]
                    p.dma("ds", xb,
                          bass.AP(src_d, (128 * dchunk) * stride + c0,
                                  [[stride, 128], [1, 512]]),
                          waits=p.bar("ds"))
                    t1 = STAT[0:128, 0:512]
                    p.tt(t1, xb, SCR[0:128, 2048:2560], AL.mult, waits=p.bar("v"))
                    if out_sbuf is not None:
                        dst = out_sbuf[0:128, 2048 * dchunk + c0:
                                       2048 * dchunk + c0 + 512]
                    else:
                        dst = EB[0:128, 0:512]
                    p.tt(dst, t1, SCR[0:128, 2560:3072], AL.subtract,
                         waits=p.bar("v"))
                    if out_dram is not None:
                        p.dma("ds",
                              bass.AP(out_dram, (128 * dchunk) * int(out_dram.shape[1]) + c0,
                                      [[int(out_dram.shape[1]), 128], [1, 512]]),
                              dst, waits=p.bar("ds"))

        def attention(layer):
            wsrc = l0w if layer == 0 else l1w
            Fd = F0d if layer == 0 else F1d
            if layer == 1:
                p.dma("ds", BIA[0:1, 16:24], tau_own[:, :], waits=p.bar("ds"))
            for pr in range(4):
                WPS = WINB[0:128, 0:3072]
                mws = []
                tw_ = [("t", p.cnt["t"]), ("s", p.cnt["s"]), ("v", p.cnt["v"])]
                for c in range(8):
                    mws.append(p.dma("dg", WPS[:, 384 * c:384 * c + 384],
                                     bass.AP(wsrc, pr * DM * 384 + (128 * c) * 384,
                                             [[384, 128], [1, 384]]),
                                     waits=tw_))
                if layer == 0:
                    mws.append(p.dma("dg", BIA[0:128, 0:3],
                                     bass.AP(l0b, pr * 128 * 3, [[3, 128], [1, 3]]),
                                     waits=p.bar("dg")))
                    p.ts(BIA[0:128, 4:5], BIA[0:128, 0:1], QS, None, AL.mult,
                         waits=mws)
                evtok = {}
                rbtok = {}
                mmv_tok = {}
                for tsl in range(8):
                    par = tsl % 2
                    B0 = PS[3 * par]
                    B1 = PS[3 * par + 1]
                    B2 = PS[3 * par + 2]
                    mm_q = mm_k = mm_v = None
                    for dchunk in range(8):
                        rb = EB[0:128, 512 * (dchunk % 4):512 * (dchunk % 4) + 512]
                        if layer == 0:
                            srcp = bass.AP(xn0_d, (128 * dchunk) * T + 512 * tsl,
                                           [[T, 128], [1, 512]])
                        else:
                            gg = tsl // 4
                            lc = 512 * (tsl % 4)
                            srcp = bass.AP(xn1_g, gg * DM * OWN + (128 * dchunk) * OWN + lc,
                                           [[OWN, 128], [1, 512]])
                        prev = mmv_tok.get((tsl * 8 + dchunk) - 4)
                        mrb = p.dma("dg", rb, srcp,
                                    waits=([x for x in (prev,) if x]
                                           + (p.bar("dg") if tsl == 0 and dchunk < 2
                                              else [])))
                        st_, sp = dchunk == 0, dchunk == 7
                        wl = [mrb]
                        if st_:
                            wl += mws
                            for role in range(3):
                                tok = evtok.get((par, role))
                                if tok:
                                    wl.append(tok)
                        mm_q = p.mm(B0[:, :], WPS[:, 384 * dchunk:384 * dchunk + 128],
                                    rb, st_, sp, waits=wl, fin=sp)
                        mm_k = p.mm(B1[:, :], WPS[:, 384 * dchunk + 128:384 * dchunk + 256],
                                    rb, st_, sp, waits=[], fin=sp)
                        mm_v = p.mm(B2[:, :], WPS[:, 384 * dchunk + 256:384 * dchunk + 384],
                                    rb, st_, sp, waits=[], fin=True)
                        mmv_tok[tsl * 8 + dchunk] = ("t", p.cnt["t"])
                    if layer == 0:
                        evtok[(par, 0)] = p.act(QP[:, 512 * tsl:512 * tsl + 512],
                                                B0[:, :], AF.Identity,
                                                bias=BIA[0:128, 4:5], scale=QS,
                                                waits=[mm_q])
                        evtok[(par, 1)] = p.ts(
                            bass.AP(M1, KR0 + (T - 1 - 512 * tsl),
                                    [[24576, 128], [-1, 512]]),
                            B1[:, :], BIA[0:128, 1:2], None, AL.add, waits=[mm_k])
                        evtok[(par, 2)] = p.ts(
                            bass.AP(M1, VR0 + (T - 1 - 512 * tsl),
                                    [[24576, 128], [-1, 512]]),
                            B2[:, :], BIA[0:128, 2:3], None, AL.add, waits=[mm_v])
                    else:
                        evtok[(par, 0)] = p.act(QP[:, 512 * tsl:512 * tsl + 512],
                                                B0[:, :], AF.Identity, scale=QS,
                                                waits=[mm_q])
                        evtok[(par, 1)] = p.vcopy(
                            bass.AP(M1, KR0 + (T - 1 - 512 * tsl),
                                    [[24576, 128], [-1, 512]]),
                            B1[:, :], waits=[mm_k])
                        evtok[(par, 2)] = p.vcopy(
                            bass.AP(M1, VR0 + (T - 1 - 512 * tsl),
                                    [[24576, 128], [-1, 512]]),
                            B2[:, :], waits=[mm_v])
                kvdone = [evtok[(q, r)] for q in range(2) for r in range(3)]
                trtok = {}
                cptok = {}
                for kc in range(NKC):
                    pdst = (PSB[0:128, 0:128] if kc % 2 == 0
                            else QD.ap().bitcast(BF16)[0:128, 0:128])
                    wl = [x for x in (cptok.get(kc - 2),) if x]
                    if kc <= 1:
                        wl = wl + kvdone
                    trtok[kc] = p.tr(pdst, M1[0:128, VR0 + 128 * kc:VR0 + 128 * kc + 128],
                                     IDT[:, :], waits=wl)
                    cptok[kc] = p.vcopy(M2[0:128, VT0 + 128 * kc:VT0 + 128 * kc + 128],
                                        pdst, waits=[trtok[kc]])
                vt_done = [cptok[NKC - 1], cptok[NKC - 2]]
                extok = {}
                multok = {}
                sgb_free = None
                pvb_free = None
                er_prev = None
                for hh in range(2):
                    head = 2 * pr + hh
                    rb0 = 64 * hh
                    tp_row = (rb0, 0)
                    tp_col = None if hh == 0 else (0, 64)
                    ob0 = 64 * hh
                    mwin = p.dma("ds", WINB[:, :],
                                 bass.AP(Fd, head * FLEN + WINBASE,
                                         [[1, 128], [1, WINW]]),
                                 waits=list(multok.values()) + kvdone)
                    sqbanks = [QA, QB]
                    npar = 2
                    for g in range(8):
                        q0 = 512 * g
                        kmin = KCMIN[g]
                        nk = NKC - kmin
                        npair = nk // 2
                        PVb = QC[0:128, 0:512]
                        SGb = QD[0:128, 0:512]
                        slab = M1 if hh == 0 else M2
                        slab0 = PTA0 if hh == 0 else PTB0
                        for j in range(npair):
                            kc = kmin + 2 * j
                            par = j % npar
                            sq = sqbanks[par]
                            rhs = QP[rb0:rb0 + 64, q0:q0 + 512]
                            wl = [mwin] + [x for x in (extok.get(par),) if x]
                            if g == 0 and j < npar and hh == 0:
                                wl += kvdone + vt_done
                            p.mm(sq[0:128, 0:512],
                                 M1[rb0:rb0 + 64, KR0 + 128 * kc:KR0 + 128 * kc + 128],
                                 rhs, True, True, tp=tp_row, waits=wl, fin=True)
                            mmS = p.mm(sq[0:128, 512:1024],
                                       M1[rb0:rb0 + 64,
                                          KR0 + 128 * kc + 128:KR0 + 128 * kc + 256],
                                       rhs, True, True, tp=tp_row, waits=[], fin=True)
                            ebuf = EB[0:128, 1024 * par:1024 * par + 1024]
                            ex = p.act(ebuf, sq[0:128, 0:1024], AF.Exp,
                                       waits=[mmS] + [x for x in (multok.get(par),) if x])
                            extok[par] = ex
                            u0 = q0 + 128 * kc - 3456
                            if layer == 0:
                                pt = slab[0:128, slab0 + 1024 * par:
                                          slab0 + 1024 * par + 1024]
                            else:
                                pt = slab[0:128, slab0 + 1024 * j:slab0 + 1024 * j + 1024]
                            win2 = bass.AP(WINB, u0, [[WINW, 128], [128, 2], [1, 512]])
                            mul = p.tt(pt, ebuf, win2, AL.mult, waits=[ex])
                            multok[par] = mul
                            p.mm(SGb[0:1, :], ONES128, pt[0:128, 0:512],
                                 j == 0, False,
                                 waits=[mul] + ([x for x in (sgb_free,) if x]
                                                if j == 0 else []),
                                 fin=False)
                            p.mm(SGb[0:1, :], ONES128, pt[0:128, 512:1024],
                                 False, j == npair - 1, waits=[],
                                 fin=(j == npair - 1))
                            if layer == 0:
                                for u in range(2):
                                    vts = M2[0:128, VT0 + 128 * (kc + u) + 64 * hh:
                                             VT0 + 128 * (kc + u) + 64 * hh + 64]
                                    p.mm(PVb[ob0:ob0 + 64, 0:512], vts,
                                         pt[0:128, 512 * u:512 * u + 512],
                                         j == 0 and u == 0, j == npair - 1 and u == 1,
                                         tp=tp_col,
                                         waits=([x for x in (pvb_free,) if x]
                                                if j == 0 and u == 0 else []),
                                         fin=(j == npair - 1 and u == 1))
                        sg_done = ("t", p.cnt["t"])
                        if layer == 1:
                            rio = STAT[0:1, 0:512]
                            mrio = p.dma("ds", rio, riot_d[0:1, q0:q0 + 512],
                                         waits=p.bar("ds"))
                            cr0 = STAT[0:1, 512:1024]
                            t1_ = p.tt(cr0, SGb[0:1, :], rio, AL.mult,
                                       waits=[mrio, sg_done])
                            cr = STAT[0:1, 1024:1536]
                            t2_ = p.ts(cr, cr0, BIA[0:1, 16 + head:17 + head], None,
                                       AL.mult, waits=[t1_])
                            crb = ROWS[0:1, 128:640]
                            c1_ = p.vcopy(crb, cr, waits=[t2_])
                            PCh = QC[0:128, 512:1024]
                            mmB = p.mm(PCh, ONES1, crb, True, True,
                                       waits=[c1_] + [x for x in (er_prev,) if x],
                                       fin=True)
                            tb2 = CWB[0:128, 0:1024]
                            e1_ = p.scopy(CWB[0:128, 0:512], PCh, waits=[mmB])
                            etb = p.scopy(CWB[0:128, 512:1024], PCh, waits=[e1_])
                            for j in range(npair):
                                kc = kmin + 2 * j
                                pt = slab[0:128, slab0 + 1024 * j:slab0 + 1024 * j + 1024]
                                pb1 = PBUF[0:128, 0:1024]
                                pbm = EB[0:128, 0:1024]
                                ad = p.tt(pb1, pt, tb2, AL.add, waits=[etb])
                                mx = p.ts(pbm, pb1, 0.0, None, AL.max, waits=[ad])
                                for u in range(2):
                                    vts = M2[0:128, VT0 + 128 * (kc + u) + 64 * hh:
                                             VT0 + 128 * (kc + u) + 64 * hh + 64]
                                    p.mm(PVb[ob0:ob0 + 64, 0:512], vts,
                                         pbm[0:128, 512 * u:512 * u + 512],
                                         j == 0 and u == 0, j == npair - 1 and u == 1,
                                         tp=tp_col,
                                         waits=([mx] if u == 0 else [])
                                         + ([x for x in (pvb_free,) if x]
                                            if j == 0 and u == 0 else []),
                                         fin=(j == npair - 1 and u == 1))
                        pv_done = ("t", p.cnt["t"])
                        rc = STAT[0:1, 1536:2048]
                        rctok = p.recip_pos(rc, SGb[0:1, :], STAT[0:1, 0:512]
                                            if layer == 0 else STAT[0:1, 512:1024],
                                            waits=[sg_done])
                        sgb_free = rctok
                        rcb = ROWS[0:1, 640:1152]
                        c2_ = p.vcopy(rcb, rc, waits=[rctok])
                        mmR = p.mm(PS[5][ob0:ob0 + 64, :], ONES1[0:1, 0:64], rcb,
                                   True, True, tp=tp_col,
                                   waits=[c2_, ("s", p.cnt["s"])]
                                   + [x for x in (er_prev,) if x],
                                   fin=True)
                        rbuf = CWB[0:128, 512:1024]
                        er = p.scopy(rbuf[ob0:ob0 + 64, 0:512], PS[5][ob0:ob0 + 64, :],
                                     waits=[mmR])
                        er_prev = er
                        ycol = 4096 * pr + 512 * g
                        pvb_free = p.tt(Y[rb0:rb0 + 64, ycol:ycol + 512],
                                        PVb[ob0:ob0 + 64, :],
                                        rbuf[ob0:ob0 + 64, 0:512], AL.mult,
                                        waits=[er, pv_done])

        def proj_partial(layer):
            wsrc = l0proj if layer == 0 else l1proj
            dst = pp_in if layer == 0 else pp_in1
            cout = pp_out if layer == 0 else pp_out1
            Wp = M1[0:128, 0:4096]
            mload = []
            tw_ = p.bar("dg")
            for c in range(4):
                mload.append(p.dma("dg", Wp[:, 1024 * c:1024 * c + 1024],
                                   bass.AP(wsrc, (128 * c) * DM, [[DM, 128], [1, 1024]]),
                                   waits=tw_))
            if layer == 0:
                mload.append(p.dma("dg", BIA[0:128, 8:16], l0pb[:, :],
                                   waits=p.bar("dg")))
            yready = [("v", p.cnt["v"]), ("s", p.cnt["s"])]
            bfreeP = {0: None, 1: None, 2: None, 3: None}
            obtok = {}
            first = True
            for od in range(8):
                for tsl in range(8):
                    bi = tsl % 4
                    bank = PS[bi]
                    for c in range(4):
                        wl = []
                        if c == 0:
                            wl = list(mload) + [x for x in (bfreeP[bi],) if x]
                            if first:
                                wl += yready
                        p.mm(bank[:, :], Wp[:, 1024 * c + 128 * od:
                                            1024 * c + 128 * od + 128],
                             Y[:, 4096 * c + 512 * tsl:4096 * c + 512 * tsl + 512],
                             c == 0, c == 3, waits=wl, fin=(c == 3))
                    first = False
                    mmf = ("t", p.cnt["t"])
                    oi = tsl % 2
                    ob = EB[0:128, 512 * oi:512 * oi + 512]
                    gg2 = tsl // 4
                    lc = 512 * (tsl % 4)
                    wl_ev = [mmf] + [x for x in (obtok.get(oi),) if x]
                    if layer == 0:
                        ev = p.act(ob, bank[:, :], AF.Identity,
                                   bias=BIA[0:128, 8 + od:9 + od], waits=wl_ev)
                    else:
                        ev = p.scopy(ob, bank[:, :], waits=wl_ev)
                    bfreeP[bi] = ev
                    obtok[oi] = p.dma("ds",
                                      bass.AP(dst, gg2 * DM * OWN + (128 * od) * OWN + lc,
                                              [[OWN, 128], [1, 512]]),
                                      ob, waits=[ev])
            p.op("g", lambda e: e.collective_compute(
                "ReduceScatter", AL.add,
                replica_groups=[[0, 1], [2, 3], [4, 5], [6, 7]],
                ins=[dst.ap().opt()], outs=[cout.ap().opt()]),
                waits=p.bar("cc"), incs=(("cc", 1),))
            return ("cc", p.cnt["cc"])

        def gate_up_h(w_gate_d, w_up_d, nhh, tcols, tw, scale_g, scale_u,
                      e_idx=None):
            cap = 20480 // tw

            def slot(i):
                if i < 0:
                    return WINB[0:128, 2 * tw:3 * tw]
                if i < cap:
                    return M2[0:128, tw * i:tw * i + tw]
                return WINB[0:128, tw * (i - cap):tw * (i - cap) + tw]

            NC_ = nhh * 128
            wstride = int(w_gate_d.shape[-1])

            nh2 = nhh // 2
            NH2 = nh2 * 128

            def wslab(src_d, half):
                toks = []
                tw_ = p.bar("dg")
                for c in range(8):
                    base = (e_idx * DM * MH if e_idx is not None else 0)
                    toks.append(p.dma("dg",
                                      M1[0:128, NC_ * c + NH2 * half:
                                         NC_ * c + NH2 * half + NH2],
                                      bass.AP(src_d, base + (128 * c) * wstride
                                              + NH2 * half,
                                              [[wstride, 128], [1, NH2]]),
                                      waits=tw_))
                return toks

            nts = tw // 512
            nbank = 4
            bfree = {b: None for b in range(nbank)}
            mwh = {0: wslab(w_gate_d, 0), 1: wslab(w_gate_d, 1)}
            m3h = {}
            for hh2 in range(nhh):
                if hh2 == nh2:
                    m3h[0] = wslab(w_up_d, 0)
                for tsub in range(nts):
                    bi = (hh2 * nts + tsub) % nbank
                    bank = PS[bi]
                    for c in range(8):
                        wl = []
                        if c == 0:
                            wl = list(mwh[min(1, hh2 // nh2)])
                            if bfree[bi] is not None:
                                wl.append(bfree[bi])
                        p.mm(bank[:, :],
                             M1[0:128, NC_ * c + 128 * hh2:NC_ * c + 128 * hh2 + 128],
                             XN[0:128, 2048 * c + tcols + 512 * tsub:
                                2048 * c + tcols + 512 * tsub + 512],
                             c == 0, c == 7, waits=wl, fin=(c == 7))
                    mmt = ("t", p.cnt["t"])
                    bfree[bi] = p.act(slot(hh2)[:, 512 * tsub:512 * tsub + 512],
                                      bank[:, :], AF.Silu, scale=scale_g,
                                      waits=[mmt])
            gate_done = list(bfree.values())
            m3h[1] = wslab(w_up_d, 1)
            bfree = {b: None for b in range(nbank)}
            htok = {}
            uetok = {}
            for hh2 in range(nhh):
                for tsub in range(nts):
                    bi = (hh2 * nts + tsub) % nbank
                    bank = PS[bi]
                    for c in range(8):
                        wl = []
                        if c == 0:
                            wl = (list(m3h[min(1, hh2 // nh2)])
                                  + [x for x in (bfree[bi],) if x])
                            if hh2 == 0:
                                wl += [x for x in gate_done if x]
                        p.mm(bank[:, :],
                             M1[0:128, NC_ * c + 128 * hh2:NC_ * c + 128 * hh2 + 128],
                             XN[0:128, 2048 * c + tcols + 512 * tsub:
                                2048 * c + tcols + 512 * tsub + 512],
                             c == 0, c == 7, waits=wl, fin=(c == 7))
                    mmt = ("t", p.cnt["t"])
                    ui = (hh2 * nts + tsub) % 2
                    ue = EB[0:128, 512 * ui:512 * ui + 512]
                    ev = p.act(ue, bank[:, :], AF.Copy, scale=scale_u,
                               waits=[mmt] + [x for x in (uetok.get(ui),) if x])
                    bfree[bi] = ev
                    tk = p.tt(slot(hh2 - 1)[:, 512 * tsub:512 * tsub + 512],
                              ue, slot(hh2)[:, 512 * tsub:512 * tsub + 512],
                              AL.mult, waits=[ev])
                    htok[(hh2, tsub)] = tk
                    uetok[ui] = tk
            return slot, htok

        if L0:
            layernorm(x_fm, T, None, xn0_d)
            layernorm(x_own, OWN, XN, None)
            attention(0)
            cc_l0 = proj_partial(0)
            sG = 1.0 / 127.0
            sU = sG * sG
            for tb in range(4):
                slot, htok = gate_up_h(wg_t, wu_t, 24, 512 * tb, 512, sG, sU)
                msdh = {0: [], 1: []}
                tw_ = p.bar("dg")
                for c in range(24):
                    msdh[c // 12].append(
                        p.dma("dg", M1[0:128, 1024 * c:1024 * c + 1024],
                              bass.AP(wd_t, (128 * c) * DM, [[DM, 128], [1, 1024]]),
                              waits=tw_))
                hall = [t for t in htok.values()]
                bfree2 = {0: None, 1: None, 2: None, 3: None}
                xbtok, rbtok, t1tok, t2tok = {}, {}, {}, {}
                for od in range(8):
                    bi = od % 4
                    bank = PS[bi]
                    for c in range(24):
                        wl = []
                        if c == 0:
                            wl = list(msdh[0]) + [x for x in (bfree2[bi],) if x]
                            if od == 0:
                                wl += hall
                        elif c == 12 and od == 0:
                            wl = list(msdh[1])
                        p.mm(bank[:, :],
                             M1[0:128, 1024 * c + 128 * od:1024 * c + 128 * od + 128],
                             slot(c - 1)[:, 0:512], c == 0, c == 23,
                             waits=wl, fin=(c == 23))
                    mmf = ("t", p.cnt["t"])
                    pi = od % 2
                    xb = SCR[0:128, 1024 + 512 * pi:1536 + 512 * pi]
                    mx_ = p.dma("ds", xb,
                                bass.AP(x_own, (128 * od) * OWN + 512 * tb,
                                        [[OWN, 128], [1, 512]]),
                                waits=[x for x in (xbtok.get(pi),) if x])
                    rb = EB[0:128, 1024 + 512 * pi:1536 + 512 * pi]
                    mr = p.dma("ds", rb,
                               bass.AP(pp_out, (128 * od) * OWN + 512 * tb,
                                       [[OWN, 128], [1, 512]]),
                               waits=[cc_l0] + [x for x in (rbtok.get(pi),) if x])
                    t1 = SCR[0:128, 512 * pi:512 * pi + 512]
                    tk1 = p.tt(t1, bank[:, :], rb, AL.add,
                               waits=[mmf, mr] + [x for x in (t1tok.get(pi),) if x])
                    bfree2[bi] = tk1
                    rbtok[pi] = tk1
                    t2 = SCR[0:128, 2048 + 512 * pi:2560 + 512 * pi]
                    tk2 = p.tt(t2, t1, xb, AL.add,
                               waits=[mx_] + [x for x in (t2tok.get(pi),) if x])
                    xbtok[pi] = tk2
                    t1tok[pi] = tk2
                    dd = x1_out if mode == "L0" else x1_d
                    t2tok[pi] = p.dma("ds", bass.AP(dd, (128 * od) * OWN + 512 * tb,
                                                    [[OWN, 128], [1, 512]]),
                                      t2, waits=[tk2])

        if L1:
            x1src = x1_in if mode == "L1" else x1_d
            layernorm(x1src, OWN, XN, xn1_in)
            p.op("g", lambda e: e.collective_compute(
                "AllGather", AL.bypass,
                replica_groups=[[0, 1], [2, 3], [4, 5], [6, 7]],
                ins=[xn1_in.ap().opt()], outs=[xn1_g.ap().opt()]),
                waits=p.bar("cc"), incs=(("cc", 1),))
            if "l1attn" not in skip:
                attention(1)
            cc_l1 = proj_partial(1)
            mwg = p.dma("dg", ROWS[0:128, 1152:1216], wg8[:, :], waits=p.bar("dg"))
            for tsl in range(4):
                for c in range(8):
                    p.mm(PS[tsl % 4][0:8, :], ROWS[0:128, 1152 + 8 * c:1160 + 8 * c],
                         XN[0:128, 2048 * c + 512 * tsl:2048 * c + 512 * tsl + 512],
                         c == 0, c == 7,
                         waits=([mwg, ("s", p.cnt["s"]), ("v", p.cnt["v"])]
                                if c == 0 else []),
                         fin=(c == 7))
                p.scopy(SCR[0:8, 512 * tsl:512 * tsl + 512], PS[tsl % 4][0:8, :],
                        waits=p.bar("s"))
            lg_done = ("s", p.cnt["s"])
            for tc in range(16):
                lgb = ROWS[0:8, 1280:1408]
                p.vcopy(lgb, SCR[0:8, 128 * tc:128 * tc + 128],
                        waits=[lg_done] + p.bar("v"))
                p.tr(PSB[0:128, 0:8], lgb, IDT[0:8, 0:8],
                     waits=p.bar("t"))
                ltm = ROWS[0:128, 1216:1224]
                p.vcopy(ltm, PSB[0:128, 0:8], waits=p.bar("v"))
                TRE = ROWS[0:128, 1408:1424]
                TREF = SCR[0:128, 2048:2072]

                def tree8(dst_col, tile, alu):
                    p.tt(TRE[0:128, 0:4], tile[0:128, 0:4], tile[0:128, 4:8], alu,
                         waits=p.bar("v"))
                    p.tt(TRE[0:128, 4:6], TRE[0:128, 0:2], TRE[0:128, 2:4], alu,
                         waits=p.bar("v"))
                    p.tt(dst_col, TRE[0:128, 4:5], TRE[0:128, 5:6], alu,
                         waits=p.bar("v"))

                m0 = TREF[0:128, 0:1]
                tree8(m0, ltm, AL.max)
                nm = TREF[0:128, 1:2]
                p.ts(nm, m0, -1.0, None, AL.mult, waits=p.bar("v"))
                e8 = ROWS[0:128, 1224:1232]
                p.act(e8, ltm, AF.Exp, bias=nm, waits=p.bar("s"))
                s8 = TREF[0:128, 2:3]
                tree8(s8, e8, AL.add)
                ri = TREF[0:128, 3:4]
                p.recip_pos(ri, s8, TREF[0:128, 4:5], waits=p.bar("s"))
                p8 = ROWS[0:128, 1232:1240]
                p.ts(p8, e8, ri, None, AL.mult, waits=p.bar("v"))
                m1 = TREF[0:128, 5:6]
                tree8(m1, p8, AL.max)
                eq = TREF[0:128, 6:14]
                p.ts(eq, p8, m1, None, AL.is_equal, waits=p.bar("v"))
                pm = ROWS[0:128, 1248:1256]
                p.stt(pm, eq, -2.0, p8, AL.mult, AL.add, waits=p.bar("v"))
                m2 = TREF[0:128, 14:15]
                tree8(m2, pm, AL.max)
                keep = ROWS[0:128, 1256:1264]
                p.ts(keep, p8, m2, None, AL.is_ge, waits=p.bar("v"))
                cwu = ROWS[0:128, 1264:1272]
                p.tt(cwu, p8, keep, AL.mult, waits=p.bar("v"))
                s2 = TREF[0:128, 15:16]
                tree8(s2, cwu, AL.add)
                ri2 = TREF[0:128, 16:17]
                p.recip_pos(ri2, s2, TREF[0:128, 4:5], waits=p.bar("s"))
                cwt = ROWS[0:128, 1272:1280]
                p.ts(cwt, cwu, ri2, None, AL.mult, waits=p.bar("v"))
                if dbg_out is not None and tc == 0:
                    for k, tile in enumerate([ltm, e8, p8, pm, keep, cwu, cwt]):
                        p.vcopy(PBUF[0:128, 8 * k:8 * k + 8], tile, waits=p.bar("v"))
                    p.vcopy(PBUF[0:128, 56:64], eq[0:128, 0:8], waits=p.bar("v"))
                    for k in range(8):
                        p.vcopy(PBUF[0:128, 64 + k:65 + k],
                                TREF[0:128, k:k + 1], waits=p.bar("v"))
                p.tr(PSB[0:8, 128:256], cwt, IDT[:, :], waits=p.bar("t"))
                p.vcopy(SMB[0:8, 128 * tc:128 * tc + 128], PSB[0:8, 128:256],
                        waits=p.bar("v"))
            cw_done = ("v", p.cnt["v"])
            for tb in range(2):
                for e_i in ([] if "moe" in skip else range(NE)):
                    slot, htok = gate_up_h(w1t, w3t, 22, 1024 * tb, 1024,
                                           1.0, 1.0, e_idx=e_i)
                    msdh = {0: [], 1: []}
                    tw_ = p.bar("dg")
                    for c in range(22):
                        msdh[c // 11].append(
                            p.dma("dg", M1[0:128, 1024 * c:1024 * c + 1024],
                                  bass.AP(w2t, e_i * MH * DM + (128 * c) * DM,
                                          [[DM, 128], [1, 1024]]),
                                  waits=tw_))
                    if "cw" in skip:
                        p.vmemset(EB[0:128, 1024:1536], 0.125, waits=p.bar("v"))
                        p.vmemset(EB[0:128, 1536:2048], 0.125, waits=p.bar("v"))
                        ecw = ("v", p.cnt["v"])
                    else:
                        mcw = p.dma("dg", ROWS[0:1, 128:1152],
                                    bass.AP(SMB, e_i * 2048 + 1024 * tb,
                                            [[2048, 1], [1, 1024]]),
                                    waits=[cw_done] + p.bar("dg"))
                        p.mm(PS[2][:, :], ONES1, ROWS[0:1, 128:640], True, True,
                             waits=[mcw, ("s", p.cnt["s"])], fin=True)
                        p.mm(PS[3][:, :], ONES1, ROWS[0:1, 640:1152], True, True,
                             waits=[mcw], fin=True)
                        p.scopy(EB[0:128, 1024:1536], PS[2][:, :], waits=p.bar("s"))
                        p.scopy(EB[0:128, 1536:2048], PS[3][:, :], waits=p.bar("s"))
                        ecw = ("s", p.cnt["s"])
                    hall = list(htok.values())
                    bfree3 = {0: None, 1: None, 2: None, 3: None}
                    first = True
                    first2 = True
                    for od in range(8):
                        for tsub in range(2):
                            bi = (od * 2 + tsub) % 4
                            bank = PS[bi]
                            for c in range(22):
                                wl = []
                                if c == 0:
                                    wl = list(msdh[0]) + [x for x in (bfree3[bi],) if x]
                                    if first:
                                        wl += hall + [ecw]
                                elif c == 11 and first2:
                                    wl = list(msdh[1])
                                p.mm(bank[:, :],
                                     M1[0:128, 1024 * c + 128 * od:
                                        1024 * c + 128 * od + 128],
                                     slot(c - 1)[:, 512 * tsub:512 * tsub + 512],
                                     c == 0, c == 21, waits=wl, fin=(c == 21))
                            first2 = False
                            first = False
                            mmf = ("t", p.cnt["t"])
                            cwb = (EB[0:128, 1024:1536] if tsub == 0
                                   else EB[0:128, 1536:2048])
                            ycol = 8192 * (e_i % 2) + 1024 * od + 512 * tsub
                            yprev = 8192 * ((e_i + 1) % 2) + 1024 * od + 512 * tsub
                            if e_i == 0:
                                bfree3[bi] = p.tt(Y[0:128, ycol:ycol + 512],
                                                  bank[:, :], cwb, AL.mult,
                                                  waits=[mmf, ecw])
                            else:
                                tmp = PBUF[0:128, 0:512]
                                bfree3[bi] = p.tt(tmp, bank[:, :], cwb, AL.mult,
                                                  waits=[mmf, ecw])
                                p.tt(Y[0:128, ycol:ycol + 512],
                                     Y[0:128, yprev:yprev + 512], tmp, AL.add,
                                     waits=[])
                ylast = 8192 * ((NE - 1) % 2)
                for od in range(8):
                    for tsub in range(2):
                        xb = SCR[0:128, 0:512]
                        mx_ = p.dma("ds", xb,
                                    bass.AP(x1src, (128 * od) * OWN + 1024 * tb
                                            + 512 * tsub,
                                            [[OWN, 128], [1, 512]]),
                                    waits=p.bar("ds"))
                        rb = EB[0:128, 0:512]
                        mr = p.dma("ds", rb,
                                   bass.AP(pp_out1, (128 * od) * OWN + 1024 * tb
                                           + 512 * tsub,
                                           [[OWN, 128], [1, 512]]),
                                   waits=[cc_l1] + p.bar("ds"))
                        t1 = SCR[0:128, 512:1024]
                        p.tt(t1, xb, rb, AL.add, waits=[mr, mx_, ("v", p.cnt["v"]),
                                                        ("t", p.cnt["t"])])
                        ycol = ylast + 1024 * od + 512 * tsub
                        t2 = SCR[0:128, 2048:2560]
                        p.tt(t2, t1, Y[0:128, ycol:ycol + 512], AL.add,
                             waits=p.bar("v"))
                        p.dma("ds", bass.AP(y_out, (128 * od) * OWN + 1024 * tb
                                            + 512 * tsub,
                                            [[OWN, 128], [1, 512]]),
                              t2, waits=p.bar("ds"))

        if dbg_out is not None:
            p.dma("dg", dbg2_out[:, 0:72], PBUF[0:128, 0:72], waits=p.bar("dg"))
            p.dma("dg", bass.AP(dbg_out, 0, [[2048, 8], [1, 2048]]),
                  SMB[0:8, 0:2048], waits=p.bar("dg"))
            p.dma("dg", bass.AP(dbg_out, 8 * 2048, [[2048, 8], [1, 2048]]),
                  SCR[0:8, 0:2048], waits=p.bar("dg"))
            p.dma("dg", bass.AP(dbg_out, 16 * 2048, [[2048, 8], [1, 2048]]),
                  XN[0:8, 0:2048], waits=p.bar("dg"))
        p.op("ds", lambda e: e.nop(), waits=p.bar(), incs=())

        @block.tensor
        def _(t):
            p.replay_eng = None
            for e, fn in p.sched:
                if e == "t":
                    fn(t)

        @block.vector
        def _(v):
            for e, fn in p.sched:
                if e == "v":
                    fn(v)

        @block.scalar
        def _(s):
            for e, fn in p.sched:
                if e == "s":
                    fn(s)

        @block.gpsimd
        def _(g):
            for e, fn in p.sched:
                if e in ("dg", "g", "cc"):
                    fn(g)

        @block.sync
        def _(s):
            for e, fn in p.sched:
                if e == "ds":
                    fn(s)

    nc._input_names = input_names
    return nc



def _run(mode, in_maps):
    nc = build_program(mode)
    names = set(nc._input_names)
    in_maps = [{k: v for k, v in m.items() if k in names} for m in in_maps]
    return run_bass_kernel_spmd(nc, in_maps, core_ids=list(range(8)))


def kernel(**inputs):
    in_maps = [stage_core(inputs, c) for c in range(8)]
    res = _run("FUSED", in_maps)
    out = np.empty((4, T, DM), np.float32)
    for c in range(8):
        b, h = c // 2, c % 2
        y = np.asarray(res.results[c]["y_out"], np.float32)
        out[b, 2048 * h:2048 * h + 2048, :] = y.T
    return out
